# version 14
# baseline (speedup 1.0000x reference)
import contextlib
import numpy as np
import concourse.bass as bass
import concourse.mybir as mybir
from concourse.bass_utils import run_bass_kernel_spmd

F32 = mybir.dt.float32
BF16 = mybir.dt.bfloat16
I32 = mybir.dt.int32
U32 = mybir.dt.uint32
AF = mybir.ActivationFunctionType
ALU = mybir.AluOpType
AX = mybir.AxisListType

D = 1024
NT = 4096
NB = 2
BTOT = 16
NTOT = 32768
S = 2048
H = 16
HD = 64
FC = 8
TCH = 256
NTC = NT // TCH
EPS = 1e-5


class Sem:
    def __init__(self, h):
        self.h = h
        self.n = 0

    def inc(self, instr, k=1):
        instr.then_inc(self.h, k)
        self.n += k
        return instr

    def wait(self, eng):
        eng.wait_ge(self.h, self.n)


class Ctx:
    def __init__(self, nc, block, exit_stack):
        self.nc = nc
        self.block = block
        self.es = exit_stack
        self._sems = {}

    def sem(self, name):
        if name not in self._sems:
            self._sems[name] = Sem(self.es.enter_context(self.nc.semaphore(name)))
        return self._sems[name]

    def sbuf(self, name, shape, dtype, scope=None):
        return (scope or self.es).enter_context(self.nc.sbuf_tensor(name, shape, dtype))

    def barrier(self, engines=("tensor", "vector", "scalar", "gpsimd", "sync")):
        sems = list(self._sems.values())
        for en in engines:
            def mk(eng, sems=sems):
                for s in sems:
                    s.wait(eng)
            getattr(self.block, en)(mk)



def sl_ap(h, g0, w):
    sl, c = divmod(g0, 4096)
    assert c + w <= 4096
    return bass.AP(h, sl * 4096 + c, [[8 * 4096, 128], [128 * 8 * 4096, 8], [1, w]])


def sl_row_ap(h, f0, g0, w):
    sl, c = divmod(g0, 4096)
    assert c + w <= 4096
    return bass.AP(h, f0 * 8 * 4096 + sl * 4096 + c, [[8 * 4096, 128], [1, w]])



def _prov_flat(h):
    def xprov(c0, w):
        return bass.AP(h, c0, [[4096, 128], [128 * 4096, 8], [1, w]])

    def xrow(f0, c0, w):
        return bass.AP(h, f0 * 4096 + c0, [[4096, 128], [256, w // 256], [1, 256]])

    def yprov(c0, w):
        return bass.AP(h, c0, [[4096, 128], [128 * 4096, 8], [1, w]])
    return xprov, xrow, yprov



def emit_ln(ctx, XR, LNB, XC, consts, tc, sems, xstride=None, tw=TCH):
    nc, block = ctx.nc, ctx.block
    s_pe, s_v, s_a = sems
    onesf, ones2, ones_row, ps_sum, ps_mb, ps_rb, row1, rowr, rowv = consts
    NTs = NT if xstride is None else xstride
    c0 = tc * tw

    @block.tensor
    def _(t):
        s_v.wait(t)
        s_a.wait(t)
        last = None
        for kc in range(FC):
            last = t.matmul(ps_sum[:, :], onesf[:, 0:1], XR[:, kc * NTs + c0:kc * NTs + c0 + tw],
                            start=(kc == 0), stop=(kc == FC - 1))
        s_pe.inc(last)

    @block.vector
    def _(v):
        s_pe.wait(v)
        with nc.allow_low_precision(reason="bf16 matmul operand"):
            s_a.inc(v.tensor_copy(row1[:, :], ps_sum[0:1, :]))

    @block.tensor
    def _(t):
        s_a.wait(t)
        s_pe.inc(t.matmul(ps_mb[:, :], ones_row[:, :], row1[:, :], start=True, stop=True))

    @block.vector
    def _(v):
        s_pe.wait(v)
        last = None
        for kc in range(FC):
            last = v.tensor_add(XC[:, kc * tw:(kc + 1) * tw],
                                XR[:, kc * NTs + c0:kc * NTs + c0 + tw], ps_mb[:, :])
        s_v.inc(last)

    @block.vector
    def _(v):
        s_v.wait(v)
        last = None
        with nc.allow_low_precision(reason="bf16 matmul operand"):
            for kc in range(FC):
                last = v.tensor_mul(LNB[:, kc * tw:(kc + 1) * tw],
                                    XC[:, kc * tw:(kc + 1) * tw],
                                    XC[:, kc * tw:(kc + 1) * tw])
        s_a.inc(last)

    @block.tensor
    def _(t):
        s_a.wait(t)
        last = None
        for kc in range(FC):
            last = t.matmul(ps_sum[:, :], ones2[:, 1:2], LNB[:, kc * tw:(kc + 1) * tw],
                            start=(kc == 0), stop=(kc == FC - 1))
        s_pe.inc(last)

    @block.vector
    def _(v):
        s_pe.wait(v)
        s_v.inc(v.tensor_scalar(rowv[:, :], ps_sum[0:1, :], 1.0 / D, EPS, ALU.mult, ALU.add))

    @block.scalar
    def _(sc):
        s_v.wait(sc)
        s_a.inc(sc.activation(out=rowv[:, :], in_=rowv[:, :], func=AF.Sqrt))

    @block.vector
    def _(v):
        s_a.wait(v)
        with nc.allow_low_precision(reason="bf16 matmul operand"):
            s_v.inc(v.reciprocal(rowr[:, :], rowv[:, :]))

    @block.tensor
    def _(t):
        s_v.wait(t)
        s_pe.inc(t.matmul(ps_rb[:, :], ones_row[:, :], rowr[:, :], start=True, stop=True))

    @block.vector
    def _(v):
        s_pe.wait(v)
        last = None
        for kc in range(FC):
            last = v.tensor_mul(LNB[:, kc * tw:(kc + 1) * tw],
                                XC[:, kc * tw:(kc + 1) * tw], ps_rb[:, :])
        s_v.inc(last)


def _mk_ln_consts(ctx, sc1, tag, tw=TCH):
    nc = ctx.nc
    ones2 = ctx.sbuf(tag + "ones2", [128, 2], BF16)
    onesf = ctx.sbuf(tag + "onesf", [128, 1], F32)
    ones_row = ctx.sbuf(tag + "onesr", [1, 128], BF16)
    row1 = ctx.sbuf(tag + "row1", [1, tw], BF16, scope=sc1)
    rowr = ctx.sbuf(tag + "rowr", [1, tw], BF16, scope=sc1)
    rowv = ctx.sbuf(tag + "rowv", [1, tw], F32, scope=sc1)
    ps_sum = sc1.enter_context(nc.psum_tensor(tag + "ps_sum", [1, tw], F32))
    ps_mb = sc1.enter_context(nc.psum_tensor(tag + "ps_mb", [128, tw], F32))
    ps_rb = sc1.enter_context(nc.psum_tensor(tag + "ps_rb", [128, tw], F32))
    consts = (onesf, ones2, ones_row, ps_sum, ps_mb, ps_rb, row1, rowr, rowv)

    block = ctx.block
    s_g = ctx.sem(tag + "gg")
    s_cs = ctx.sem(tag + "cs")

    @block.gpsimd
    def _(g):
        s_cs.inc(g.memset(ones2[:, 0:1], -1.0 / D))
        s_cs.wait(g)
        g.memset(ones2[:, 1:2], 1.0)
        g.memset(onesf[:, 0:1], -1.0 / D)
        s_g.inc(g.memset(ones_row[:, :], 1.0))
    return consts, (ones2, onesf, ones_row)



def emit_l0_qkv(ctx, xT_h, wq_d, bq_d, qkT_d, vtok_d, tag="p1"):
    TW = 512
    nc, block = ctx.nc, ctx.block
    s_ld = ctx.sem(tag + "ld")
    s_sp = ctx.sem(tag + "sp")
    s_pe = ctx.sem(tag + "pe")
    s_v = ctx.sem(tag + "vv")
    s_a = ctx.sem(tag + "aa")
    s_g = ctx.sem(tag + "gg")

    with contextlib.ExitStack() as sc1:
        consts, _ = _mk_ln_consts(ctx, sc1, tag, tw=TW)
        WQ = ctx.sbuf(tag + "wq", [128, FC * 3072], BF16, scope=sc1)
        XRC = ctx.sbuf(tag + "xrc", [128, FC * TW], F32, scope=sc1)
        XC = ctx.sbuf(tag + "xcen", [128, FC * TW], F32, scope=sc1)
        LNB = ctx.sbuf(tag + "lnb", [128, FC * TW], BF16, scope=sc1)
        STG = ctx.sbuf(tag + "stg", [128, 8 * TW], BF16, scope=sc1)
        VSTG = ctx.sbuf(tag + "vstg", [128, 4 * 1024], BF16, scope=sc1)
        bq_sb = ctx.sbuf(tag + "bq", [128, 16], F32, scope=sc1)

        ps_q0 = sc1.enter_context(nc.psum_tensor(tag + "ps_q0", [128, TW], F32))
        ps_q1 = sc1.enter_context(nc.psum_tensor(tag + "ps_q1", [128, TW], F32))
        ps_v0 = sc1.enter_context(nc.psum_tensor(tag + "ps_v0", [128, 512], F32))
        ps_v1 = sc1.enter_context(nc.psum_tensor(tag + "ps_v1", [128, 512], F32))

        @block.gpsimd
        def _(g):
            s_ld.inc(g.dma_start(
                out=WQ[:, :].rearrange("p (kc m) -> p kc m", kc=FC),
                in_=wq_d.rearrange("(kc p) m -> p kc m", p=128)), 16)
            s_ld.inc(g.dma_start(out=bq_sb[:, :], in_=bq_d[:, 0:16]), 16)

        for tc in range(NTOT // TW):
            c0 = tc * TW

            @block.sync
            def _(sy, c0=c0):
                s_a.wait(sy)
                s_v.wait(sy)
                s_pe.wait(sy)
                s_sp.wait(sy)
                s_g.inc(sy.dma_start(
                    out=XRC[:, :].rearrange("p (kc n) -> p kc n", kc=FC),
                    in_=sl_ap(xT_h, c0, TW)), 16)

            @block.tensor
            def _(t):
                s_g.wait(t)
                s_ld.wait(t)

            @block.vector
            def _(v):
                s_g.wait(v)

            @block.scalar
            def _(sc):
                s_g.wait(sc)

            emit_ln(ctx, XRC, LNB, XC, consts, 0, (s_pe, s_v, s_a), xstride=TW, tw=TW)

            for grp in range(2):
                base_v = s_v.n
                base_pe = s_pe.n

                @block.tensor
                def _(t, grp=grp, base_v=base_v):
                    s_v.wait(t)
                    s_sp.wait(t)
                    for j in range(8):
                        ms = grp * 8 + j
                        ps = ps_q0 if j % 2 == 0 else ps_q1
                        if j >= 2:
                            t.wait_ge(s_v.h, base_v + j - 1)
                        last = None
                        for kc in range(FC):
                            last = t.matmul(ps[:, :],
                                            WQ[:, kc * 3072 + ms * 128:kc * 3072 + (ms + 1) * 128],
                                            LNB[:, kc * TW:(kc + 1) * TW],
                                            start=(kc == 0), stop=(kc == FC - 1))
                        s_pe.inc(last)

                @block.vector
                def _(v, grp=grp, base_pe=base_pe):
                    for j in range(8):
                        ms = grp * 8 + j
                        ps = ps_q0 if j % 2 == 0 else ps_q1
                        v.wait_ge(s_pe.h, base_pe + j + 1)
                        s_v.inc(v.tensor_scalar(STG[:, j * TW:(j + 1) * TW], ps[:, :],
                                                bq_sb[:, ms:ms + 1], None, ALU.add))

                @block.sync
                def _(sy, c0=c0, grp=grp):
                    s_v.wait(sy)
                    s_sp.inc(sy.dma_start(
                        out=qkT_d.rearrange("(ms p) n -> p ms n", p=128)[:, grp * 8:(grp + 1) * 8,
                                                                        c0:c0 + TW],
                        in_=STG[:, :].rearrange("p (ms n) -> p ms n", ms=8)), 16)

            base_pe = s_pe.n
            base_v = s_v.n

            @block.tensor
            def _(t, base_v=base_v):
                s_sp.wait(t)
                for i in range(8):
                    ts, half = divmod(i, 2)
                    ps = ps_v0 if i % 2 == 0 else ps_v1
                    if i >= 2:
                        t.wait_ge(s_v.h, base_v + i - 1)
                    last = None
                    for kc in range(FC):
                        last = t.matmul(
                            ps[:, :],
                            LNB[:, kc * TW + ts * 128:kc * TW + (ts + 1) * 128],
                            WQ[:, kc * 3072 + 2048 + half * 512:kc * 3072 + 2048 + (half + 1) * 512],
                            start=(kc == 0), stop=(kc == FC - 1))
                    s_pe.inc(last)

            @block.vector
            def _(v, base_pe=base_pe):
                for i in range(8):
                    ts, half = divmod(i, 2)
                    ps = ps_v0 if i % 2 == 0 else ps_v1
                    v.wait_ge(s_pe.h, base_pe + i + 1)
                    s_v.inc(v.tensor_copy(
                        VSTG[:, ts * 1024 + half * 512:ts * 1024 + (half + 1) * 512], ps[:, :]))

            @block.scalar
            def _(sc, c0=c0):
                s_v.wait(sc)
                s_sp.inc(sc.dma_start(
                    out=vtok_d[c0:c0 + TW, :].rearrange("(ts p) n -> p ts n", p=128),
                    in_=VSTG[:, :].rearrange("p (ts n) -> p ts n", ts=4)), 16)

        ctx.barrier()



def emit_qk_norms(ctx, qkT_d, qn_d, km_d, tag="p15"):
    nc, block = ctx.nc, ctx.block
    s_ld = ctx.sem(tag + "ld")
    s_sp = ctx.sem(tag + "sp")
    s_pe = ctx.sem(tag + "pe")
    s_v = ctx.sem(tag + "vv")
    s_g = ctx.sem(tag + "gg")

    with contextlib.ExitStack() as scn:
        KC = ctx.sbuf(tag + "kc", [128, NTOT], BF16, scope=scn)
        SQ = ctx.sbuf(tag + "sq", [128, NTOT], BF16, scope=scn)
        QNS = ctx.sbuf(tag + "qns", [2, NTOT], BF16, scope=scn)
        KMC = ctx.sbuf(tag + "kmc", [2, 16], F32, scope=scn)
        KMT = ctx.sbuf(tag + "kmt", [2, 1], F32, scope=scn)
        hmask = ctx.sbuf(tag + "hm", [128, 2], BF16, scope=scn)
        ps_n = [scn.enter_context(nc.psum_tensor(tag + "psn%d" % i, [2, 512], F32))
                for i in range(2)]

        @block.gpsimd
        def _(g):
            s_ld.inc(g.memset(hmask[:, :], 0.0))
            s_ld.wait(g)
            s_ld.inc(g.memset(hmask[0:64, 0:1], 1.0))
            s_ld.wait(g)
            s_ld.inc(g.memset(hmask[64:128, 1:2], 1.0))

        for which in ("k", "q"):
            rows0 = 1024 if which == "k" else 0
            for kc in range(FC):
                @block.sync
                def _(sy, kc=kc, rows0=rows0):
                    s_pe.wait(sy)
                    s_v.wait(sy)
                    s_sp.wait(sy)
                    for hh in range(2):
                        s_g.inc(sy.dma_start(
                            out=KC[:, hh * (NTOT // 2):(hh + 1) * (NTOT // 2)],
                            in_=qkT_d[rows0 + kc * 128:rows0 + (kc + 1) * 128,
                                      hh * (NTOT // 2):(hh + 1) * (NTOT // 2)]), 16)

                @block.vector
                def _(v):
                    s_g.wait(v)
                    with nc.allow_low_precision(reason="bf16 matmul operand"):
                        for hh in range(4):
                            s_v.inc(v.tensor_mul(
                                SQ[:, hh * (NTOT // 4):(hh + 1) * (NTOT // 4)],
                                KC[:, hh * (NTOT // 4):(hh + 1) * (NTOT // 4)],
                                KC[:, hh * (NTOT // 4):(hh + 1) * (NTOT // 4)]))

                if which == "k":
                    for b in range(BTOT):
                        base_pe = s_pe.n
                        base_v = s_v.n

                        @block.tensor
                        def _(t, b=b, base_v=base_v):
                            if b == 0:
                                s_v.wait(t)
                                s_ld.wait(t)
                            for j in range(4):
                                if b > 0 or j >= 2:
                                    t.wait_ge(s_v.h, base_v + 2 * j - 3)
                                s_pe.inc(t.matmul(
                                    ps_n[j % 2][:, :], hmask[:, :],
                                    SQ[:, b * S + j * 512:b * S + (j + 1) * 512],
                                    start=True, stop=True))

                        @block.vector
                        def _(v, b=b, base_pe=base_pe):
                            for j in range(4):
                                v.wait_ge(s_pe.h, base_pe + j + 1)
                                s_v.inc(v.reduce_max(KMT[:, 0:1], ps_n[j % 2][:, :],
                                                     axis=AX.X))
                                s_v.wait(v)
                                if j == 0:
                                    s_v.inc(v.tensor_copy(KMC[:, b:b + 1], KMT[:, 0:1]))
                                else:
                                    s_v.inc(v.tensor_tensor(KMC[:, b:b + 1], KMC[:, b:b + 1],
                                                            KMT[:, 0:1], ALU.max))
                                s_v.wait(v)

                    @block.sync
                    def _(sy, kc=kc):
                        s_v.wait(sy)
                        s_sp.inc(sy.dma_start(out=km_d[2 * kc:2 * kc + 2, :],
                                              in_=KMC[:, :]), 16)
                else:
                    for j in range(NTOT // 512):
                        base_pe = s_pe.n
                        base_v = s_v.n

                        @block.tensor
                        def _(t, j=j, base_v=base_v):
                            s_v.wait(t) if j == 0 else None
                            if j >= 2:
                                t.wait_ge(s_v.h, base_v)
                            s_pe.inc(t.matmul(
                                ps_n[j % 2][:, :], hmask[:, :],
                                SQ[:, j * 512:(j + 1) * 512], start=True, stop=True))

                        @block.vector
                        def _(v, j=j, base_pe=base_pe):
                            v.wait_ge(s_pe.h, base_pe + 1)
                            with nc.allow_low_precision(reason="bf16 matmul operand"):
                                s_v.inc(v.tensor_copy(QNS[:, j * 512:(j + 1) * 512],
                                                      ps_n[j % 2][:, :]))

                    @block.sync
                    def _(sy, kc=kc):
                        s_v.wait(sy)
                        for hh in range(2):
                            s_sp.inc(sy.dma_start(
                                out=qn_d[2 * kc:2 * kc + 2,
                                         hh * (NTOT // 2):(hh + 1) * (NTOT // 2)],
                                in_=QNS[:, hh * (NTOT // 2):(hh + 1) * (NTOT // 2)]), 16)

        ctx.barrier()


def emit_l0_attn2(ctx, xT_h, x1_h, wo_d, bq_d, bo_d, qkT_d, vtok_d, qn_d, km_d,
                  tag="p2"):
    nc, block = ctx.nc, ctx.block
    s_ld = ctx.sem(tag + "ld")
    s_sp = ctx.sem(tag + "sp")
    s_at = ctx.sem(tag + "at")
    s_pe = ctx.sem(tag + "pe")
    s_v = ctx.sem(tag + "vv")
    s_a = ctx.sem(tag + "aa")
    s_g = ctx.sem(tag + "gg")
    s_cs = ctx.sem(tag + "cs")
    QG = 1024

    ones2 = ctx.sbuf(tag + "ones2", [128, 2], BF16)
    ones_row = ctx.sbuf(tag + "onesr", [1, 128], BF16)

    with contextlib.ExitStack() as sc2:
        WO = ctx.sbuf(tag + "wo", [128, FC * 1024], BF16, scope=sc2)
        bv_sb = ctx.sbuf(tag + "bv", [128, 8], BF16, scope=sc2)
        bo_sb = ctx.sbuf(tag + "bo", [128, 8], F32, scope=sc2)
        qa = [ctx.sbuf(tag + "qa%d" % i, [65, QG], BF16, scope=sc2) for i in range(2)]
        ka = [ctx.sbuf(tag + "ka%d" % i, [65, S], BF16, scope=sc2) for i in range(2)]
        va = [ctx.sbuf(tag + "va%d" % i, [128, 16 * 65], BF16, scope=sc2) for i in range(2)]
        ex = [ctx.sbuf(tag + "ex%d" % i, [128, QG], BF16, scope=sc2) for i in range(2)]
        rb_sb = ctx.sbuf(tag + "rb", [64, QG], BF16, scope=sc2)
        rec_row = ctx.sbuf(tag + "rec", [1, QG], BF16, scope=sc2)
        outn = ctx.sbuf(tag + "outn", [128, FC * QG], BF16, scope=sc2)
        KMS = ctx.sbuf(tag + "kms", [65, 16], F32, scope=sc2)
        XLD = ctx.sbuf(tag + "xld", [128, QG], F32, scope=sc2)
        OST = ctx.sbuf(tag + "ost", [128, QG], F32, scope=sc2)

        ps_s = [sc2.enter_context(nc.psum_tensor(tag + "ps_s%d" % i, [128, QG], F32))
                for i in range(2)]
        ps_av = sc2.enter_context(nc.psum_tensor(tag + "ps_av", [65, QG], F32))
        ps_rbb = sc2.enter_context(nc.psum_tensor(tag + "ps_rbb", [64, QG], F32))
        ps_bo = ps_s[0]

        @block.gpsimd
        def _(g):
            s_ld.inc(g.dma_start(
                out=WO[:, :].rearrange("p (kc m) -> p kc m", kc=FC),
                in_=wo_d.rearrange("(kc p) m -> p kc m", p=128)), 16)
            s_ld.inc(g.dma_start(out=bv_sb[:, :], in_=bq_d[:, 16:24]), 16)
            s_ld.inc(g.dma_start(out=bo_sb[:, :], in_=bo_d[:, :]), 16)
            s_cs.inc(g.memset(ones2[:, 0:1], -1.0 / D))
            s_cs.wait(g)
            g.memset(ones2[:, 1:2], 1.0)
            g.memset(ones_row[:, :], 1.0)
            for i in range(2):
                g.memset(va[i][:, :], 1.0)
                s_g.inc(g.memset(ka[i][64:65, :], 1.0))

        for ms in range(8):
            @block.tensor
            def _(t, ms=ms):
                if ms == 0:
                    s_ld.wait(t)
                    s_g.wait(t)
                else:
                    s_v.wait(t)
                last = None
                for kc in range(FC):
                    last = t.matmul(ps_bo[:, 0:1],
                                    WO[:, kc * 1024 + ms * 128:kc * 1024 + (ms + 1) * 128],
                                    bv_sb[:, kc:kc + 1],
                                    start=(kc == 0), stop=(kc == FC - 1))
                s_pe.inc(last)

            @block.vector
            def _(v, ms=ms):
                s_pe.wait(v)
                s_v.inc(v.tensor_add(bo_sb[:, ms:ms + 1], bo_sb[:, ms:ms + 1], ps_bo[:, 0:1]))

        last_end = {0: 0, 1: 0}
        for b in range(BTOT):
            for qg in range(S // QG):
                for h in range(H):
                    ib = h % 2

                    @block.sync
                    def _(sy, h=h, ib=ib, b=b, qg=qg, free_at=last_end[ib]):
                        sy.wait_ge(s_pe.h, free_at)
                        s_sp.wait(sy)
                        s_g.wait(sy)
                        if h == 0 and qg == 0:
                            s_v.wait(sy)
                            s_at.inc(sy.dma_start(
                                out=KMS[64:65, :],
                                in_=bass.AP(km_d, b, [[256, 1], [16, 16]]),
                                allow_slow_non_contiguous=True), 16)
                        s_at.inc(sy.dma_start(
                            out=qa[ib][64:65, :],
                            in_=qn_d[h:h + 1, b * S + qg * QG:b * S + (qg + 1) * QG]), 16)
                        s_at.inc(sy.dma_start(
                            out=qa[ib][0:64, :],
                            in_=qkT_d[h * 64:(h + 1) * 64,
                                      b * S + qg * QG:b * S + (qg + 1) * QG]), 16)
                        s_at.inc(sy.dma_start(
                            out=ka[ib][0:64, :],
                            in_=qkT_d[1024 + h * 64:1024 + (h + 1) * 64, b * S:(b + 1) * S]), 16)
                        s_at.inc(sy.dma_start(
                            out=va[ib][:, :].rearrange("p (kc d) -> p kc d", kc=16)[:, :, 0:64],
                            in_=vtok_d[b * S:(b + 1) * S, h * 64:(h + 1) * 64]
                            .rearrange("(kc p) d -> p kc d", p=128)), 16)

                    @block.vector
                    def _(v, h=h, ib=ib):
                        s_at.wait(v)
                        with nc.allow_low_precision(reason="bf16 matmul operand"):
                            s_v.inc(v.tensor_scalar(qa[ib][64:65, :], qa[ib][64:65, :],
                                                    KMS[64:65, h:h + 1], -0.5,
                                                    ALU.add, ALU.mult))

                    base_a = s_a.n
                    base_pe = s_pe.n

                    def score_sec(kc, ib=ib, base_a=base_a):
                        @block.tensor
                        def _(t, kc=kc, ib=ib):
                            if kc == 0:
                                s_v.wait(t)
                                s_at.wait(t)
                                s_a.wait(t)
                            elif kc >= 2:
                                t.wait_ge(s_a.h, base_a + kc - 1)
                            last = None
                            for half in range(2):
                                last = t.matmul(
                                    ps_s[kc % 2][:, half * 512:(half + 1) * 512],
                                    ka[ib][:, kc * 128:(kc + 1) * 128],
                                    qa[ib][:, half * 512:(half + 1) * 512],
                                    start=True, stop=True)
                            s_pe.inc(last)

                    def exp_all(base_pe=base_pe):
                        @block.scalar
                        def _(sc):
                            for kc in range(16):
                                w = 1 if kc == 0 else (2 if kc == 1 else 2 * kc)
                                sc.wait_ge(s_pe.h, base_pe + w)
                                s_a.inc(sc.activation(out=ex[kc % 2][:, :],
                                                      in_=ps_s[kc % 2][:, :],
                                                      func=AF.Exp, scale=0.125))

                    def av_sec(kc, ib=ib, base_a=base_a):
                        @block.tensor
                        def _(t, kc=kc, ib=ib):
                            t.wait_ge(s_a.h, base_a + kc + 1)
                            last = None
                            for half in range(2):
                                last = t.matmul(
                                    ps_av[:, half * 512:(half + 1) * 512],
                                    va[ib][:, kc * 65:(kc + 1) * 65],
                                    ex[kc % 2][:, half * 512:(half + 1) * 512],
                                    start=(kc == 0), stop=(kc == 15))
                            s_pe.inc(last)

                    score_sec(0)
                    exp_all()
                    score_sec(1)
                    for kc in range(2, 16):
                        av_sec(kc - 2)
                        score_sec(kc)
                    av_sec(14)
                    av_sec(15)
                    last_end[ib] = s_pe.n

                    @block.vector
                    def _(v):
                        s_pe.wait(v)
                        with nc.allow_low_precision(reason="bf16 matmul operand"):
                            s_v.inc(v.reciprocal(rec_row[:, :], ps_av[64:65, :]))

                    @block.tensor
                    def _(t):
                        s_v.wait(t)
                        t.matmul(ps_rbb[:, 0:512], ones_row[:, 0:64], rec_row[:, 0:512],
                                 start=True, stop=True)
                        s_pe.inc(t.matmul(ps_rbb[:, 512:1024], ones_row[:, 0:64],
                                          rec_row[:, 512:1024], start=True, stop=True))

                    @block.vector
                    def _(v, h=h):
                        s_pe.wait(v)
                        with nc.allow_low_precision(reason="bf16 matmul operand"):
                            s_a.inc(v.tensor_copy(rb_sb[:, :], ps_rbb[:, :]))
                        s_a.wait(v)
                        ch = h // 2
                        rowbase = (h % 2) * 64
                        s_v.inc(v.tensor_mul(outn[rowbase:rowbase + 64, ch * QG:(ch + 1) * QG],
                                             ps_av[0:64, :], rb_sb[:, :]))

                for ms in range(8):
                    g0 = b * S + qg * QG

                    @block.sync
                    def _(sy, ms=ms, g0=g0):
                        s_v.wait(sy)
                        s_sp.wait(sy)
                        s_g.inc(sy.dma_start(out=XLD[:, :],
                                             in_=sl_row_ap(xT_h, ms * 128, g0, QG)), 16)

                    @block.tensor
                    def _(t, ms=ms):
                        s_v.wait(t)
                        s_a.wait(t)
                        for half in range(2):
                            ps = ps_s[half]
                            last = None
                            for kc in range(FC):
                                last = t.matmul(
                                    ps[:, 0:512],
                                    WO[:, kc * 1024 + ms * 128:kc * 1024 + (ms + 1) * 128],
                                    outn[:, kc * QG + half * 512:kc * QG + (half + 1) * 512],
                                    start=(kc == 0), stop=(kc == FC - 1))
                            s_pe.inc(last)

                    @block.vector
                    def _(v, ms=ms):
                        base = s_pe.n - 2
                        s_g.wait(v)
                        for half in range(2):
                            ps = ps_s[half]
                            v.wait_ge(s_pe.h, base + half + 1)
                            s_v.inc(v.scalar_tensor_tensor(
                                OST[:, half * 512:(half + 1) * 512], ps[:, 0:512],
                                bo_sb[:, ms:ms + 1], XLD[:, half * 512:(half + 1) * 512],
                                ALU.add, ALU.add))

                    @block.scalar
                    def _(sc, ms=ms, g0=g0):
                        s_v.wait(sc)
                        s_sp.inc(sc.dma_start(out=sl_row_ap(x1_h, ms * 128, g0, QG),
                                              in_=OST[:, :]), 16)

                ctx.barrier()



def emit_ln_all(ctx, x1_h, lnx_h, tag="p3"):
    TW = 512
    nc, block = ctx.nc, ctx.block
    s_sp = ctx.sem(tag + "sp")
    s_pe = ctx.sem(tag + "pe")
    s_v = ctx.sem(tag + "vv")
    s_a = ctx.sem(tag + "aa")
    s_g = ctx.sem(tag + "gg")

    with contextlib.ExitStack() as sca:
        consts, _ = _mk_ln_consts(ctx, sca, tag, tw=TW)
        XRC = ctx.sbuf(tag + "xrc", [128, FC * TW], F32, scope=sca)
        XC = ctx.sbuf(tag + "xc", [128, FC * TW], F32, scope=sca)
        LNB = ctx.sbuf(tag + "lnb", [128, FC * TW], BF16, scope=sca)

        for tc in range(NTOT // TW):
            c0 = tc * TW

            @block.sync
            def _(sy, c0=c0):
                s_a.wait(sy)
                s_v.wait(sy)
                s_pe.wait(sy)
                s_sp.wait(sy)
                s_g.inc(sy.dma_start(
                    out=XRC[:, :].rearrange("p (kc n) -> p kc n", kc=FC),
                    in_=sl_ap(x1_h, c0, TW)), 16)

            @block.tensor
            def _(t):
                s_g.wait(t)

            @block.vector
            def _(v):
                s_g.wait(v)

            @block.scalar
            def _(sc):
                s_g.wait(sc)

            emit_ln(ctx, XRC, LNB, XC, consts, 0, (s_pe, s_v, s_a), xstride=TW, tw=TW)

            @block.sync
            def _(sy, c0=c0):
                s_v.wait(sy)
                s_sp.inc(sy.dma_start(
                    out=sl_ap(lnx_h, c0, TW),
                    in_=LNB[:, :].rearrange("p (kc n) -> p kc n", kc=FC)), 16)

        ctx.barrier()



def emit_scan(ctx, lnx_h, wT, uT, bcol_d, inc_d, cc_d, cb_d, tag="p5"):
    nc, block = ctx.nc, ctx.block
    s_w = ctx.sem(tag + "wl")
    s_pe = ctx.sem(tag + "pe")
    s_v = ctx.sem(tag + "vv")
    s_a = ctx.sem(tag + "aa")
    s_g = ctx.sem(tag + "gg")
    s_st = ctx.sem(tag + "st")

    bcol = ctx.sbuf(tag + "bcol", [128, 32], F32)
    incc = ctx.sbuf(tag + "incc", [128, 8], F32)

    def cc_ap(sc):
        return cc_d.rearrange("(kc p) n -> p kc n", p=128)[:, :, sc * 512:(sc + 1) * 512]

    def cb_ap(sc):
        return cb_d.rearrange("(kc p) n -> p kc n", p=128)[:, :, sc * 512:(sc + 1) * 512]

    with contextlib.ExitStack() as scs:
        WUa = [ctx.sbuf(tag + "wu%d" % i, [128, FC * 1024], BF16, scope=scs)
               for i in range(6)]
        CCc = [ctx.sbuf(tag + "ccc%d" % i, [128, FC * 512], F32, scope=scs)
               for i in range(2)]
        CBin = [ctx.sbuf(tag + "cbi%d" % i, [128, FC * 512], BF16, scope=scs)
                for i in range(2)]
        CBout = [ctx.sbuf(tag + "cbo%d" % i, [128, FC * 512], BF16, scope=scs)
                 for i in range(2)]
        LNXc = [ctx.sbuf(tag + "lnxc%d" % i, [128, FC * 512], BF16, scope=scs)
                for i in range(2)]
        G1 = ctx.sbuf(tag + "g1", [128, 512], F32, scope=scs)
        G2 = ctx.sbuf(tag + "g2", [128, 512], F32, scope=scs)
        G3 = ctx.sbuf(tag + "g3", [128, 512], F32, scope=scs)
        bch = ctx.sbuf(tag + "bch", [128, 32], F32, scope=scs)
        ps_p = [scs.enter_context(nc.psum_tensor(tag + "ps%d" % i, [128, 512], F32))
                for i in range(3)]

        @block.gpsimd
        def _(g):
            s_w.inc(g.dma_start(out=bcol[:, :], in_=bcol_d[:, :]), 16)
            s_w.inc(g.dma_start(out=incc[:, :], in_=inc_d[:, :]), 16)
            for gi in range(3):
                s_w.inc(g.dma_start(
                    out=WUa[2 * gi][:, :].rearrange("p (kc m) -> p kc m", kc=FC),
                    in_=wT[gi].rearrange("(kc p) m -> p kc m", p=128)), 16)
                s_w.inc(g.dma_start(
                    out=WUa[2 * gi + 1][:, :].rearrange("p (kc m) -> p kc m", kc=FC),
                    in_=uT[gi].rearrange("(kc p) m -> p kc m", p=128)), 16)

        @block.vector
        def _(v):
            s_w.wait(v)
            s_v.inc(v.tensor_scalar(bch[:, :], bcol[:, :], 0.5, None, ALU.mult))
            with nc.allow_low_precision(reason="bf16 matmul operand"):
                for wi in range(4):
                    s_w.inc(v.tensor_scalar(WUa[wi][:, :], WUa[wi][:, :], 0.5,
                                            None, ALU.mult), 1)

        for sc_i in range(4):
            @block.vector
            def _(v):
                s_w.wait(v)
                s_st.wait(v)
                last = None
                for kc in range(FC):
                    last = v.tensor_copy(CCc[0][:, kc * 512:(kc + 1) * 512],
                                         bass.AP(incc, kc, [[8, 128], [0, 512]]))
                s_v.inc(last)
                s_v.wait(v)
                for kc in range(FC):
                    last = v.tensor_copy(CBout[0][:, kc * 512:(kc + 1) * 512],
                                         CCc[0][:, kc * 512:(kc + 1) * 512])
                s_v.inc(last)

            @block.scalar
            def _(sc, sc_i=sc_i):
                s_v.wait(sc)
                s_st.inc(sc.dma_start(
                    out=cc_ap(sc_i),
                    in_=CCc[0][:, :].rearrange("p (kc n) -> p kc n", kc=FC)), 16)
                s_st.inc(sc.dma_start(
                    out=cb_ap(sc_i),
                    in_=CBout[0][:, :].rearrange("p (kc n) -> p kc n", kc=FC)), 16)

        pe_end = []
        v_end = []
        st_end = []
        st_init = s_st.n
        v_init = s_v.n
        for t_ in range(BTOT):
            for sc_i in range(4):
                it = t_ * 4 + sc_i
                sl_ = it % 2
                g0 = t_ * S + sc_i * 512
                CCs, CBi, CBo, LNs = CCc[sl_], CBin[sl_], CBout[sl_], LNXc[sl_]

                @block.sync
                def _(sy, it=it, sc_i=sc_i, g0=g0, CCs=CCs, CBi=CBi, LNs=LNs):
                    if it >= 2:
                        sy.wait_ge(s_pe.h, pe_end[it - 2])
                        sy.wait_ge(s_v.h, v_end[it - 2])
                        sy.wait_ge(s_st.h, st_end[it - 2])
                    else:
                        sy.wait_ge(s_st.h, st_init)
                        sy.wait_ge(s_v.h, v_init)
                    s_g.inc(sy.dma_start(
                        out=CCs[:, :].rearrange("p (kc n) -> p kc n", kc=FC),
                        in_=cc_ap(sc_i)), 16)
                    s_g.inc(sy.dma_start(
                        out=CBi[:, :].rearrange("p (kc n) -> p kc n", kc=FC),
                        in_=cb_ap(sc_i)), 16)
                    s_g.inc(sy.dma_start(
                        out=LNs[:, :].rearrange("p (kc n) -> p kc n", kc=FC),
                        in_=sl_ap(lnx_h, g0, 512)), 16)

                ch_pe = s_pe.n
                ch_a = s_a.n
                ch_v = s_v.n
                g_need = s_g.n
                for ms in range(8):
                    @block.tensor
                    def _(t, ms=ms, ch_a=ch_a, g_need=g_need, CBi=CBi, LNs=LNs):
                        t.wait_ge(s_g.h, g_need)
                        if ms == 0:
                            s_w.wait(t)
                        t.wait_ge(s_a.h, ch_a + 3 * ms)
                        for oi, gi in enumerate((2, 1, 0)):
                            ps = ps_p[oi]
                            last = None
                            for kc in range(FC):
                                last = t.matmul(
                                    ps[:, :],
                                    WUa[2 * gi][:, kc * 1024 + ms * 128:kc * 1024 + (ms + 1) * 128],
                                    LNs[:, kc * 512:(kc + 1) * 512],
                                    start=(kc == 0), stop=False)
                            for kc in range(FC):
                                last = t.matmul(
                                    ps[:, :],
                                    WUa[2 * gi + 1][:, kc * 1024 + ms * 128:kc * 1024 + (ms + 1) * 128],
                                    CBi[:, kc * 512:(kc + 1) * 512],
                                    start=False, stop=(kc == FC - 1))
                            s_pe.inc(last)

                    @block.vector
                    def _(v, ms=ms, ch_a=ch_a, CCs=CCs, CBo=CBo):
                        v.wait_ge(s_a.h, ch_a + 3 * ms + 2)
                        s_v.inc(v.tensor_scalar(G2[:, :], G2[:, :], 0.5, 0.5,
                                                ALU.mult, ALU.add))
                        s_v.wait(v)
                        s_v.inc(v.tensor_mul(G2[:, :], G2[:, :], G1[:, :]))
                        v.wait_ge(s_a.h, ch_a + 3 * ms + 3)
                        s_v.inc(v.tensor_scalar(G3[:, :], G3[:, :], 0.5, 0.5,
                                                ALU.mult, ALU.add))
                        s_v.wait(v)
                        sl2 = CCs[:, ms * 512:(ms + 1) * 512]
                        s_v.inc(v.tensor_mul(sl2, sl2, G3[:, :]))
                        s_v.wait(v)
                        s_v.inc(v.tensor_add(sl2, sl2, G2[:, :]))
                        s_v.wait(v)
                        s_v.inc(v.tensor_copy(CBo[:, ms * 512:(ms + 1) * 512], sl2))

                @block.scalar
                def _(sc, ch_pe=ch_pe, ch_v=ch_v):
                    for ms in range(8):
                        sc.wait_ge(s_v.h, ch_v + 6 * ms)
                        sc.wait_ge(s_pe.h, ch_pe + 3 * ms + 1)
                        s_a.inc(sc.activation(out=G1[:, :], in_=ps_p[0][:, :],
                                              func=AF.Tanh,
                                              bias=bcol[:, 16 + ms:17 + ms]))
                        sc.wait_ge(s_pe.h, ch_pe + 3 * ms + 2)
                        s_a.inc(sc.activation(out=G2[:, :], in_=ps_p[1][:, :],
                                              func=AF.Tanh,
                                              bias=bch[:, 8 + ms:9 + ms]))
                        sc.wait_ge(s_pe.h, ch_pe + 3 * ms + 3)
                        s_a.inc(sc.activation(out=G3[:, :], in_=ps_p[2][:, :],
                                              func=AF.Tanh,
                                              bias=bch[:, ms:ms + 1]))

                pe_end.append(s_pe.n)
                v_end.append(s_v.n)

                @block.scalar
                def _(sc, sc_i=sc_i, CCs=CCs, CBo=CBo):
                    s_v.wait(sc)
                    s_st.inc(sc.dma_start(
                        out=cc_ap(sc_i),
                        in_=CCs[:, :].rearrange("p (kc n) -> p kc n", kc=FC)), 16)
                    s_st.inc(sc.dma_start(
                        out=cb_ap(sc_i),
                        in_=CBo[:, :].rearrange("p (kc n) -> p kc n", kc=FC)), 16)
                st_end.append(s_st.n)

        ctx.barrier()



def emit_own_tail(ctx, x1_h, lnx_h, idx_d, xown_h, lnown_h, cb_d, wo3_d, uo_d,
                  bcol_d, y2_h, tag="p7"):
    nc, block = ctx.nc, ctx.block
    s_w = ctx.sem(tag + "wl")
    s_pe = ctx.sem(tag + "pe")
    s_v = ctx.sem(tag + "vv")
    s_a = ctx.sem(tag + "aa")
    s_g = ctx.sem(tag + "gg")
    s_st = ctx.sem(tag + "st")

    bcol = ctx.sbuf(tag + "bcol", [128, 32], F32)

    with contextlib.ExitStack() as scu:
        UOT = ctx.sbuf(tag + "uot", [128, FC * 2048], F32, scope=scu)
        with contextlib.ExitStack() as scv:
            CBF = ctx.sbuf(tag + "cbf", [128, FC * 2048], BF16, scope=scv)
            UOW = ctx.sbuf(tag + "uow", [128, FC * 1024], BF16, scope=scv)
            ps_p = [scv.enter_context(nc.psum_tensor(tag + "psu%d" % i, [128, 512], F32))
                    for i in range(2)]

            @block.gpsimd
            def _(g):
                s_w.inc(g.dma_start(out=bcol[:, :], in_=bcol_d[:, :]), 16)
                s_w.inc(g.dma_start(
                    out=UOW[:, :].rearrange("p (kc m) -> p kc m", kc=FC),
                    in_=uo_d.rearrange("(kc p) m -> p kc m", p=128)), 16)
                s_w.inc(g.dma_start(
                    out=CBF[:, :].rearrange("p (kc n) -> p kc n", kc=FC),
                    in_=cb_d.rearrange("(kc p) n -> p kc n", p=128)), 16)

            for ms in range(8):
                for sc_i in range(4):
                    it = ms * 4 + sc_i
                    base_v = s_v.n

                    @block.tensor
                    def _(t, ms=ms, sc_i=sc_i, it=it, base_v=base_v):
                        if it == 0:
                            s_w.wait(t)
                        ps = ps_p[it % 2]
                        if it >= 2:
                            t.wait_ge(s_v.h, base_v - 1)
                        last = None
                        for kc in range(FC):
                            last = t.matmul(
                                ps[:, :],
                                UOW[:, kc * 1024 + ms * 128:kc * 1024 + (ms + 1) * 128],
                                CBF[:, kc * 2048 + sc_i * 512:kc * 2048 + (sc_i + 1) * 512],
                                start=(kc == 0), stop=(kc == FC - 1))
                        s_pe.inc(last)

                    @block.vector
                    def _(v, ms=ms, sc_i=sc_i, it=it):
                        v.wait_ge(s_pe.h, it + 1)
                        ps = ps_p[it % 2]
                        s_v.inc(v.tensor_copy(
                            UOT[:, ms * 2048 + sc_i * 512:ms * 2048 + (sc_i + 1) * 512],
                            ps[:, :]))

            ctx.barrier()

        with contextlib.ExitStack() as scg:
            idx_sb = ctx.sbuf(tag + "idx", [128, 8], U32, scope=scg)
            STF = ctx.sbuf(tag + "stf", [128, 4096], F32, scope=scg)
            STB = ctx.sbuf(tag + "stb", [128, 4096], BF16, scope=scg)

            @block.gpsimd
            def _(g):
                s_g.inc(g.dma_start(out=idx_sb[:, :], in_=idx_d), 16)

            for kc in range(FC):
                @block.gpsimd
                def _(g, kc=kc):
                    s_g.wait(g)
                    s_st.wait(g)
                    s_at = g.indirect_dma_start(
                        out=STF[:, :], out_offset=None,
                        in_=x1_h.ap(),
                        in_offset=bass.IndirectOffsetOnAxis(ap=idx_sb[:, kc:kc + 1], axis=0))
                    s_g.inc(s_at, 16)
                    s_g.inc(g.indirect_dma_start(
                        out=STB[:, :], out_offset=None,
                        in_=lnx_h.ap(),
                        in_offset=bass.IndirectOffsetOnAxis(ap=idx_sb[:, kc:kc + 1], axis=0)), 16)

                @block.sync
                def _(sy, kc=kc):
                    s_g.wait(sy)
                    s_st.inc(sy.dma_start(out=xown_h.ap()[kc * 128:(kc + 1) * 128, :],
                                          in_=STF[:, :]), 16)
                    s_st.inc(sy.dma_start(out=lnown_h.ap()[kc * 128:(kc + 1) * 128, :],
                                          in_=STB[:, :]), 16)

            ctx.barrier()

        with contextlib.ExitStack() as scd:
            WO3 = ctx.sbuf(tag + "wo3", [128, FC * 1024], BF16, scope=scd)
            LNOW = ctx.sbuf(tag + "lnow", [128, FC * 512], BF16, scope=scd)
            XOW = ctx.sbuf(tag + "xow", [128, FC * 512], F32, scope=scd)
            OST = ctx.sbuf(tag + "ostd", [128, FC * 512], F32, scope=scd)
            TT = ctx.sbuf(tag + "tt", [128, 512], F32, scope=scd)
            HT = ctx.sbuf(tag + "ht", [128, 512], F32, scope=scd)
            ps_p = [scd.enter_context(nc.psum_tensor(tag + "psd%d" % i, [128, 512], F32))
                    for i in range(2)]

            @block.gpsimd
            def _(g):
                s_w.inc(g.dma_start(
                    out=WO3[:, :].rearrange("p (kc m) -> p kc m", kc=FC),
                    in_=wo3_d.rearrange("(kc p) m -> p kc m", p=128)), 16)

            for tc in range(8):
                c0 = tc * 512
                s0 = (tc % 4) * 512

                @block.sync
                def _(sy, c0=c0):
                    s_pe.wait(sy)
                    s_v.wait(sy)
                    s_st.wait(sy)
                    s_g.inc(sy.dma_start(
                        out=LNOW[:, :].rearrange("p (kc n) -> p kc n", kc=FC),
                        in_=lnown_h.ap().rearrange("(kc p) n -> p kc n", p=128)
                        [:, :, c0:c0 + 512]), 16)
                    s_g.inc(sy.dma_start(
                        out=XOW[:, :].rearrange("p (kc n) -> p kc n", kc=FC),
                        in_=xown_h.ap().rearrange("(kc p) n -> p kc n", p=128)
                        [:, :, c0:c0 + 512]), 16)

                tc_v0 = s_v.n
                for ms in range(8):
                    base_pe = s_pe.n
                    base_a = s_a.n

                    @block.tensor
                    def _(t, ms=ms, tc_v0=tc_v0):
                        s_g.wait(t)
                        if ms == 0:
                            s_w.wait(t)
                        ps = ps_p[ms % 2]
                        if ms >= 2:
                            t.wait_ge(s_v.h, tc_v0 + 2 * (ms - 2) + 1)
                        last = None
                        for kc in range(FC):
                            last = t.matmul(
                                ps[:, :],
                                WO3[:, kc * 1024 + ms * 128:kc * 1024 + (ms + 1) * 128],
                                LNOW[:, kc * 512:(kc + 1) * 512],
                                start=(kc == 0), stop=(kc == FC - 1))
                        s_pe.inc(last)

                    @block.vector
                    def _(v, ms=ms, s0=s0, base_pe=base_pe):
                        v.wait_ge(s_pe.h, base_pe + 1)
                        ps = ps_p[ms % 2]
                        s_v.inc(v.tensor_add(TT[:, :], ps[:, :],
                                             UOT[:, ms * 2048 + s0:ms * 2048 + s0 + 512]))

                    @block.scalar
                    def _(sc, ms=ms):
                        s_v.wait(sc)
                        s_a.inc(sc.activation(out=HT[:, :], in_=TT[:, :],
                                              func=AF.Tanh, bias=bcol[:, 24 + ms:25 + ms]))

                    @block.vector
                    def _(v, ms=ms):
                        s_a.wait(v)
                        s_v.inc(v.tensor_add(OST[:, ms * 512:(ms + 1) * 512],
                                             XOW[:, ms * 512:(ms + 1) * 512], HT[:, :]))

                @block.scalar
                def _(sc, c0=c0):
                    s_v.wait(sc)
                    s_st.inc(sc.dma_start(
                        out=y2_h.ap().rearrange("(kc p) n -> p kc n", p=128)[:, :, c0:c0 + 512],
                        in_=OST[:, :].rearrange("p (kc n) -> p kc n", kc=FC)), 16)

            ctx.barrier()



def emit_l1_attn(ctx, xprov, xrow, yout_d, wqT_d, woT_d, qkT_d, vtok_d, tag="b1"):
    nc, block = ctx.nc, ctx.block
    s_ld = ctx.sem(tag + "ld")
    s_sp = ctx.sem(tag + "sp")
    s_at = ctx.sem(tag + "at")
    s_pe = ctx.sem(tag + "pe")
    s_v = ctx.sem(tag + "vv")
    s_a = ctx.sem(tag + "aa")
    s_g = ctx.sem(tag + "gg")
    s_cs = ctx.sem(tag + "cs")
    PI = 3.14159265358979323846

    ones2 = ctx.sbuf(tag + "ones2", [128, 2], BF16)
    onesf = ctx.sbuf(tag + "onesf", [128, 1], F32)
    ones_row = ctx.sbuf(tag + "onesr", [1, 128], BF16)

    @block.gpsimd
    def _(g):
        s_cs.inc(g.memset(ones2[:, 0:1], -1.0 / D))
        s_cs.wait(g)
        g.memset(ones2[:, 1:2], 1.0)
        g.memset(onesf[:, 0:1], -1.0 / D)
        s_g.inc(g.memset(ones_row[:, :], 1.0))

    with contextlib.ExitStack() as sc1:
        WQ = ctx.sbuf(tag + "wq", [128, FC * 3072], BF16, scope=sc1)
        XRC = ctx.sbuf(tag + "xrc", [128, FC * TCH], F32, scope=sc1)
        XC = ctx.sbuf(tag + "xcen", [128, FC * TCH], F32, scope=sc1)
        LNB = ctx.sbuf(tag + "lnb", [128, FC * TCH], BF16, scope=sc1)
        STG = ctx.sbuf(tag + "stg", [128, 8 * TCH], BF16, scope=sc1)
        VSTG = ctx.sbuf(tag + "vstg", [128, 2 * 1024], BF16, scope=sc1)
        row1 = ctx.sbuf(tag + "row1", [1, TCH], BF16, scope=sc1)
        rowr = ctx.sbuf(tag + "rowr", [1, TCH], BF16, scope=sc1)
        rowv = ctx.sbuf(tag + "rowv", [1, TCH], F32, scope=sc1)

        ps_sum = sc1.enter_context(nc.psum_tensor(tag + "ps_sum", [1, TCH], F32))
        ps_mb = sc1.enter_context(nc.psum_tensor(tag + "ps_mb", [128, TCH], F32))
        ps_rb = sc1.enter_context(nc.psum_tensor(tag + "ps_rb", [128, TCH], F32))
        ps_q0 = sc1.enter_context(nc.psum_tensor(tag + "ps_q0", [128, TCH], F32))
        ps_q1 = sc1.enter_context(nc.psum_tensor(tag + "ps_q1", [128, TCH], F32))
        ps_v0 = sc1.enter_context(nc.psum_tensor(tag + "ps_v0", [128, 512], F32))
        ps_v1 = sc1.enter_context(nc.psum_tensor(tag + "ps_v1", [128, 512], F32))
        consts = (onesf, ones2, ones_row, ps_sum, ps_mb, ps_rb, row1, rowr, rowv)

        @block.gpsimd
        def _(g):
            s_ld.inc(g.dma_start(
                out=WQ[:, :].rearrange("p (kc m) -> p kc m", kc=FC),
                in_=wqT_d.rearrange("(kc p) m -> p kc m", p=128)), 16)

        for tc in range(NTC):
            c0 = tc * TCH

            @block.sync
            def _(sy, c0=c0):
                s_a.wait(sy)
                s_v.wait(sy)
                s_pe.wait(sy)
                s_sp.wait(sy)
                s_g.inc(sy.dma_start(
                    out=XRC[:, :].rearrange("p (kc n) -> p kc n", kc=FC),
                    in_=xprov(c0, TCH)), 16)

            @block.tensor
            def _(t):
                s_g.wait(t)
                s_ld.wait(t)

            @block.vector
            def _(v):
                s_g.wait(v)

            @block.scalar
            def _(sc):
                s_g.wait(sc)

            emit_ln(ctx, XRC, LNB, XC, consts, 0, (s_pe, s_v, s_a), xstride=TCH)

            for grp in range(2):
                base_v = s_v.n
                base_pe = s_pe.n

                @block.tensor
                def _(t, grp=grp, base_v=base_v):
                    s_v.wait(t)
                    s_sp.wait(t)
                    for j in range(8):
                        ms = grp * 8 + j
                        ps = ps_q0 if j % 2 == 0 else ps_q1
                        if j >= 2:
                            t.wait_ge(s_v.h, base_v + j - 1)
                        last = None
                        for kc in range(FC):
                            last = t.matmul(ps[:, :],
                                            WQ[:, kc * 3072 + ms * 128:kc * 3072 + (ms + 1) * 128],
                                            LNB[:, kc * TCH:(kc + 1) * TCH],
                                            start=(kc == 0), stop=(kc == FC - 1))
                        s_pe.inc(last)

                @block.vector
                def _(v, grp=grp, base_pe=base_pe):
                    for j in range(8):
                        ps = ps_q0 if j % 2 == 0 else ps_q1
                        v.wait_ge(s_pe.h, base_pe + j + 1)
                        s_v.inc(v.tensor_copy(STG[:, j * TCH:(j + 1) * TCH], ps[:, :]))

                @block.sync
                def _(sy, c0=c0, grp=grp):
                    s_v.wait(sy)
                    s_sp.inc(sy.dma_start(
                        out=qkT_d.rearrange("(ms p) n -> p ms n", p=128)[:, grp * 8:(grp + 1) * 8,
                                                                        c0:c0 + TCH],
                        in_=STG[:, :].rearrange("p (ms n) -> p ms n", ms=8)), 16)

            base_pe = s_pe.n
            base_v = s_v.n

            @block.tensor
            def _(t, base_v=base_v):
                s_sp.wait(t)
                for i in range(4):
                    ts, half = divmod(i, 2)
                    ps = ps_v0 if i % 2 == 0 else ps_v1
                    if i >= 2:
                        t.wait_ge(s_v.h, base_v + i - 1)
                    last = None
                    for kc in range(FC):
                        last = t.matmul(
                            ps[:, :],
                            LNB[:, kc * TCH + ts * 128:kc * TCH + (ts + 1) * 128],
                            WQ[:, kc * 3072 + 2048 + half * 512:kc * 3072 + 2048 + (half + 1) * 512],
                            start=(kc == 0), stop=(kc == FC - 1))
                    s_pe.inc(last)

            @block.vector
            def _(v, base_pe=base_pe):
                for i in range(4):
                    ts, half = divmod(i, 2)
                    ps = ps_v0 if i % 2 == 0 else ps_v1
                    v.wait_ge(s_pe.h, base_pe + i + 1)
                    s_v.inc(v.tensor_copy(
                        VSTG[:, ts * 1024 + half * 512:ts * 1024 + (half + 1) * 512], ps[:, :]))

            @block.scalar
            def _(sc, c0=c0):
                s_v.wait(sc)
                s_sp.inc(sc.dma_start(
                    out=vtok_d[c0:c0 + TCH, :].rearrange("(ts p) n -> p ts n", p=128),
                    in_=VSTG[:, :].rearrange("p (ts n) -> p ts n", ts=2)), 16)

        ctx.barrier()

    with contextlib.ExitStack() as sc2:
        C1 = ctx.sbuf(tag + "c1", [128, S], F32, scope=sc2)
        C2 = ctx.sbuf(tag + "c2", [128, S], F32, scope=sc2)
        T1 = ctx.sbuf(tag + "t1", [128, S], F32, scope=sc2)
        T2 = ctx.sbuf(tag + "t2", [128, S], F32, scope=sc2)
        QIN = ctx.sbuf(tag + "qin", [128, S], BF16, scope=sc2)
        KFC = ctx.sbuf(tag + "kfc", [128, S], BF16, scope=sc2)
        Pm = ctx.sbuf(tag + "pm", [128, 128], BF16, scope=sc2)
        IDN = ctx.sbuf(tag + "idn", [128, 128], BF16, scope=sc2)
        pi32 = ctx.sbuf(tag + "pi32", [128, 1], I32, scope=sc2)
        sgn = ctx.sbuf(tag + "sgn", [128, 1], F32, scope=sc2)
        tsw = ctx.sbuf(tag + "tsw", [128, 1], F32, scope=sc2)
        invf = ctx.sbuf(tag + "invf", [128, 1], F32, scope=sc2)
        QF = ctx.sbuf(tag + "qf", [128, FC * S], BF16, scope=sc2)
        KT = ctx.sbuf(tag + "kt", [128, 16 * 1024], BF16, scope=sc2)
        va = [ctx.sbuf(tag + "va%d" % i, [128, 16 * 65], BF16, scope=sc2) for i in range(2)]
        kvs = ctx.sbuf(tag + "kvs", [64, 65], BF16, scope=sc2)
        kvs_hi = ctx.sbuf(tag + "kvsh", [128, 65], BF16, scope=sc2)
        outn = ctx.sbuf(tag + "outn", [128, FC * S], BF16, scope=sc2)
        rec_row = ctx.sbuf(tag + "rec", [1, S], BF16, scope=sc2)
        rb_sb = ctx.sbuf(tag + "rb", [64, 1024], BF16, scope=sc2)
        WOUT = ctx.sbuf(tag + "wout", [128, FC * 1024], BF16, scope=sc2)
        tbl_scope = contextlib.ExitStack()
        TH = ctx.sbuf(tag + "th", [128, S], F32, scope=tbl_scope)
        TI = ctx.sbuf(tag + "ti", [128, S], I32, scope=tbl_scope)
        i5 = ctx.sbuf(tag + "i5", [128, 1], I32, scope=tbl_scope)
        ib1 = ctx.sbuf(tag + "ib1", [128, 1], I32, scope=tbl_scope)
        iw = ctx.sbuf(tag + "iw", [128, 1], I32, scope=tbl_scope)
        recf = T1

        @block.gpsimd
        def _(g):
            s_ld.inc(g.dma_start(
                out=WOUT[:, :].rearrange("p (kc m) -> p kc m", kc=FC),
                in_=woT_d.rearrange("(kc p) m -> p kc m", p=128)), 16)
            g.iota(pi32[:, :], [[1, 1]], base=0, channel_multiplier=1)
            g.iota(TH[:, :], [[1, S]], base=0, channel_multiplier=0,
                   allow_small_or_imprecise_dtypes=True)
            s_cs.inc(g.memset(IDN[:, :], 0.0))
            s_cs.wait(g)
            g.affine_select(out=IDN[:, :], in_=IDN[:, :], compare_op=ALU.not_equal,
                            fill=1.0, base=0, pattern=[[-1, 128]], channel_multiplier=1)
            for i in range(2):
                g.memset(va[i][:, :], 1.0)
            s_g.inc(g.nop())

        @block.vector
        def _(v):
            s_g.wait(v)
            s_v.inc(v.tensor_scalar(i5[:, :], pi32[:, :], 5, None, ALU.arith_shift_right))
            s_v.wait(v)
            s_v.inc(v.tensor_scalar(ib1[:, :], i5[:, :], 1, None, ALU.bitwise_and))
            s_v.wait(v)
            s_v.inc(v.tensor_scalar(iw[:, :], ib1[:, :], -64, 32, ALU.mult, ALU.add))
            s_v.wait(v)
            s_v.inc(v.tensor_tensor(iw[:, :], iw[:, :], pi32[:, :], ALU.add))
            s_v.wait(v)
            s_v.inc(v.tensor_copy(tsw[:, :], iw[:, :]))
            s_v.wait(v)
            s_v.inc(v.tensor_scalar(sgn[:, :], ib1[:, :], 2, -1, ALU.mult, ALU.add))
            s_v.wait(v)
            s_v.inc(v.tensor_scalar(Pm[:, :], TH[:, 0:128], tsw[:, 0:1], None,
                                    ALU.is_equal))
            s_v.wait(v)
            s_v.inc(v.tensor_scalar(iw[:, :], i5[:, :], -32, None, ALU.mult))
            s_v.wait(v)
            s_v.inc(v.tensor_tensor(iw[:, :], iw[:, :], pi32[:, :], ALU.add))
            s_v.wait(v)
            s_v.inc(v.tensor_copy(invf[:, :], iw[:, :]))

        @block.scalar
        def _(sc):
            s_v.wait(sc)
            import math as _m
            s_a.inc(sc.activation(out=invf[:, :], in_=invf[:, :], func=AF.Exp,
                                  scale=-_m.log(10000.0) / 32.0))

        @block.vector
        def _(v):
            s_a.wait(v)
            s_v.inc(v.tensor_scalar(TH[:, :], TH[:, :], invf[:, 0:1], None, ALU.mult))
            s_v.wait(v)
            s_v.inc(v.tensor_scalar(T1[:, :], TH[:, :], 1.0 / (2 * PI), None, ALU.mult))
            s_v.wait(v)
            s_v.inc(v.tensor_copy(TI[:, :], T1[:, :]))
            s_v.wait(v)
            s_v.inc(v.tensor_copy(T1[:, :], TI[:, :]))
            s_v.wait(v)
            s_v.inc(v.scalar_tensor_tensor(T2[:, :], T1[:, :], -2 * PI, TH[:, :],
                                           ALU.mult, ALU.add))

        @block.scalar
        def _(sc):
            s_v.wait(sc)
            s_a.inc(sc.activation(out=C2[:, :], in_=T2[:, :], func=AF.Sin))

        @block.vector
        def _(v):
            s_a.wait(v)
            s_v.inc(v.tensor_scalar(C2[:, :], C2[:, :], sgn[:, 0:1], None, ALU.mult))
            s_v.wait(v)
            s_v.inc(v.tensor_scalar(T1[:, :], TH[:, :], 1.0 / (2 * PI), 1.0 / 4.0,
                                    ALU.mult, ALU.add))
            s_v.wait(v)
            s_v.inc(v.tensor_copy(TI[:, :], T1[:, :]))
            s_v.wait(v)
            s_v.inc(v.tensor_copy(T1[:, :], TI[:, :]))
            s_v.wait(v)
            s_v.inc(v.scalar_tensor_tensor(T2[:, :], T1[:, :], -2 * PI, TH[:, :],
                                           ALU.mult, ALU.add))
            s_v.wait(v)
            s_v.inc(v.tensor_scalar(T2[:, :], T2[:, :], 1.0, PI / 2.0, ALU.mult, ALU.add))

        @block.scalar
        def _(sc):
            s_v.wait(sc)
            s_a.inc(sc.activation(out=C1[:, :], in_=T2[:, :], func=AF.Sin))

        ctx.barrier()
        tbl_scope.close()

        for b in range(NB):
            sc2b = contextlib.ExitStack()
            ps_sw = sc2b.enter_context(nc.psum_tensor(tag + "ps_sw%d" % b, [128, S], F32))
            ps_tr = sc2b.enter_context(nc.psum_tensor(tag + "ps_tr%d" % b, [128, 128], BF16))
            for c in range(FC):
                for which in ("k", "q"):
                    rows0 = 1024 if which == "k" else 0

                    @block.sync
                    def _(sy, c=c, rows0=rows0, b=b):
                        s_v.wait(sy)
                        s_a.wait(sy)
                        s_sp.wait(sy)
                        s_at.inc(sy.dma_start(
                            out=QIN[:, :],
                            in_=qkT_d[rows0 + c * 128:rows0 + (c + 1) * 128,
                                      b * S:(b + 1) * S]), 16)

                    @block.tensor
                    def _(t, c=c):
                        s_at.wait(t)
                        s_v.wait(t)
                        s_g.wait(t)
                        last = None
                        for nq in range(4):
                            last = t.matmul(ps_sw[:, nq * 512:(nq + 1) * 512], Pm[:, :],
                                            QIN[:, nq * 512:(nq + 1) * 512],
                                            start=True, stop=True)
                        s_pe.inc(last)

                    @block.vector
                    def _(v, which=which, c=c):
                        s_pe.wait(v)
                        s_a.wait(v)
                        sc_ = 0.125 if which == "q" else 1.0
                        s_v.inc(v.tensor_mul(T1[:, :], QIN[:, :], C1[:, :]))
                        s_v.wait(v)
                        s_v.inc(v.tensor_mul(T2[:, :], ps_sw[:, :], C2[:, :]))
                        s_v.wait(v)
                        s_v.inc(v.tensor_add(T1[:, :], T1[:, :], T2[:, :]))
                        s_v.wait(v)
                        s_v.inc(v.tensor_scalar(T2[:, :], T1[:, :], 0.0, sc_,
                                                ALU.min, ALU.mult))
                        s_v.wait(v)
                        s_v.inc(v.tensor_scalar(T1[:, :], T1[:, :], 0.0, sc_,
                                                ALU.max, ALU.mult))

                    @block.scalar
                    def _(sc):
                        s_v.wait(sc)
                        s_a.inc(sc.activation(out=T2[:, :], in_=T2[:, :], func=AF.Exp))

                    @block.vector
                    def _(v, which=which, c=c):
                        s_a.wait(v)
                        dst = KFC[:, :] if which == "k" else QF[:, c * S:(c + 1) * S]
                        s_v.inc(v.tensor_add(dst, T1[:, :], T2[:, :]))

                    if which == "k":
                        @block.tensor
                        def _(t, c=c):
                            s_v.wait(t)
                            for j in range(16):
                                if j > 0:
                                    t.wait_ge(s_v.h, s_v.n + j)
                                t.transpose(ps_tr[:, :], KFC[:, j * 128:(j + 1) * 128],
                                            IDN[:, :])
                                s_pe.inc(t.nop())

                        @block.vector
                        def _(v, c=c):
                            base = s_pe.n - 16
                            for j in range(16):
                                v.wait_ge(s_pe.h, base + j + 1)
                                s_v.inc(v.tensor_copy(
                                    KT[:, j * 1024 + c * 128:j * 1024 + (c + 1) * 128],
                                    ps_tr[:, :]))

            ctx.barrier()
            sc2b.close()

            with contextlib.ExitStack() as sc3:
                ps_kv = sc3.enter_context(nc.psum_tensor(tag + "ps_kv%d" % b, [64, 65], F32))
                ps_out = sc3.enter_context(nc.psum_tensor(tag + "ps_out%d" % b, [65, S], F32))
                ps_rb2 = sc3.enter_context(nc.psum_tensor(tag + "ps_rb2%d" % b, [64, 1024], F32))

                for h in range(H):
                    ib = h % 2

                    @block.sync
                    def _(sy, h=h, ib=ib, b=b):
                        s_pe.wait(sy)
                        s_g.wait(sy)
                        s_at.inc(sy.dma_start(
                            out=va[ib][:, :].rearrange("p (kc d) -> p kc d", kc=16)[:, :, 0:64],
                            in_=vtok_d[b * S:(b + 1) * S, h * 64:(h + 1) * 64]
                            .rearrange("(kc p) d -> p kc d", p=128)), 16)

                    @block.tensor
                    def _(t, h=h, ib=ib):
                        s_at.wait(t)
                        s_v.wait(t)
                        last = None
                        for j in range(16):
                            last = t.matmul(ps_kv[:, :],
                                            KT[:, j * 1024 + h * 64:j * 1024 + (h + 1) * 64],
                                            va[ib][:, j * 65:(j + 1) * 65],
                                            start=(j == 0), stop=(j == 15))
                        s_pe.inc(last)

                    @block.vector
                    def _(v):
                        s_pe.wait(v)
                        with nc.allow_low_precision(reason="bf16 matmul operand"):
                            s_v.inc(v.tensor_copy(kvs[:, :], ps_kv[:, :]))

                    if h % 2 == 1:
                        @block.gpsimd
                        def _(g):
                            s_v.wait(g)
                            s_at.inc(g.dma_start(out=kvs_hi[64:128, :], in_=kvs[:, :]), 16)

                    @block.tensor
                    def _(t, h=h):
                        s_v.wait(t)
                        s_at.wait(t)
                        ch, half64 = h // 2, (h % 2) * 64
                        lhs = kvs[:, :] if h % 2 == 0 else kvs_hi[64:128, :]
                        last = None
                        for nq in range(4):
                            last = t.matmul(ps_out[:, nq * 512:(nq + 1) * 512], lhs,
                                            QF[half64:half64 + 64,
                                               ch * S + nq * 512:ch * S + (nq + 1) * 512],
                                            start=True, stop=True)
                        s_pe.inc(last)

                    @block.vector
                    def _(v):
                        s_pe.wait(v)
                        s_v.inc(v.tensor_scalar(recf[0:1, :], ps_out[64:65, :], 1e-6,
                                                None, ALU.max))
                        s_v.wait(v)
                        with nc.allow_low_precision(reason="bf16 matmul operand"):
                            s_v.inc(v.reciprocal(rec_row[:, :], recf[0:1, :]))

                    for half in range(2):
                        @block.tensor
                        def _(t, half=half):
                            s_v.wait(t)
                            if half == 1:
                                s_a.wait(t)
                            t.matmul(ps_rb2[:, 0:512], ones_row[:, 0:64],
                                     rec_row[:, half * 1024:half * 1024 + 512],
                                     start=True, stop=True)
                            s_pe.inc(t.matmul(ps_rb2[:, 512:1024], ones_row[:, 0:64],
                                              rec_row[:, half * 1024 + 512:(half + 1) * 1024],
                                              start=True, stop=True))

                        @block.scalar
                        def _(sc, half=half):
                            s_pe.wait(sc)
                            s_v.wait(sc)
                            s_a.inc(sc.copy(out=rb_sb[:, :], in_=ps_rb2[:, :]))

                        @block.vector
                        def _(v, h=h, half=half):
                            s_a.wait(v)
                            ch, half64 = h // 2, (h % 2) * 64
                            s_v.inc(v.tensor_mul(
                                outn[half64:half64 + 64,
                                     ch * S + half * 1024:ch * S + (half + 1) * 1024],
                                ps_out[0:64, half * 1024:(half + 1) * 1024], rb_sb[:, :]))

            ctx.barrier()

            with contextlib.ExitStack() as sc4:
                XLD = ctx.sbuf(tag + "xld%d" % b, [128, S], F32, scope=sc4)
                OST = ctx.sbuf(tag + "ost%d" % b, [128, S], F32, scope=sc4)
                ps_o = [sc4.enter_context(nc.psum_tensor(tag + "ps_o%d_%d" % (b, i),
                                                          [128, 512], F32))
                        for i in range(2)]

                for ms in range(8):
                    @block.sync
                    def _(sy, ms=ms, b=b):
                        s_v.wait(sy)
                        s_g.inc(sy.dma_start(
                            out=XLD[:, :].rearrange("p (j n) -> p j n", n=256),
                            in_=xrow(ms * 128, b * S, S)), 16)

                    @block.tensor
                    def _(t, ms=ms):
                        s_v.wait(t)
                        if ms == 0:
                            s_ld.wait(t)
                        s_a.wait(t)
                        s_sp.wait(t)
                        for q4 in range(4):
                            ps = ps_o[q4 % 2]
                            if q4 >= 2:
                                t.wait_ge(s_v.h, s_v.n + q4 - 1)
                            last = None
                            for kc in range(FC):
                                last = t.matmul(
                                    ps[:, :],
                                    WOUT[:, kc * 1024 + ms * 128:kc * 1024 + (ms + 1) * 128],
                                    outn[:, kc * S + q4 * 512:kc * S + (q4 + 1) * 512],
                                    start=(kc == 0), stop=(kc == FC - 1))
                            s_pe.inc(last)

                    @block.vector
                    def _(v, ms=ms):
                        base = s_pe.n - 4
                        s_g.wait(v)
                        s_sp.wait(v)
                        for q4 in range(4):
                            ps = ps_o[q4 % 2]
                            v.wait_ge(s_pe.h, base + q4 + 1)
                            s_v.inc(v.tensor_add(OST[:, q4 * 512:(q4 + 1) * 512],
                                                 XLD[:, q4 * 512:(q4 + 1) * 512], ps[:, :]))

                    @block.scalar
                    def _(sc, ms=ms, b=b):
                        s_v.wait(sc)
                        s_sp.inc(sc.dma_start(
                            out=yout_d[ms * 128:(ms + 1) * 128, b * S:(b + 1) * S],
                            in_=OST[:, :]), 16)

            ctx.barrier()




def emit_l1_ffn(ctx, xprov, xrow, yout_d, wr_d, br_d, w1_d, b1_d, w2_d, b2_d, lnx_d,
                tag="m1"):
    nc, block = ctx.nc, ctx.block
    s_ld = ctx.sem(tag + "ld")
    s_w = ctx.sem(tag + "wl")
    s_sp = ctx.sem(tag + "sp")
    s_pe = ctx.sem(tag + "pe")
    s_v = ctx.sem(tag + "vv")
    s_a = ctx.sem(tag + "aa")
    s_g = ctx.sem(tag + "gg")
    s_cs = ctx.sem(tag + "cs")

    ones2 = ctx.sbuf(tag + "ones2", [128, 2], BF16)
    onesf = ctx.sbuf(tag + "onesf", [128, 1], F32)
    ones_row = ctx.sbuf(tag + "onesr", [1, 128], BF16)

    @block.gpsimd
    def _(g):
        s_cs.inc(g.memset(ones2[:, 0:1], -1.0 / D))
        s_cs.wait(g)
        g.memset(ones2[:, 1:2], 1.0)
        g.memset(onesf[:, 0:1], -1.0 / D)
        s_g.inc(g.memset(ones_row[:, :], 1.0))

    with contextlib.ExitStack() as sca:
        XRC = ctx.sbuf(tag + "xrc", [128, FC * TCH], F32, scope=sca)
        XC = ctx.sbuf(tag + "xc", [128, FC * TCH], F32, scope=sca)
        LNB = ctx.sbuf(tag + "lnb", [128, FC * TCH], BF16, scope=sca)
        row1 = ctx.sbuf(tag + "row1", [1, TCH], BF16, scope=sca)
        rowr = ctx.sbuf(tag + "rowr", [1, TCH], BF16, scope=sca)
        rowv = ctx.sbuf(tag + "rowv", [1, TCH], F32, scope=sca)
        ps_sum = sca.enter_context(nc.psum_tensor(tag + "ps_sum", [1, TCH], F32))
        ps_mb = sca.enter_context(nc.psum_tensor(tag + "ps_mb", [128, TCH], F32))
        ps_rb = sca.enter_context(nc.psum_tensor(tag + "ps_rb", [128, TCH], F32))
        consts = (onesf, ones2, ones_row, ps_sum, ps_mb, ps_rb, row1, rowr, rowv)

        for tc in range(NTC):
            c0 = tc * TCH

            @block.sync
            def _(sy, c0=c0):
                s_a.wait(sy)
                s_v.wait(sy)
                s_pe.wait(sy)
                s_sp.wait(sy)
                s_g.inc(sy.dma_start(
                    out=XRC[:, :].rearrange("p (kc n) -> p kc n", kc=FC),
                    in_=xprov(c0, TCH)), 16)

            @block.tensor
            def _(t):
                s_g.wait(t)

            @block.vector
            def _(v):
                s_g.wait(v)

            @block.scalar
            def _(sc):
                s_g.wait(sc)

            emit_ln(ctx, XRC, LNB, XC, consts, 0, (s_pe, s_v, s_a), xstride=TCH)

            @block.sync
            def _(sy, c0=c0):
                s_v.wait(sy)
                s_sp.inc(sy.dma_start(
                    out=lnx_d.rearrange("(kc p) n -> p kc n", p=128)[:, :, c0:c0 + TCH],
                    in_=LNB[:, :].rearrange("p (kc n) -> p kc n", kc=FC)), 16)

        ctx.barrier()

    comb = ctx.sbuf(tag + "comb", [128, 32 * 16], F32)
    combT = ctx.sbuf(tag + "combT", [16, NT], BF16)
    SEL = ctx.sbuf(tag + "sel", [16, 16 * 128], BF16)

    with contextlib.ExitStack() as scb:
        WR = ctx.sbuf(tag + "wrr", [128, FC * 16], BF16, scope=scb)
        brb = ctx.sbuf(tag + "brb", [128, 16], F32, scope=scb)
        brrow = ctx.sbuf(tag + "brrow", [1, 16], BF16, scope=scb)
        LNL = ctx.sbuf(tag + "lnl", [128, FC * 128], BF16, scope=scb)
        lg = ctx.sbuf(tag + "lg", [128, 16], F32, scope=scb)
        pr2 = ctx.sbuf(tag + "pr2", [128, 16], F32, scope=scb)
        msk1 = ctx.sbuf(tag + "msk1", [128, 16], F32, scope=scb)
        msk2 = ctx.sbuf(tag + "msk2", [128, 16], F32, scope=scb)
        m1c = ctx.sbuf(tag + "m1c", [128, 1], F32, scope=scb)
        m2c = ctx.sbuf(tag + "m2c", [128, 1], F32, scope=scb)
        rcc = ctx.sbuf(tag + "rcc", [128, 1], F32, scope=scb)
        selio = ctx.sbuf(tag + "selio", [16, 16 * 128], F32, scope=scb)
        pi16 = ctx.sbuf(tag + "pi16", [16, 1], F32, scope=scb)
        ps_r = scb.enter_context(nc.psum_tensor(tag + "ps_r", [128, 16], F32))
        ps_t = scb.enter_context(nc.psum_tensor(tag + "ps_t", [16, 128], BF16))
        cmb_bf = ctx.sbuf(tag + "cmbbf", [128, 16], BF16, scope=scb)
        idn2 = ctx.sbuf(tag + "idn2", [128, 128], BF16, scope=scb)

        @block.gpsimd
        def _(g):
            s_ld.inc(g.dma_start(
                out=WR[:, :].rearrange("p (kc m) -> p kc m", kc=FC),
                in_=wr_d.rearrange("(kc p) m -> p kc m", p=128)), 16)
            s_ld.inc(g.dma_start(out=brrow[:, :], in_=br_d[0:1, :]), 16)
            g.iota(selio[:, :].rearrange("p (e n) -> p e n", e=16), [[1, 16], [0, 128]],
                   base=0, channel_multiplier=0, allow_small_or_imprecise_dtypes=True)
            s_cs.inc(g.memset(idn2[:, :], 0.0))
            s_cs.wait(g)
            g.affine_select(out=idn2[:, :], in_=idn2[:, :], compare_op=ALU.not_equal,
                            fill=1.0, base=0, pattern=[[-1, 128]], channel_multiplier=1)
            s_g.inc(g.iota(pi16[:, :], [[1, 1]], base=0, channel_multiplier=1,
                           allow_small_or_imprecise_dtypes=True))

        @block.vector
        def _(v):
            s_g.wait(v)
            s_v.inc(v.tensor_scalar(SEL[:, :], selio[:, :], pi16[:, 0:1], None,
                                    ALU.is_equal))

        @block.tensor
        def _(t):
            s_ld.wait(t)
            s_g.wait(t)
            s_pe.inc(t.matmul(ps_r[:, :], ones_row[:, :], brrow[:, :],
                              start=True, stop=True))

        @block.vector
        def _(v):
            s_pe.wait(v)
            s_v.inc(v.tensor_copy(brb[:, :], ps_r[:, :]))

        for ts in range(32):
            @block.sync
            def _(sy, ts=ts):
                s_pe.wait(sy)
                s_v.wait(sy)
                s_g.inc(sy.dma_start(
                    out=LNL[:, :].rearrange("p (kc n) -> p kc n", kc=FC),
                    in_=lnx_d.rearrange("(kc p) n -> p kc n", p=128)
                    [:, :, ts * 128:(ts + 1) * 128]), 16)

            @block.tensor
            def _(t, ts=ts):
                s_g.wait(t)
                s_v.wait(t)
                last = None
                for kc in range(FC):
                    last = t.matmul(ps_r[:, :], LNL[:, kc * 128:(kc + 1) * 128],
                                    WR[:, kc * 16:(kc + 1) * 16],
                                    start=(kc == 0), stop=(kc == FC - 1))
                s_pe.inc(last)

            @block.vector
            def _(v, ts=ts):
                s_pe.wait(v)
                s_a.wait(v)
                s_v.inc(v.tensor_add(lg[:, :], ps_r[:, :], brb[:, :]))
                s_v.wait(v)
                s_v.inc(v.reduce_max(m1c[:, :], lg[:, :], axis=AX.X))
                s_v.wait(v)
                s_v.inc(v.tensor_scalar(m2c[:, :], m1c[:, :], -1.0, None, ALU.mult))

            @block.scalar
            def _(sc):
                s_v.wait(sc)
                s_a.inc(sc.activation(out=lg[:, :], in_=lg[:, :], func=AF.Exp,
                                      bias=m2c[:, 0:1]))

            @block.vector
            def _(v, ts=ts):
                s_a.wait(v)
                s_v.inc(v.reduce_sum(m1c[:, :], lg[:, :], axis=AX.X))
                s_v.wait(v)
                s_v.inc(v.reciprocal(rcc[:, :], m1c[:, :]))
                s_v.wait(v)
                s_v.inc(v.tensor_scalar(lg[:, :], lg[:, :], rcc[:, 0:1], None, ALU.mult))
                s_v.wait(v)
                s_v.inc(v.reduce_max(m1c[:, :], lg[:, :], axis=AX.X))
                s_v.wait(v)
                s_v.inc(v.tensor_scalar(msk1[:, :], lg[:, :], m1c[:, 0:1], None,
                                        ALU.is_equal))
                s_v.wait(v)
                s_v.inc(v.scalar_tensor_tensor(pr2[:, :], msk1[:, :], -4.0, lg[:, :],
                                               ALU.mult, ALU.add))
                s_v.wait(v)
                s_v.inc(v.reduce_max(m2c[:, :], pr2[:, :], axis=AX.X))
                s_v.wait(v)
                s_v.inc(v.tensor_scalar(msk2[:, :], pr2[:, :], m2c[:, 0:1], None,
                                        ALU.is_equal))
                s_v.wait(v)
                s_v.inc(v.tensor_add(rcc[:, :], m1c[:, :], m2c[:, :]))
                s_v.wait(v)
                s_v.inc(v.tensor_scalar(rcc[:, :], rcc[:, :], 1e-9, None, ALU.max))
                s_v.wait(v)
                s_v.inc(v.reciprocal(rcc[:, :], rcc[:, :]))
                s_v.wait(v)
                s_v.inc(v.tensor_scalar(msk1[:, :], msk1[:, :], m1c[:, 0:1], None,
                                        ALU.mult))
                s_v.wait(v)
                s_v.inc(v.tensor_scalar(msk2[:, :], msk2[:, :], m2c[:, 0:1], None,
                                        ALU.mult))
                s_v.wait(v)
                s_v.inc(v.tensor_add(msk1[:, :], msk1[:, :], msk2[:, :]))
                s_v.wait(v)
                s_v.inc(v.tensor_scalar(comb[:, ts * 16:(ts + 1) * 16], msk1[:, :],
                                        rcc[:, 0:1], None, ALU.mult))
                s_v.wait(v)
                with nc.allow_low_precision(reason="bf16 matmul operand"):
                    s_v.inc(v.tensor_copy(cmb_bf[:, :], comb[:, ts * 16:(ts + 1) * 16]))

            @block.tensor
            def _(t, ts=ts):
                s_v.wait(t)
                s_g.wait(t)
                s_pe.inc(t.transpose(ps_t[:, 0:128], cmb_bf[:, :], idn2[:, :]))

            @block.vector
            def _(v, ts=ts):
                s_pe.wait(v)
                s_v.inc(v.tensor_copy(combT[:, ts * 128:(ts + 1) * 128], ps_t[0:16, 0:128]))

        ctx.barrier()

    with contextlib.ExitStack() as scc:
        W1B = [ctx.sbuf(tag + "w1b%d" % i, [128, FC * 2048], BF16, scope=scc)
               for i in range(2)]
        W2B = ctx.sbuf(tag + "w2b", [128, 16 * 1024], BF16, scope=scc)
        ACC = ctx.sbuf(tag + "acc", [128, FC * 1024], F32, scope=scc)
        LNX = ctx.sbuf(tag + "lnx3", [128, 2 * FC * 512], BF16, scope=scc)
        HSB = ctx.sbuf(tag + "hsb", [128, 16 * 512], BF16, scope=scc)
        GTMP = ctx.sbuf(tag + "gtmp", [128, 2 * 512], F32, scope=scc)
        B2S = ctx.sbuf(tag + "b2s", [16, 1024], BF16, scope=scc)
        b1c = ctx.sbuf(tag + "b1c", [128, 256], F32, scope=scc)
        XLD = ctx.sbuf(tag + "xld", [128, 512], F32, scope=scc)
        LNL2 = ctx.sbuf(tag + "lnl2", [128, 512], BF16, scope=scc)
        OST = ctx.sbuf(tag + "ost", [128, 512], F32, scope=scc)

        ps_h = [scc.enter_context(nc.psum_tensor(tag + "ps_h%d" % i, [128, 512], F32))
                for i in range(2)]
        ps_m = [scc.enter_context(nc.psum_tensor(tag + "ps_m%d" % i, [128, 512], F32))
                for i in range(2)]
        ps_cb = scc.enter_context(nc.psum_tensor(tag + "ps_cb", [128, 512], F32))

        @block.gpsimd
        def _(g):
            s_ld.inc(g.dma_start(out=B2S[:, :], in_=b2_d[:, :]), 16)
            s_ld.inc(g.dma_start(out=b1c[:, :], in_=b1_d[:, :]), 16)

        def load_w1(g, e, buf):
            s_w.inc(g.dma_start(
                out=W1B[buf][:, :].rearrange("p (kc m) -> p kc m", kc=FC),
                in_=w1_d[e].rearrange("(kc p) m -> p kc m", p=128)), 16)

        def load_w2(g, e):
            s_w.inc(g.dma_start(
                out=W2B[:, :].rearrange("p (hc m) -> p hc m", hc=16),
                in_=w2_d[e].rearrange("(hc p) m -> p hc m", p=128)), 16)

        NPASS = 4
        for half in range(NPASS):
            hc0 = half * 1024

            for tc in range(2):
                cbase = hc0 + tc * 512

                base_v0 = s_v.n

                @block.tensor
                def _(t, tc=tc, cbase=cbase, half=half, base_v0=base_v0):
                    s_v.wait(t)
                    s_ld.wait(t)
                    for os in range(8):
                        ps = ps_m[os % 2]
                        if os >= 2:
                            t.wait_ge(s_v.h, base_v0 + os - 1)
                        s_pe.inc(t.matmul(ps[:, :], B2S[:, os * 128:(os + 1) * 128],
                                          combT[:, cbase:cbase + 512],
                                          start=True, stop=True))

                @block.vector
                def _(v, tc=tc):
                    base = s_pe.n - 8
                    for os in range(8):
                        ps = ps_m[os % 2]
                        v.wait_ge(s_pe.h, base + os + 1)
                        s_v.inc(v.tensor_copy(ACC[:, os * 1024 + tc * 512:os * 1024 + (tc + 1) * 512],
                                              ps[:, :]))

            nld = [0]

            def lnx_load(it, half=half):
                e2, tc2 = divmod(it, 2)
                cb2 = half * 1024 + tc2 * 512

                @block.sync
                def _(sy, it=it, cb2=cb2):
                    s_pe.wait(sy)
                    sl0 = (it % 2) * FC * 512
                    s_g.inc(sy.dma_start(
                        out=LNX[:, sl0:sl0 + FC * 512].rearrange(
                            "p (kc n) -> p kc n", kc=FC),
                        in_=lnx_d.rearrange("(kc p) n -> p kc n", p=128)
                        [:, :, cb2:cb2 + 512]), 16)
                nld[0] += 1

            lnx_base_g = s_g.n
            lnx_load(0)

            for e in range(16):
                @block.gpsimd
                def _(g, e=e, half=half):
                    if e == 0 and half == 0:
                        load_w1(g, 0, 0)
                        load_w2(g, 0)
                    s_pe.wait(g)
                    nxt = e + 1 + half * 16
                    if nxt < 16 * 4:
                        load_w1(g, (e + 1) % 16, (e + 1) % 2)

                @block.gpsimd
                def _(g, e=e, half=half):
                    if e > 0 or half > 0:
                        s_pe.wait(g)
                        load_w2(g, e)

                for tc in range(2):
                    cbase = hc0 + tc * 512
                    it = e * 2 + tc
                    if it + 1 < 32:
                        lnx_load(it + 1)
                    lslot = (it % 2) * FC * 512

                    @block.tensor
                    def _(t, tc=tc, cbase=cbase, e=e, it=it, lslot=lslot):
                        t.wait_ge(s_g.h, lnx_base_g + 16 * (it + 1))
                        s_w.wait(t)
                        s_v.wait(t)
                        t.matmul(ps_cb[:, :], SEL[:, e * 128:(e + 1) * 128],
                                 combT[:, cbase:cbase + 512], start=True, stop=True)
                        s_pe.inc(t.nop())
                        base_a = s_a.n
                        for hs in range(16):
                            ps = ps_h[hs % 2]
                            if hs >= 2:
                                t.wait_ge(s_a.h, base_a + hs - 1)
                            last = None
                            for kc in range(FC):
                                last = t.matmul(
                                    ps[:, :],
                                    W1B[e % 2][:, kc * 2048 + hs * 128:kc * 2048 + (hs + 1) * 128],
                                    LNX[:, lslot + kc * 512:lslot + (kc + 1) * 512],
                                    start=(kc == 0), stop=(kc == FC - 1))
                            s_pe.inc(last)

                    @block.scalar
                    def _(sc, e=e):
                        base = s_pe.n - 16
                        base_v = s_v.n
                        for hs in range(16):
                            ps = ps_h[hs % 2]
                            sc.wait_ge(s_pe.h, base + hs + 1)
                            if hs > 1:
                                sc.wait_ge(s_v.h, base_v + hs - 1)
                            s_a.inc(sc.activation(
                                out=GTMP[:, (hs % 2) * 512:(hs % 2) * 512 + 512],
                                in_=ps[:, :], func=AF.Gelu,
                                bias=b1c[:, e * 16 + hs:e * 16 + hs + 1]))

                    @block.vector
                    def _(v, e=e):
                        base = s_a.n - 16
                        for hs in range(16):
                            v.wait_ge(s_a.h, base + hs + 1)
                            s_v.inc(v.tensor_mul(
                                HSB[:, hs * 512:(hs + 1) * 512],
                                GTMP[:, (hs % 2) * 512:(hs % 2) * 512 + 512],
                                ps_cb[:, :]))

                    @block.tensor
                    def _(t, tc=tc, e=e):
                        base_mul = s_v.n - 16
                        for os in range(8):
                            ps = ps_m[os % 2]
                            if os == 0:
                                pass
                            elif os == 1:
                                t.wait_ge(s_v.h, base_mul + 16)
                            elif os >= 2:
                                t.wait_ge(s_v.h, s_v.n + os - 1)
                            last = None
                            for hc in range(16):
                                if os == 0:
                                    t.wait_ge(s_v.h, base_mul + hc + 1)
                                last = t.matmul(
                                    ps[:, :],
                                    W2B[:, hc * 1024 + os * 128:hc * 1024 + (os + 1) * 128],
                                    HSB[:, hc * 512:(hc + 1) * 512],
                                    start=(hc == 0), stop=(hc == 15))
                            s_pe.inc(last)

                    @block.vector
                    def _(v, tc=tc):
                        base = s_pe.n - 8
                        for os in range(8):
                            ps = ps_m[os % 2]
                            v.wait_ge(s_pe.h, base + os + 1)
                            sl = ACC[:, os * 1024 + tc * 512:os * 1024 + (tc + 1) * 512]
                            s_v.inc(v.tensor_add(sl, sl, ps[:, :]))

            for os in range(8):
                for tc in range(2):
                    cbase = hc0 + tc * 512

                    @block.sync
                    def _(sy, os=os, tc=tc, cbase=cbase):
                        s_v.wait(sy)
                        s_sp.wait(sy)
                        s_g.inc(sy.dma_start(out=XLD[:, :].rearrange("p (j n) -> p j n", n=256),
                                             in_=xrow(os * 128, cbase, 512)), 16)
                        s_g.inc(sy.dma_start(out=LNL2[:, :],
                                             in_=lnx_d[os * 128:(os + 1) * 128,
                                                       cbase:cbase + 512]), 16)

                    @block.vector
                    def _(v, os=os, tc=tc):
                        s_g.wait(v)
                        s_v.inc(v.tensor_add(OST[:, :], XLD[:, :], LNL2[:, :]))
                        s_v.wait(v)
                        s_v.inc(v.tensor_add(OST[:, :], OST[:, :],
                                             ACC[:, os * 1024 + tc * 512:os * 1024 + (tc + 1) * 512]))

                    @block.scalar
                    def _(sc, os=os, cbase=cbase):
                        s_v.wait(sc)
                        s_sp.inc(sc.dma_start(
                            out=yout_d[os * 128:(os + 1) * 128, cbase:cbase + 512],
                            in_=OST[:, :]), 16)

            ctx.barrier()





def _build_fused(dbg=False):
    nc = bass.Bass("TRN2", target_bir_lowering=False, debug=False)
    F = lambda nm, sh: nc.dram_tensor(nm, sh, F32, kind="ExternalInput").ap()
    xT_h = nc.dram_tensor("xT", [8192, 4096], F32, kind="ExternalInput")
    idx_d = nc.dram_tensor("idx8", [128, 8], U32, kind="ExternalInput").ap()
    a_wq = F("a_wq", [D, 3072])
    a_bq = F("a_bq", [128, 24])
    a_wo = F("a_wo", [D, D])
    a_bo = F("a_bo", [128, 8])
    f_w = [F("f_w%d" % i, [D, D]) for i in range(3)]
    f_u = [F("f_u%d" % i, [D, D]) for i in range(3)]
    f_wo3 = F("f_wo3", [D, D])
    f_uo = F("f_uo", [D, D])
    f_bc = F("f_bc", [128, 32])
    f_ic = F("f_ic", [128, 8])
    b_wq = F("b_wq", [D, 3072])
    b_wo = F("b_wo", [D, D])
    m_wr = F("m_wr", [D, 16])
    m_br = F("m_br", [1, 16])
    m_w1 = F("m_w1", [16, D, 2048])
    m_b1 = F("m_b1", [128, 256])
    m_w2 = F("m_w2", [16, 2048, D])
    m_b2 = F("m_b2", [16, D])
    yout = nc.dram_tensor("yT", [D, NT], F32, kind="ExternalOutput").ap()

    qkT0 = nc.dram_tensor("qkT0", [2048, NTOT], BF16).ap()
    vtok0 = nc.dram_tensor("vtok0", [NTOT, D], BF16).ap()
    qn_h = nc.dram_tensor("qnd", [16, NTOT], BF16)
    km_h = nc.dram_tensor("kmd", [16, 16], F32)
    x1_h = nc.dram_tensor("x1d", [8192, 4096], F32)
    lnx_h = nc.dram_tensor("lnxf", [8192, 4096], BF16)
    cc_d = nc.dram_tensor("ccd", [D, S], F32).ap()
    cb_d = nc.dram_tensor("cbd", [D, S], BF16).ap()
    xown_h = nc.dram_tensor("xown", [D, NT], F32)
    lnown_h = nc.dram_tensor("lnown", [D, NT], BF16)
    y2_h = nc.dram_tensor("y2d", [D, NT], F32)
    qkT1 = nc.dram_tensor("qkT1", [2048, NT], BF16).ap()
    vtok1 = nc.dram_tensor("vtok1", [NT, D], BF16).ap()
    lnxm = nc.dram_tensor("lnxm", [D, NT], BF16).ap()
    y3 = nc.dram_tensor("y3", [D, NT], F32)
    if dbg:
        dx1 = nc.dram_tensor("dbg_x1", [8192, 4096], F32, kind="ExternalOutput").ap()
        dcc = nc.dram_tensor("dbg_cc", [D, S], F32, kind="ExternalOutput").ap()
        dy2 = nc.dram_tensor("dbg_y2", [D, NT], F32, kind="ExternalOutput").ap()

    with contextlib.ExitStack() as es:
        block = es.enter_context(nc.Block())
        ctx = Ctx(nc, block, es)
        emit_l0_qkv(ctx, xT_h, a_wq, a_bq, qkT0, vtok0)
        emit_qk_norms(ctx, qkT0, qn_h.ap(), km_h.ap())
        emit_l0_attn2(ctx, xT_h, x1_h, a_wo, a_bq, a_bo, qkT0, vtok0, qn_h.ap(), km_h)
        emit_ln_all(ctx, x1_h, lnx_h)
        emit_scan(ctx, lnx_h, f_w, f_u, f_bc, f_ic, cc_d, cb_d)
        emit_own_tail(ctx, x1_h, lnx_h, idx_d, xown_h, lnown_h, cb_d, f_wo3, f_uo,
                      f_bc, y2_h)
        xp, xr, _ = _prov_flat(y2_h)
        emit_l1_attn(ctx, xp, xr, y3.ap(), b_wq, b_wo, qkT1, vtok1)
        ctx.barrier()
        xp3, xr3, _ = _prov_flat(y3)
        emit_l1_ffn(ctx, xp3, xr3, yout, m_wr, m_br, m_w1, m_b1, m_w2, m_b2, lnxm)
        if dbg:
            s_dbg = ctx.sem("dbg")

            @block.sync
            def _(sy):
                for s in ctx._sems.values():
                    s.wait(sy)
                s_dbg.inc(sy.dma_start(out=dx1, in_=x1_h.ap()), 16)
                s_dbg.inc(sy.dma_start(out=dcc, in_=cc_d), 16)
                s_dbg.inc(sy.dma_start(out=dy2, in_=y2_h.ap()), 16)
                s_dbg.wait(sy)
        ctx.barrier(engines=("sync",))
    return nc


def _from_fm(y_t):
    return np.ascontiguousarray(
        y_t.reshape(8, D, NB, S).transpose(0, 2, 3, 1).reshape(16, S, D))


def _prep_inputs(inputs):
    f32 = lambda k: np.asarray(inputs[k], np.float32)
    tcol = lambda a: np.ascontiguousarray(np.asarray(a, np.float32).T)
    x = f32("L0_attn_x")
    xfm = np.ascontiguousarray(x.transpose(2, 0, 1).reshape(D, NTOT))
    xsl = np.ascontiguousarray(xfm.reshape(8192, 4096))
    bq2 = np.ascontiguousarray(f32("L0_attn_bqkv").reshape(24, 128).T)
    bo2 = np.ascontiguousarray(f32("L0_attn_bout").reshape(8, 128).T)
    bcol = np.ascontiguousarray(
        np.stack([f32("L0_ffn_b" + c) for c in "fico"]).reshape(32, 128).T)
    incc = np.ascontiguousarray(f32("L0_ffn_in_c").reshape(8, 128).T)
    b1c = np.ascontiguousarray(f32("L1_ffn_b1").reshape(16, 16, 128)
                               .transpose(2, 0, 1).reshape(128, 256))
    base = {"xT": xsl, "a_wq": f32("L0_attn_Wqkv"), "a_bq": bq2,
            "a_wo": f32("L0_attn_Wout"), "a_bo": bo2,
            "f_w0": tcol(f32("L0_ffn_wf")), "f_w1": tcol(f32("L0_ffn_wi")),
            "f_w2": tcol(f32("L0_ffn_wc")),
            "f_u0": tcol(f32("L0_ffn_uf")), "f_u1": tcol(f32("L0_ffn_ui")),
            "f_u2": tcol(f32("L0_ffn_uc")),
            "f_wo3": tcol(f32("L0_ffn_wo")), "f_uo": tcol(f32("L0_ffn_uo")),
            "f_bc": bcol, "f_ic": incc,
            "b_wq": tcol(f32("L1_attn_w_qkv")), "b_wo": tcol(f32("L1_attn_w_out")),
            "m_wr": f32("L1_ffn_Wr"), "m_br": f32("L1_ffn_br").reshape(1, 16),
            "m_w1": f32("L1_ffn_W1"), "m_b1": b1c, "m_w2": f32("L1_ffn_W2"),
            "m_b2": f32("L1_ffn_b2")}
    in_maps = []
    for r in range(8):
        idx = ((np.arange(8)[None, :] * 128 + np.arange(128)[:, None]) * 8
               + r).astype(np.uint32)
        in_maps.append(dict(base, idx8=np.ascontiguousarray(idx)))
    return in_maps


def kernel(**inputs):
    nc = _build_fused(dbg=False)
    in_maps = _prep_inputs(inputs)
    res = run_bass_kernel_spmd(nc, in_maps, core_ids=list(range(8)))
    return _from_fm(np.stack([np.asarray(r["yT"]) for r in res.results]))


# version 15 (final)
# speedup vs baseline: 1.0116x; 1.0116x over previous
import contextlib
import numpy as np
import concourse.bass as bass
import concourse.mybir as mybir
from concourse.bass_utils import run_bass_kernel_spmd

F32 = mybir.dt.float32
BF16 = mybir.dt.bfloat16
I32 = mybir.dt.int32
U32 = mybir.dt.uint32
AF = mybir.ActivationFunctionType
ALU = mybir.AluOpType
AX = mybir.AxisListType

D = 1024
NT = 4096
NB = 2
BTOT = 16
NTOT = 32768
S = 2048
H = 16
HD = 64
FC = 8
TCH = 512
NTC = NT // TCH
EPS = 1e-5


class Sem:
    def __init__(self, h):
        self.h = h
        self.n = 0

    def inc(self, instr, k=1):
        instr.then_inc(self.h, k)
        self.n += k
        return instr

    def wait(self, eng):
        eng.wait_ge(self.h, self.n)


class Ctx:
    def __init__(self, nc, block, exit_stack):
        self.nc = nc
        self.block = block
        self.es = exit_stack
        self._sems = {}

    def sem(self, name):
        if name not in self._sems:
            self._sems[name] = Sem(self.es.enter_context(self.nc.semaphore(name)))
        return self._sems[name]

    def sbuf(self, name, shape, dtype, scope=None):
        return (scope or self.es).enter_context(self.nc.sbuf_tensor(name, shape, dtype))

    def barrier(self, engines=("tensor", "vector", "scalar", "gpsimd", "sync")):
        sems = list(self._sems.values())
        for en in engines:
            def mk(eng, sems=sems):
                for s in sems:
                    s.wait(eng)
            getattr(self.block, en)(mk)



def sl_ap(h, g0, w):
    sl, c = divmod(g0, 4096)
    assert c + w <= 4096
    return bass.AP(h, sl * 4096 + c, [[8 * 4096, 128], [128 * 8 * 4096, 8], [1, w]])


def sl_row_ap(h, f0, g0, w):
    sl, c = divmod(g0, 4096)
    assert c + w <= 4096
    return bass.AP(h, f0 * 8 * 4096 + sl * 4096 + c, [[8 * 4096, 128], [1, w]])



def _prov_flat(h):
    def xprov(c0, w):
        return bass.AP(h, c0, [[4096, 128], [128 * 4096, 8], [1, w]])

    def xrow(f0, c0, w):
        return bass.AP(h, f0 * 4096 + c0, [[4096, 128], [256, w // 256], [1, 256]])

    def yprov(c0, w):
        return bass.AP(h, c0, [[4096, 128], [128 * 4096, 8], [1, w]])
    return xprov, xrow, yprov



def emit_ln(ctx, XR, LNB, XC, consts, tc, sems, xstride=None, tw=TCH):
    nc, block = ctx.nc, ctx.block
    s_pe, s_v, s_a = sems
    onesf, ones2, ones_row, ps_sum, ps_mb, ps_rb, row1, rowr, rowv = consts
    NTs = NT if xstride is None else xstride
    c0 = tc * tw

    @block.tensor
    def _(t):
        s_v.wait(t)
        s_a.wait(t)
        last = None
        for kc in range(FC):
            last = t.matmul(ps_sum[:, :], onesf[:, 0:1], XR[:, kc * NTs + c0:kc * NTs + c0 + tw],
                            start=(kc == 0), stop=(kc == FC - 1))
        s_pe.inc(last)

    @block.vector
    def _(v):
        s_pe.wait(v)
        with nc.allow_low_precision(reason="bf16 matmul operand"):
            s_a.inc(v.tensor_copy(row1[:, :], ps_sum[0:1, :]))

    @block.tensor
    def _(t):
        s_a.wait(t)
        s_pe.inc(t.matmul(ps_mb[:, :], ones_row[:, :], row1[:, :], start=True, stop=True))

    @block.vector
    def _(v):
        s_pe.wait(v)
        last = None
        for kc in range(FC):
            last = v.tensor_add(XC[:, kc * tw:(kc + 1) * tw],
                                XR[:, kc * NTs + c0:kc * NTs + c0 + tw], ps_mb[:, :])
        s_v.inc(last)

    @block.vector
    def _(v):
        s_v.wait(v)
        last = None
        with nc.allow_low_precision(reason="bf16 matmul operand"):
            for kc in range(FC):
                last = v.tensor_mul(LNB[:, kc * tw:(kc + 1) * tw],
                                    XC[:, kc * tw:(kc + 1) * tw],
                                    XC[:, kc * tw:(kc + 1) * tw])
        s_a.inc(last)

    @block.tensor
    def _(t):
        s_a.wait(t)
        last = None
        for kc in range(FC):
            last = t.matmul(ps_sum[:, :], ones2[:, 1:2], LNB[:, kc * tw:(kc + 1) * tw],
                            start=(kc == 0), stop=(kc == FC - 1))
        s_pe.inc(last)

    @block.vector
    def _(v):
        s_pe.wait(v)
        s_v.inc(v.tensor_scalar(rowv[:, :], ps_sum[0:1, :], 1.0 / D, EPS, ALU.mult, ALU.add))

    @block.scalar
    def _(sc):
        s_v.wait(sc)
        s_a.inc(sc.activation(out=rowv[:, :], in_=rowv[:, :], func=AF.Sqrt))

    @block.vector
    def _(v):
        s_a.wait(v)
        with nc.allow_low_precision(reason="bf16 matmul operand"):
            s_v.inc(v.reciprocal(rowr[:, :], rowv[:, :]))

    @block.tensor
    def _(t):
        s_v.wait(t)
        s_pe.inc(t.matmul(ps_rb[:, :], ones_row[:, :], rowr[:, :], start=True, stop=True))

    @block.vector
    def _(v):
        s_pe.wait(v)
        last = None
        for kc in range(FC):
            last = v.tensor_mul(LNB[:, kc * tw:(kc + 1) * tw],
                                XC[:, kc * tw:(kc + 1) * tw], ps_rb[:, :])
        s_v.inc(last)


def _mk_ln_consts(ctx, sc1, tag, tw=TCH):
    nc = ctx.nc
    ones2 = ctx.sbuf(tag + "ones2", [128, 2], BF16)
    onesf = ctx.sbuf(tag + "onesf", [128, 1], F32)
    ones_row = ctx.sbuf(tag + "onesr", [1, 128], BF16)
    row1 = ctx.sbuf(tag + "row1", [1, tw], BF16, scope=sc1)
    rowr = ctx.sbuf(tag + "rowr", [1, tw], BF16, scope=sc1)
    rowv = ctx.sbuf(tag + "rowv", [1, tw], F32, scope=sc1)
    ps_sum = sc1.enter_context(nc.psum_tensor(tag + "ps_sum", [1, tw], F32))
    ps_mb = sc1.enter_context(nc.psum_tensor(tag + "ps_mb", [128, tw], F32))
    ps_rb = sc1.enter_context(nc.psum_tensor(tag + "ps_rb", [128, tw], F32))
    consts = (onesf, ones2, ones_row, ps_sum, ps_mb, ps_rb, row1, rowr, rowv)

    block = ctx.block
    s_g = ctx.sem(tag + "gg")
    s_cs = ctx.sem(tag + "cs")

    @block.gpsimd
    def _(g):
        s_cs.inc(g.memset(ones2[:, 0:1], -1.0 / D))
        s_cs.wait(g)
        g.memset(ones2[:, 1:2], 1.0)
        g.memset(onesf[:, 0:1], -1.0 / D)
        s_g.inc(g.memset(ones_row[:, :], 1.0))
    return consts, (ones2, onesf, ones_row)



def emit_l0_qkv(ctx, xT_h, wq_d, bq_d, qkT_d, vtok_d, tag="p1"):
    TW = 512
    nc, block = ctx.nc, ctx.block
    s_ld = ctx.sem(tag + "ld")
    s_sp = ctx.sem(tag + "sp")
    s_pe = ctx.sem(tag + "pe")
    s_v = ctx.sem(tag + "vv")
    s_a = ctx.sem(tag + "aa")
    s_g = ctx.sem(tag + "gg")

    with contextlib.ExitStack() as sc1:
        consts, _ = _mk_ln_consts(ctx, sc1, tag, tw=TW)
        WQ = ctx.sbuf(tag + "wq", [128, FC * 3072], BF16, scope=sc1)
        XRC = ctx.sbuf(tag + "xrc", [128, FC * TW], F32, scope=sc1)
        XC = ctx.sbuf(tag + "xcen", [128, FC * TW], F32, scope=sc1)
        LNB = ctx.sbuf(tag + "lnb", [128, FC * TW], BF16, scope=sc1)
        STG = ctx.sbuf(tag + "stg", [128, 8 * TW], BF16, scope=sc1)
        VSTG = ctx.sbuf(tag + "vstg", [128, 4 * 1024], BF16, scope=sc1)
        bq_sb = ctx.sbuf(tag + "bq", [128, 16], F32, scope=sc1)

        ps_q0 = sc1.enter_context(nc.psum_tensor(tag + "ps_q0", [128, TW], F32))
        ps_q1 = sc1.enter_context(nc.psum_tensor(tag + "ps_q1", [128, TW], F32))
        ps_v0 = sc1.enter_context(nc.psum_tensor(tag + "ps_v0", [128, 512], F32))
        ps_v1 = sc1.enter_context(nc.psum_tensor(tag + "ps_v1", [128, 512], F32))

        @block.gpsimd
        def _(g):
            s_ld.inc(g.dma_start(
                out=WQ[:, :].rearrange("p (kc m) -> p kc m", kc=FC),
                in_=wq_d.rearrange("(kc p) m -> p kc m", p=128)), 16)
            s_ld.inc(g.dma_start(out=bq_sb[:, :], in_=bq_d[:, 0:16]), 16)

        for tc in range(NTOT // TW):
            c0 = tc * TW

            @block.sync
            def _(sy, c0=c0):
                s_a.wait(sy)
                s_v.wait(sy)
                s_pe.wait(sy)
                s_sp.wait(sy)
                s_g.inc(sy.dma_start(
                    out=XRC[:, :].rearrange("p (kc n) -> p kc n", kc=FC),
                    in_=sl_ap(xT_h, c0, TW)), 16)

            @block.tensor
            def _(t):
                s_g.wait(t)
                s_ld.wait(t)

            @block.vector
            def _(v):
                s_g.wait(v)

            @block.scalar
            def _(sc):
                s_g.wait(sc)

            emit_ln(ctx, XRC, LNB, XC, consts, 0, (s_pe, s_v, s_a), xstride=TW, tw=TW)

            for grp in range(2):
                base_v = s_v.n
                base_pe = s_pe.n

                @block.tensor
                def _(t, grp=grp, base_v=base_v):
                    s_v.wait(t)
                    s_sp.wait(t)
                    for j in range(8):
                        ms = grp * 8 + j
                        ps = ps_q0 if j % 2 == 0 else ps_q1
                        if j >= 2:
                            t.wait_ge(s_v.h, base_v + j - 1)
                        last = None
                        for kc in range(FC):
                            last = t.matmul(ps[:, :],
                                            WQ[:, kc * 3072 + ms * 128:kc * 3072 + (ms + 1) * 128],
                                            LNB[:, kc * TW:(kc + 1) * TW],
                                            start=(kc == 0), stop=(kc == FC - 1))
                        s_pe.inc(last)

                @block.vector
                def _(v, grp=grp, base_pe=base_pe):
                    for j in range(8):
                        ms = grp * 8 + j
                        ps = ps_q0 if j % 2 == 0 else ps_q1
                        v.wait_ge(s_pe.h, base_pe + j + 1)
                        s_v.inc(v.tensor_scalar(STG[:, j * TW:(j + 1) * TW], ps[:, :],
                                                bq_sb[:, ms:ms + 1], None, ALU.add))

                @block.sync
                def _(sy, c0=c0, grp=grp):
                    s_v.wait(sy)
                    s_sp.inc(sy.dma_start(
                        out=qkT_d.rearrange("(ms p) n -> p ms n", p=128)[:, grp * 8:(grp + 1) * 8,
                                                                        c0:c0 + TW],
                        in_=STG[:, :].rearrange("p (ms n) -> p ms n", ms=8)), 16)

            base_pe = s_pe.n
            base_v = s_v.n

            @block.tensor
            def _(t, base_v=base_v):
                s_sp.wait(t)
                for i in range(8):
                    ts, half = divmod(i, 2)
                    ps = ps_v0 if i % 2 == 0 else ps_v1
                    if i >= 2:
                        t.wait_ge(s_v.h, base_v + i - 1)
                    last = None
                    for kc in range(FC):
                        last = t.matmul(
                            ps[:, :],
                            LNB[:, kc * TW + ts * 128:kc * TW + (ts + 1) * 128],
                            WQ[:, kc * 3072 + 2048 + half * 512:kc * 3072 + 2048 + (half + 1) * 512],
                            start=(kc == 0), stop=(kc == FC - 1))
                    s_pe.inc(last)

            @block.vector
            def _(v, base_pe=base_pe):
                for i in range(8):
                    ts, half = divmod(i, 2)
                    ps = ps_v0 if i % 2 == 0 else ps_v1
                    v.wait_ge(s_pe.h, base_pe + i + 1)
                    s_v.inc(v.tensor_copy(
                        VSTG[:, ts * 1024 + half * 512:ts * 1024 + (half + 1) * 512], ps[:, :]))

            @block.scalar
            def _(sc, c0=c0):
                s_v.wait(sc)
                s_sp.inc(sc.dma_start(
                    out=vtok_d[c0:c0 + TW, :].rearrange("(ts p) n -> p ts n", p=128),
                    in_=VSTG[:, :].rearrange("p (ts n) -> p ts n", ts=4)), 16)

        ctx.barrier()



def emit_qk_norms(ctx, qkT_d, qn_d, km_d, tag="p15"):
    nc, block = ctx.nc, ctx.block
    s_ld = ctx.sem(tag + "ld")
    s_sp = ctx.sem(tag + "sp")
    s_pe = ctx.sem(tag + "pe")
    s_v = ctx.sem(tag + "vv")
    s_g = ctx.sem(tag + "gg")

    with contextlib.ExitStack() as scn:
        KC = ctx.sbuf(tag + "kc", [128, NTOT], BF16, scope=scn)
        SQ = ctx.sbuf(tag + "sq", [128, NTOT], BF16, scope=scn)
        QNS = ctx.sbuf(tag + "qns", [2, NTOT], BF16, scope=scn)
        KMC = ctx.sbuf(tag + "kmc", [2, 16], F32, scope=scn)
        KMT = ctx.sbuf(tag + "kmt", [2, 1], F32, scope=scn)
        hmask = ctx.sbuf(tag + "hm", [128, 2], BF16, scope=scn)
        ps_n = [scn.enter_context(nc.psum_tensor(tag + "psn%d" % i, [2, 512], F32))
                for i in range(2)]

        @block.gpsimd
        def _(g):
            s_ld.inc(g.memset(hmask[:, :], 0.0))
            s_ld.wait(g)
            s_ld.inc(g.memset(hmask[0:64, 0:1], 1.0))
            s_ld.wait(g)
            s_ld.inc(g.memset(hmask[64:128, 1:2], 1.0))

        for which in ("k", "q"):
            rows0 = 1024 if which == "k" else 0
            for kc in range(FC):
                @block.sync
                def _(sy, kc=kc, rows0=rows0):
                    s_pe.wait(sy)
                    s_v.wait(sy)
                    s_sp.wait(sy)
                    for hh in range(2):
                        s_g.inc(sy.dma_start(
                            out=KC[:, hh * (NTOT // 2):(hh + 1) * (NTOT // 2)],
                            in_=qkT_d[rows0 + kc * 128:rows0 + (kc + 1) * 128,
                                      hh * (NTOT // 2):(hh + 1) * (NTOT // 2)]), 16)

                @block.vector
                def _(v):
                    s_g.wait(v)
                    with nc.allow_low_precision(reason="bf16 matmul operand"):
                        for hh in range(4):
                            s_v.inc(v.tensor_mul(
                                SQ[:, hh * (NTOT // 4):(hh + 1) * (NTOT // 4)],
                                KC[:, hh * (NTOT // 4):(hh + 1) * (NTOT // 4)],
                                KC[:, hh * (NTOT // 4):(hh + 1) * (NTOT // 4)]))

                if which == "k":
                    for b in range(BTOT):
                        base_pe = s_pe.n
                        base_v = s_v.n

                        @block.tensor
                        def _(t, b=b, base_v=base_v):
                            if b == 0:
                                s_v.wait(t)
                                s_ld.wait(t)
                            for j in range(4):
                                if b > 0 or j >= 2:
                                    t.wait_ge(s_v.h, base_v + 2 * j - 3)
                                s_pe.inc(t.matmul(
                                    ps_n[j % 2][:, :], hmask[:, :],
                                    SQ[:, b * S + j * 512:b * S + (j + 1) * 512],
                                    start=True, stop=True))

                        @block.vector
                        def _(v, b=b, base_pe=base_pe):
                            for j in range(4):
                                v.wait_ge(s_pe.h, base_pe + j + 1)
                                s_v.inc(v.reduce_max(KMT[:, 0:1], ps_n[j % 2][:, :],
                                                     axis=AX.X))
                                s_v.wait(v)
                                if j == 0:
                                    s_v.inc(v.tensor_copy(KMC[:, b:b + 1], KMT[:, 0:1]))
                                else:
                                    s_v.inc(v.tensor_tensor(KMC[:, b:b + 1], KMC[:, b:b + 1],
                                                            KMT[:, 0:1], ALU.max))
                                s_v.wait(v)

                    @block.sync
                    def _(sy, kc=kc):
                        s_v.wait(sy)
                        s_sp.inc(sy.dma_start(out=km_d[2 * kc:2 * kc + 2, :],
                                              in_=KMC[:, :]), 16)
                else:
                    for j in range(NTOT // 512):
                        base_pe = s_pe.n
                        base_v = s_v.n

                        @block.tensor
                        def _(t, j=j, base_v=base_v):
                            s_v.wait(t) if j == 0 else None
                            if j >= 2:
                                t.wait_ge(s_v.h, base_v)
                            s_pe.inc(t.matmul(
                                ps_n[j % 2][:, :], hmask[:, :],
                                SQ[:, j * 512:(j + 1) * 512], start=True, stop=True))

                        @block.vector
                        def _(v, j=j, base_pe=base_pe):
                            v.wait_ge(s_pe.h, base_pe + 1)
                            with nc.allow_low_precision(reason="bf16 matmul operand"):
                                s_v.inc(v.tensor_copy(QNS[:, j * 512:(j + 1) * 512],
                                                      ps_n[j % 2][:, :]))

                    @block.sync
                    def _(sy, kc=kc):
                        s_v.wait(sy)
                        for hh in range(2):
                            s_sp.inc(sy.dma_start(
                                out=qn_d[2 * kc:2 * kc + 2,
                                         hh * (NTOT // 2):(hh + 1) * (NTOT // 2)],
                                in_=QNS[:, hh * (NTOT // 2):(hh + 1) * (NTOT // 2)]), 16)

        ctx.barrier()


def emit_l0_attn2(ctx, xT_h, x1_h, wo_d, bq_d, bo_d, qkT_d, vtok_d, qn_d, km_d,
                  tag="p2"):
    nc, block = ctx.nc, ctx.block
    s_ld = ctx.sem(tag + "ld")
    s_sp = ctx.sem(tag + "sp")
    s_at = ctx.sem(tag + "at")
    s_pe = ctx.sem(tag + "pe")
    s_v = ctx.sem(tag + "vv")
    s_a = ctx.sem(tag + "aa")
    s_g = ctx.sem(tag + "gg")
    s_cs = ctx.sem(tag + "cs")
    QG = 1024

    ones2 = ctx.sbuf(tag + "ones2", [128, 2], BF16)
    ones_row = ctx.sbuf(tag + "onesr", [1, 128], BF16)

    with contextlib.ExitStack() as sc2:
        WO = ctx.sbuf(tag + "wo", [128, FC * 1024], BF16, scope=sc2)
        bv_sb = ctx.sbuf(tag + "bv", [128, 8], BF16, scope=sc2)
        bo_sb = ctx.sbuf(tag + "bo", [128, 8], F32, scope=sc2)
        qa = [ctx.sbuf(tag + "qa%d" % i, [65, QG], BF16, scope=sc2) for i in range(2)]
        ka = [ctx.sbuf(tag + "ka%d" % i, [65, S], BF16, scope=sc2) for i in range(2)]
        va = [ctx.sbuf(tag + "va%d" % i, [128, 16 * 65], BF16, scope=sc2) for i in range(2)]
        ex = [ctx.sbuf(tag + "ex%d" % i, [128, QG], BF16, scope=sc2) for i in range(2)]
        rb_sb = ctx.sbuf(tag + "rb", [64, QG], BF16, scope=sc2)
        rec_row = ctx.sbuf(tag + "rec", [1, QG], BF16, scope=sc2)
        outn = ctx.sbuf(tag + "outn", [128, FC * QG], BF16, scope=sc2)
        KMS = ctx.sbuf(tag + "kms", [65, 16], F32, scope=sc2)
        XLD = ctx.sbuf(tag + "xld", [128, QG], F32, scope=sc2)
        OST = ctx.sbuf(tag + "ost", [128, QG], F32, scope=sc2)

        ps_s = [sc2.enter_context(nc.psum_tensor(tag + "ps_s%d" % i, [128, QG], F32))
                for i in range(2)]
        ps_av = sc2.enter_context(nc.psum_tensor(tag + "ps_av", [65, QG], F32))
        ps_rbb = sc2.enter_context(nc.psum_tensor(tag + "ps_rbb", [64, QG], F32))
        ps_bo = ps_s[0]

        @block.gpsimd
        def _(g):
            s_ld.inc(g.dma_start(
                out=WO[:, :].rearrange("p (kc m) -> p kc m", kc=FC),
                in_=wo_d.rearrange("(kc p) m -> p kc m", p=128)), 16)
            s_ld.inc(g.dma_start(out=bv_sb[:, :], in_=bq_d[:, 16:24]), 16)
            s_ld.inc(g.dma_start(out=bo_sb[:, :], in_=bo_d[:, :]), 16)
            s_cs.inc(g.memset(ones2[:, 0:1], -1.0 / D))
            s_cs.wait(g)
            g.memset(ones2[:, 1:2], 1.0)
            g.memset(ones_row[:, :], 1.0)
            for i in range(2):
                g.memset(va[i][:, :], 1.0)
                s_g.inc(g.memset(ka[i][64:65, :], 1.0))

        for ms in range(8):
            @block.tensor
            def _(t, ms=ms):
                if ms == 0:
                    s_ld.wait(t)
                    s_g.wait(t)
                else:
                    s_v.wait(t)
                last = None
                for kc in range(FC):
                    last = t.matmul(ps_bo[:, 0:1],
                                    WO[:, kc * 1024 + ms * 128:kc * 1024 + (ms + 1) * 128],
                                    bv_sb[:, kc:kc + 1],
                                    start=(kc == 0), stop=(kc == FC - 1))
                s_pe.inc(last)

            @block.vector
            def _(v, ms=ms):
                s_pe.wait(v)
                s_v.inc(v.tensor_add(bo_sb[:, ms:ms + 1], bo_sb[:, ms:ms + 1], ps_bo[:, 0:1]))

        last_end = {0: 0, 1: 0}
        for b in range(BTOT):
            for qg in range(S // QG):
                for h in range(H):
                    ib = h % 2

                    @block.sync
                    def _(sy, h=h, ib=ib, b=b, qg=qg, free_at=last_end[ib]):
                        sy.wait_ge(s_pe.h, free_at)
                        s_sp.wait(sy)
                        s_g.wait(sy)
                        if h == 0 and qg == 0:
                            s_v.wait(sy)
                            s_at.inc(sy.dma_start(
                                out=KMS[64:65, :],
                                in_=bass.AP(km_d, b, [[256, 1], [16, 16]]),
                                allow_slow_non_contiguous=True), 16)
                        s_at.inc(sy.dma_start(
                            out=qa[ib][64:65, :],
                            in_=qn_d[h:h + 1, b * S + qg * QG:b * S + (qg + 1) * QG]), 16)
                        s_at.inc(sy.dma_start(
                            out=qa[ib][0:64, :],
                            in_=qkT_d[h * 64:(h + 1) * 64,
                                      b * S + qg * QG:b * S + (qg + 1) * QG]), 16)
                        s_at.inc(sy.dma_start(
                            out=ka[ib][0:64, :],
                            in_=qkT_d[1024 + h * 64:1024 + (h + 1) * 64, b * S:(b + 1) * S]), 16)
                        s_at.inc(sy.dma_start(
                            out=va[ib][:, :].rearrange("p (kc d) -> p kc d", kc=16)[:, :, 0:64],
                            in_=vtok_d[b * S:(b + 1) * S, h * 64:(h + 1) * 64]
                            .rearrange("(kc p) d -> p kc d", p=128)), 16)

                    @block.vector
                    def _(v, h=h, ib=ib):
                        s_at.wait(v)
                        with nc.allow_low_precision(reason="bf16 matmul operand"):
                            s_v.inc(v.tensor_scalar(qa[ib][64:65, :], qa[ib][64:65, :],
                                                    KMS[64:65, h:h + 1], -0.5,
                                                    ALU.add, ALU.mult))

                    base_a = s_a.n
                    base_pe = s_pe.n

                    def score_sec(kc, ib=ib, base_a=base_a):
                        @block.tensor
                        def _(t, kc=kc, ib=ib):
                            if kc == 0:
                                s_v.wait(t)
                                s_at.wait(t)
                                s_a.wait(t)
                            elif kc >= 2:
                                t.wait_ge(s_a.h, base_a + kc - 1)
                            last = None
                            for half in range(2):
                                last = t.matmul(
                                    ps_s[kc % 2][:, half * 512:(half + 1) * 512],
                                    ka[ib][:, kc * 128:(kc + 1) * 128],
                                    qa[ib][:, half * 512:(half + 1) * 512],
                                    start=True, stop=True)
                            s_pe.inc(last)

                    def exp_all(base_pe=base_pe):
                        @block.scalar
                        def _(sc):
                            for kc in range(16):
                                w = 1 if kc == 0 else (2 if kc == 1 else 2 * kc)
                                sc.wait_ge(s_pe.h, base_pe + w)
                                s_a.inc(sc.activation(out=ex[kc % 2][:, :],
                                                      in_=ps_s[kc % 2][:, :],
                                                      func=AF.Exp, scale=0.125))

                    def av_sec(kc, ib=ib, base_a=base_a):
                        @block.tensor
                        def _(t, kc=kc, ib=ib):
                            t.wait_ge(s_a.h, base_a + kc + 1)
                            last = None
                            for half in range(2):
                                last = t.matmul(
                                    ps_av[:, half * 512:(half + 1) * 512],
                                    va[ib][:, kc * 65:(kc + 1) * 65],
                                    ex[kc % 2][:, half * 512:(half + 1) * 512],
                                    start=(kc == 0), stop=(kc == 15))
                            s_pe.inc(last)

                    score_sec(0)
                    exp_all()
                    score_sec(1)
                    for kc in range(2, 16):
                        av_sec(kc - 2)
                        score_sec(kc)
                    av_sec(14)
                    av_sec(15)
                    last_end[ib] = s_pe.n

                    @block.vector
                    def _(v):
                        s_pe.wait(v)
                        with nc.allow_low_precision(reason="bf16 matmul operand"):
                            s_v.inc(v.reciprocal(rec_row[:, :], ps_av[64:65, :]))

                    @block.tensor
                    def _(t):
                        s_v.wait(t)
                        t.matmul(ps_rbb[:, 0:512], ones_row[:, 0:64], rec_row[:, 0:512],
                                 start=True, stop=True)
                        s_pe.inc(t.matmul(ps_rbb[:, 512:1024], ones_row[:, 0:64],
                                          rec_row[:, 512:1024], start=True, stop=True))

                    @block.vector
                    def _(v, h=h):
                        s_pe.wait(v)
                        with nc.allow_low_precision(reason="bf16 matmul operand"):
                            s_a.inc(v.tensor_copy(rb_sb[:, :], ps_rbb[:, :]))
                        s_a.wait(v)
                        ch = h // 2
                        rowbase = (h % 2) * 64
                        s_v.inc(v.tensor_mul(outn[rowbase:rowbase + 64, ch * QG:(ch + 1) * QG],
                                             ps_av[0:64, :], rb_sb[:, :]))

                for ms in range(8):
                    g0 = b * S + qg * QG

                    @block.sync
                    def _(sy, ms=ms, g0=g0):
                        s_v.wait(sy)
                        s_sp.wait(sy)
                        s_g.inc(sy.dma_start(out=XLD[:, :],
                                             in_=sl_row_ap(xT_h, ms * 128, g0, QG)), 16)

                    @block.tensor
                    def _(t, ms=ms):
                        s_v.wait(t)
                        s_a.wait(t)
                        for half in range(2):
                            ps = ps_s[half]
                            last = None
                            for kc in range(FC):
                                last = t.matmul(
                                    ps[:, 0:512],
                                    WO[:, kc * 1024 + ms * 128:kc * 1024 + (ms + 1) * 128],
                                    outn[:, kc * QG + half * 512:kc * QG + (half + 1) * 512],
                                    start=(kc == 0), stop=(kc == FC - 1))
                            s_pe.inc(last)

                    @block.vector
                    def _(v, ms=ms):
                        base = s_pe.n - 2
                        s_g.wait(v)
                        for half in range(2):
                            ps = ps_s[half]
                            v.wait_ge(s_pe.h, base + half + 1)
                            s_v.inc(v.scalar_tensor_tensor(
                                OST[:, half * 512:(half + 1) * 512], ps[:, 0:512],
                                bo_sb[:, ms:ms + 1], XLD[:, half * 512:(half + 1) * 512],
                                ALU.add, ALU.add))

                    @block.scalar
                    def _(sc, ms=ms, g0=g0):
                        s_v.wait(sc)
                        s_sp.inc(sc.dma_start(out=sl_row_ap(x1_h, ms * 128, g0, QG),
                                              in_=OST[:, :]), 16)

                ctx.barrier()



def emit_ln_all(ctx, x1_h, lnx_h, tag="p3"):
    TW = 512
    nc, block = ctx.nc, ctx.block
    s_sp = ctx.sem(tag + "sp")
    s_pe = ctx.sem(tag + "pe")
    s_v = ctx.sem(tag + "vv")
    s_a = ctx.sem(tag + "aa")
    s_g = ctx.sem(tag + "gg")

    with contextlib.ExitStack() as sca:
        consts, _ = _mk_ln_consts(ctx, sca, tag, tw=TW)
        XRC = ctx.sbuf(tag + "xrc", [128, FC * TW], F32, scope=sca)
        XC = ctx.sbuf(tag + "xc", [128, FC * TW], F32, scope=sca)
        LNB = ctx.sbuf(tag + "lnb", [128, FC * TW], BF16, scope=sca)

        for tc in range(NTOT // TW):
            c0 = tc * TW

            @block.sync
            def _(sy, c0=c0):
                s_a.wait(sy)
                s_v.wait(sy)
                s_pe.wait(sy)
                s_sp.wait(sy)
                s_g.inc(sy.dma_start(
                    out=XRC[:, :].rearrange("p (kc n) -> p kc n", kc=FC),
                    in_=sl_ap(x1_h, c0, TW)), 16)

            @block.tensor
            def _(t):
                s_g.wait(t)

            @block.vector
            def _(v):
                s_g.wait(v)

            @block.scalar
            def _(sc):
                s_g.wait(sc)

            emit_ln(ctx, XRC, LNB, XC, consts, 0, (s_pe, s_v, s_a), xstride=TW, tw=TW)

            @block.sync
            def _(sy, c0=c0):
                s_v.wait(sy)
                s_sp.inc(sy.dma_start(
                    out=sl_ap(lnx_h, c0, TW),
                    in_=LNB[:, :].rearrange("p (kc n) -> p kc n", kc=FC)), 16)

        ctx.barrier()



def emit_scan(ctx, lnx_h, wT, uT, bcol_d, inc_d, cc_d, cb_d, tag="p5"):
    nc, block = ctx.nc, ctx.block
    s_w = ctx.sem(tag + "wl")
    s_pe = ctx.sem(tag + "pe")
    s_v = ctx.sem(tag + "vv")
    s_a = ctx.sem(tag + "aa")
    s_g = ctx.sem(tag + "gg")
    s_st = ctx.sem(tag + "st")

    bcol = ctx.sbuf(tag + "bcol", [128, 32], F32)
    incc = ctx.sbuf(tag + "incc", [128, 8], F32)

    def cc_ap(sc):
        return cc_d.rearrange("(kc p) n -> p kc n", p=128)[:, :, sc * 512:(sc + 1) * 512]

    def cb_ap(sc):
        return cb_d.rearrange("(kc p) n -> p kc n", p=128)[:, :, sc * 512:(sc + 1) * 512]

    with contextlib.ExitStack() as scs:
        WUa = [ctx.sbuf(tag + "wu%d" % i, [128, FC * 1024], BF16, scope=scs)
               for i in range(6)]
        CCc = [ctx.sbuf(tag + "ccc%d" % i, [128, FC * 512], F32, scope=scs)
               for i in range(2)]
        CBin = [ctx.sbuf(tag + "cbi%d" % i, [128, FC * 512], BF16, scope=scs)
                for i in range(2)]
        CBout = [ctx.sbuf(tag + "cbo%d" % i, [128, FC * 512], BF16, scope=scs)
                 for i in range(2)]
        LNXc = [ctx.sbuf(tag + "lnxc%d" % i, [128, FC * 512], BF16, scope=scs)
                for i in range(2)]
        G1 = ctx.sbuf(tag + "g1", [128, 512], F32, scope=scs)
        G2 = ctx.sbuf(tag + "g2", [128, 512], F32, scope=scs)
        G3 = ctx.sbuf(tag + "g3", [128, 512], F32, scope=scs)
        bch = ctx.sbuf(tag + "bch", [128, 32], F32, scope=scs)
        ps_p = [scs.enter_context(nc.psum_tensor(tag + "ps%d" % i, [128, 512], F32))
                for i in range(3)]

        @block.gpsimd
        def _(g):
            s_w.inc(g.dma_start(out=bcol[:, :], in_=bcol_d[:, :]), 16)
            s_w.inc(g.dma_start(out=incc[:, :], in_=inc_d[:, :]), 16)
            for gi in range(3):
                s_w.inc(g.dma_start(
                    out=WUa[2 * gi][:, :].rearrange("p (kc m) -> p kc m", kc=FC),
                    in_=wT[gi].rearrange("(kc p) m -> p kc m", p=128)), 16)
                s_w.inc(g.dma_start(
                    out=WUa[2 * gi + 1][:, :].rearrange("p (kc m) -> p kc m", kc=FC),
                    in_=uT[gi].rearrange("(kc p) m -> p kc m", p=128)), 16)

        @block.vector
        def _(v):
            s_w.wait(v)
            s_v.inc(v.tensor_scalar(bch[:, :], bcol[:, :], 0.5, None, ALU.mult))
            with nc.allow_low_precision(reason="bf16 matmul operand"):
                for wi in range(4):
                    s_w.inc(v.tensor_scalar(WUa[wi][:, :], WUa[wi][:, :], 0.5,
                                            None, ALU.mult), 1)

        for sc_i in range(4):
            @block.vector
            def _(v):
                s_w.wait(v)
                s_st.wait(v)
                last = None
                for kc in range(FC):
                    last = v.tensor_copy(CCc[0][:, kc * 512:(kc + 1) * 512],
                                         bass.AP(incc, kc, [[8, 128], [0, 512]]))
                s_v.inc(last)
                s_v.wait(v)
                for kc in range(FC):
                    last = v.tensor_copy(CBout[0][:, kc * 512:(kc + 1) * 512],
                                         CCc[0][:, kc * 512:(kc + 1) * 512])
                s_v.inc(last)

            @block.scalar
            def _(sc, sc_i=sc_i):
                s_v.wait(sc)
                s_st.inc(sc.dma_start(
                    out=cc_ap(sc_i),
                    in_=CCc[0][:, :].rearrange("p (kc n) -> p kc n", kc=FC)), 16)
                s_st.inc(sc.dma_start(
                    out=cb_ap(sc_i),
                    in_=CBout[0][:, :].rearrange("p (kc n) -> p kc n", kc=FC)), 16)

        pe_end = []
        v_end = []
        st_end = []
        st_init = s_st.n
        v_init = s_v.n
        for t_ in range(BTOT):
            for sc_i in range(4):
                it = t_ * 4 + sc_i
                sl_ = it % 2
                g0 = t_ * S + sc_i * 512
                CCs, CBi, CBo, LNs = CCc[sl_], CBin[sl_], CBout[sl_], LNXc[sl_]

                @block.sync
                def _(sy, it=it, sc_i=sc_i, g0=g0, CCs=CCs, CBi=CBi, LNs=LNs):
                    if it >= 2:
                        sy.wait_ge(s_pe.h, pe_end[it - 2])
                        sy.wait_ge(s_v.h, v_end[it - 2])
                        sy.wait_ge(s_st.h, st_end[it - 2])
                    else:
                        sy.wait_ge(s_st.h, st_init)
                        sy.wait_ge(s_v.h, v_init)
                    s_g.inc(sy.dma_start(
                        out=CCs[:, :].rearrange("p (kc n) -> p kc n", kc=FC),
                        in_=cc_ap(sc_i)), 16)
                    s_g.inc(sy.dma_start(
                        out=CBi[:, :].rearrange("p (kc n) -> p kc n", kc=FC),
                        in_=cb_ap(sc_i)), 16)
                    s_g.inc(sy.dma_start(
                        out=LNs[:, :].rearrange("p (kc n) -> p kc n", kc=FC),
                        in_=sl_ap(lnx_h, g0, 512)), 16)

                ch_pe = s_pe.n
                ch_a = s_a.n
                ch_v = s_v.n
                g_need = s_g.n
                for ms in range(8):
                    @block.tensor
                    def _(t, ms=ms, ch_a=ch_a, g_need=g_need, CBi=CBi, LNs=LNs):
                        t.wait_ge(s_g.h, g_need)
                        if ms == 0:
                            s_w.wait(t)
                        t.wait_ge(s_a.h, ch_a + 3 * ms)
                        for oi, gi in enumerate((2, 1, 0)):
                            ps = ps_p[oi]
                            last = None
                            for kc in range(FC):
                                last = t.matmul(
                                    ps[:, :],
                                    WUa[2 * gi][:, kc * 1024 + ms * 128:kc * 1024 + (ms + 1) * 128],
                                    LNs[:, kc * 512:(kc + 1) * 512],
                                    start=(kc == 0), stop=False)
                            for kc in range(FC):
                                last = t.matmul(
                                    ps[:, :],
                                    WUa[2 * gi + 1][:, kc * 1024 + ms * 128:kc * 1024 + (ms + 1) * 128],
                                    CBi[:, kc * 512:(kc + 1) * 512],
                                    start=False, stop=(kc == FC - 1))
                            s_pe.inc(last)

                    @block.vector
                    def _(v, ms=ms, ch_a=ch_a, CCs=CCs, CBo=CBo):
                        v.wait_ge(s_a.h, ch_a + 3 * ms + 2)
                        s_v.inc(v.tensor_scalar(G2[:, :], G2[:, :], 0.5, 0.5,
                                                ALU.mult, ALU.add))
                        s_v.wait(v)
                        s_v.inc(v.tensor_mul(G2[:, :], G2[:, :], G1[:, :]))
                        v.wait_ge(s_a.h, ch_a + 3 * ms + 3)
                        s_v.inc(v.tensor_scalar(G3[:, :], G3[:, :], 0.5, 0.5,
                                                ALU.mult, ALU.add))
                        s_v.wait(v)
                        sl2 = CCs[:, ms * 512:(ms + 1) * 512]
                        s_v.inc(v.tensor_mul(sl2, sl2, G3[:, :]))
                        s_v.wait(v)
                        s_v.inc(v.tensor_add(sl2, sl2, G2[:, :]))
                        s_v.wait(v)
                        s_v.inc(v.tensor_copy(CBo[:, ms * 512:(ms + 1) * 512], sl2))

                @block.scalar
                def _(sc, ch_pe=ch_pe, ch_v=ch_v):
                    for ms in range(8):
                        sc.wait_ge(s_v.h, ch_v + 6 * ms)
                        sc.wait_ge(s_pe.h, ch_pe + 3 * ms + 1)
                        s_a.inc(sc.activation(out=G1[:, :], in_=ps_p[0][:, :],
                                              func=AF.Tanh,
                                              bias=bcol[:, 16 + ms:17 + ms]))
                        sc.wait_ge(s_pe.h, ch_pe + 3 * ms + 2)
                        s_a.inc(sc.activation(out=G2[:, :], in_=ps_p[1][:, :],
                                              func=AF.Tanh,
                                              bias=bch[:, 8 + ms:9 + ms]))
                        sc.wait_ge(s_pe.h, ch_pe + 3 * ms + 3)
                        s_a.inc(sc.activation(out=G3[:, :], in_=ps_p[2][:, :],
                                              func=AF.Tanh,
                                              bias=bch[:, ms:ms + 1]))

                pe_end.append(s_pe.n)
                v_end.append(s_v.n)

                @block.scalar
                def _(sc, sc_i=sc_i, CCs=CCs, CBo=CBo):
                    s_v.wait(sc)
                    s_st.inc(sc.dma_start(
                        out=cc_ap(sc_i),
                        in_=CCs[:, :].rearrange("p (kc n) -> p kc n", kc=FC)), 16)
                    s_st.inc(sc.dma_start(
                        out=cb_ap(sc_i),
                        in_=CBo[:, :].rearrange("p (kc n) -> p kc n", kc=FC)), 16)
                st_end.append(s_st.n)

        ctx.barrier()



def emit_own_tail(ctx, x1_h, lnx_h, idx_d, xown_h, lnown_h, cb_d, wo3_d, uo_d,
                  bcol_d, y2_h, tag="p7"):
    nc, block = ctx.nc, ctx.block
    s_w = ctx.sem(tag + "wl")
    s_pe = ctx.sem(tag + "pe")
    s_v = ctx.sem(tag + "vv")
    s_a = ctx.sem(tag + "aa")
    s_g = ctx.sem(tag + "gg")
    s_st = ctx.sem(tag + "st")

    bcol = ctx.sbuf(tag + "bcol", [128, 32], F32)

    with contextlib.ExitStack() as scu:
        UOT = ctx.sbuf(tag + "uot", [128, FC * 2048], F32, scope=scu)
        with contextlib.ExitStack() as scv:
            CBF = ctx.sbuf(tag + "cbf", [128, FC * 2048], BF16, scope=scv)
            UOW = ctx.sbuf(tag + "uow", [128, FC * 1024], BF16, scope=scv)
            ps_p = [scv.enter_context(nc.psum_tensor(tag + "psu%d" % i, [128, 512], F32))
                    for i in range(2)]

            @block.gpsimd
            def _(g):
                s_w.inc(g.dma_start(out=bcol[:, :], in_=bcol_d[:, :]), 16)
                s_w.inc(g.dma_start(
                    out=UOW[:, :].rearrange("p (kc m) -> p kc m", kc=FC),
                    in_=uo_d.rearrange("(kc p) m -> p kc m", p=128)), 16)
                s_w.inc(g.dma_start(
                    out=CBF[:, :].rearrange("p (kc n) -> p kc n", kc=FC),
                    in_=cb_d.rearrange("(kc p) n -> p kc n", p=128)), 16)

            for ms in range(8):
                for sc_i in range(4):
                    it = ms * 4 + sc_i
                    base_v = s_v.n

                    @block.tensor
                    def _(t, ms=ms, sc_i=sc_i, it=it, base_v=base_v):
                        if it == 0:
                            s_w.wait(t)
                        ps = ps_p[it % 2]
                        if it >= 2:
                            t.wait_ge(s_v.h, base_v - 1)
                        last = None
                        for kc in range(FC):
                            last = t.matmul(
                                ps[:, :],
                                UOW[:, kc * 1024 + ms * 128:kc * 1024 + (ms + 1) * 128],
                                CBF[:, kc * 2048 + sc_i * 512:kc * 2048 + (sc_i + 1) * 512],
                                start=(kc == 0), stop=(kc == FC - 1))
                        s_pe.inc(last)

                    @block.vector
                    def _(v, ms=ms, sc_i=sc_i, it=it):
                        v.wait_ge(s_pe.h, it + 1)
                        ps = ps_p[it % 2]
                        s_v.inc(v.tensor_copy(
                            UOT[:, ms * 2048 + sc_i * 512:ms * 2048 + (sc_i + 1) * 512],
                            ps[:, :]))

            ctx.barrier()

        with contextlib.ExitStack() as scg:
            idx_sb = ctx.sbuf(tag + "idx", [128, 8], U32, scope=scg)
            STF = ctx.sbuf(tag + "stf", [128, 4096], F32, scope=scg)
            STB = ctx.sbuf(tag + "stb", [128, 4096], BF16, scope=scg)

            @block.gpsimd
            def _(g):
                s_g.inc(g.dma_start(out=idx_sb[:, :], in_=idx_d), 16)

            for kc in range(FC):
                @block.gpsimd
                def _(g, kc=kc):
                    s_g.wait(g)
                    s_st.wait(g)
                    s_at = g.indirect_dma_start(
                        out=STF[:, :], out_offset=None,
                        in_=x1_h.ap(),
                        in_offset=bass.IndirectOffsetOnAxis(ap=idx_sb[:, kc:kc + 1], axis=0))
                    s_g.inc(s_at, 16)
                    s_g.inc(g.indirect_dma_start(
                        out=STB[:, :], out_offset=None,
                        in_=lnx_h.ap(),
                        in_offset=bass.IndirectOffsetOnAxis(ap=idx_sb[:, kc:kc + 1], axis=0)), 16)

                @block.sync
                def _(sy, kc=kc):
                    s_g.wait(sy)
                    s_st.inc(sy.dma_start(out=xown_h.ap()[kc * 128:(kc + 1) * 128, :],
                                          in_=STF[:, :]), 16)
                    s_st.inc(sy.dma_start(out=lnown_h.ap()[kc * 128:(kc + 1) * 128, :],
                                          in_=STB[:, :]), 16)

            ctx.barrier()

        with contextlib.ExitStack() as scd:
            WO3 = ctx.sbuf(tag + "wo3", [128, FC * 1024], BF16, scope=scd)
            LNOW = ctx.sbuf(tag + "lnow", [128, FC * 512], BF16, scope=scd)
            XOW = ctx.sbuf(tag + "xow", [128, FC * 512], F32, scope=scd)
            OST = ctx.sbuf(tag + "ostd", [128, FC * 512], F32, scope=scd)
            TT = ctx.sbuf(tag + "tt", [128, 512], F32, scope=scd)
            HT = ctx.sbuf(tag + "ht", [128, 512], F32, scope=scd)
            ps_p = [scd.enter_context(nc.psum_tensor(tag + "psd%d" % i, [128, 512], F32))
                    for i in range(2)]

            @block.gpsimd
            def _(g):
                s_w.inc(g.dma_start(
                    out=WO3[:, :].rearrange("p (kc m) -> p kc m", kc=FC),
                    in_=wo3_d.rearrange("(kc p) m -> p kc m", p=128)), 16)

            for tc in range(8):
                c0 = tc * 512
                s0 = (tc % 4) * 512

                @block.sync
                def _(sy, c0=c0):
                    s_pe.wait(sy)
                    s_v.wait(sy)
                    s_st.wait(sy)
                    s_g.inc(sy.dma_start(
                        out=LNOW[:, :].rearrange("p (kc n) -> p kc n", kc=FC),
                        in_=lnown_h.ap().rearrange("(kc p) n -> p kc n", p=128)
                        [:, :, c0:c0 + 512]), 16)
                    s_g.inc(sy.dma_start(
                        out=XOW[:, :].rearrange("p (kc n) -> p kc n", kc=FC),
                        in_=xown_h.ap().rearrange("(kc p) n -> p kc n", p=128)
                        [:, :, c0:c0 + 512]), 16)

                tc_v0 = s_v.n
                for ms in range(8):
                    base_pe = s_pe.n
                    base_a = s_a.n

                    @block.tensor
                    def _(t, ms=ms, tc_v0=tc_v0):
                        s_g.wait(t)
                        if ms == 0:
                            s_w.wait(t)
                        ps = ps_p[ms % 2]
                        if ms >= 2:
                            t.wait_ge(s_v.h, tc_v0 + 2 * (ms - 2) + 1)
                        last = None
                        for kc in range(FC):
                            last = t.matmul(
                                ps[:, :],
                                WO3[:, kc * 1024 + ms * 128:kc * 1024 + (ms + 1) * 128],
                                LNOW[:, kc * 512:(kc + 1) * 512],
                                start=(kc == 0), stop=(kc == FC - 1))
                        s_pe.inc(last)

                    @block.vector
                    def _(v, ms=ms, s0=s0, base_pe=base_pe):
                        v.wait_ge(s_pe.h, base_pe + 1)
                        ps = ps_p[ms % 2]
                        s_v.inc(v.tensor_add(TT[:, :], ps[:, :],
                                             UOT[:, ms * 2048 + s0:ms * 2048 + s0 + 512]))

                    @block.scalar
                    def _(sc, ms=ms):
                        s_v.wait(sc)
                        s_a.inc(sc.activation(out=HT[:, :], in_=TT[:, :],
                                              func=AF.Tanh, bias=bcol[:, 24 + ms:25 + ms]))

                    @block.vector
                    def _(v, ms=ms):
                        s_a.wait(v)
                        s_v.inc(v.tensor_add(OST[:, ms * 512:(ms + 1) * 512],
                                             XOW[:, ms * 512:(ms + 1) * 512], HT[:, :]))

                @block.scalar
                def _(sc, c0=c0):
                    s_v.wait(sc)
                    s_st.inc(sc.dma_start(
                        out=y2_h.ap().rearrange("(kc p) n -> p kc n", p=128)[:, :, c0:c0 + 512],
                        in_=OST[:, :].rearrange("p (kc n) -> p kc n", kc=FC)), 16)

            ctx.barrier()



def emit_l1_attn(ctx, xprov, xrow, yout_d, wqT_d, woT_d, qkT_d, vtok_d, tag="b1"):
    nc, block = ctx.nc, ctx.block
    s_ld = ctx.sem(tag + "ld")
    s_sp = ctx.sem(tag + "sp")
    s_at = ctx.sem(tag + "at")
    s_pe = ctx.sem(tag + "pe")
    s_v = ctx.sem(tag + "vv")
    s_a = ctx.sem(tag + "aa")
    s_g = ctx.sem(tag + "gg")
    s_cs = ctx.sem(tag + "cs")
    PI = 3.14159265358979323846

    ones2 = ctx.sbuf(tag + "ones2", [128, 2], BF16)
    onesf = ctx.sbuf(tag + "onesf", [128, 1], F32)
    ones_row = ctx.sbuf(tag + "onesr", [1, 128], BF16)

    @block.gpsimd
    def _(g):
        s_cs.inc(g.memset(ones2[:, 0:1], -1.0 / D))
        s_cs.wait(g)
        g.memset(ones2[:, 1:2], 1.0)
        g.memset(onesf[:, 0:1], -1.0 / D)
        s_g.inc(g.memset(ones_row[:, :], 1.0))

    with contextlib.ExitStack() as sc1:
        WQ = ctx.sbuf(tag + "wq", [128, FC * 3072], BF16, scope=sc1)
        XRC = ctx.sbuf(tag + "xrc", [128, FC * TCH], F32, scope=sc1)
        XC = ctx.sbuf(tag + "xcen", [128, FC * TCH], F32, scope=sc1)
        LNB = ctx.sbuf(tag + "lnb", [128, FC * TCH], BF16, scope=sc1)
        STG = ctx.sbuf(tag + "stg", [128, 8 * TCH], BF16, scope=sc1)
        VSTG = ctx.sbuf(tag + "vstg", [128, 4 * 1024], BF16, scope=sc1)
        row1 = ctx.sbuf(tag + "row1", [1, TCH], BF16, scope=sc1)
        rowr = ctx.sbuf(tag + "rowr", [1, TCH], BF16, scope=sc1)
        rowv = ctx.sbuf(tag + "rowv", [1, TCH], F32, scope=sc1)

        ps_sum = sc1.enter_context(nc.psum_tensor(tag + "ps_sum", [1, TCH], F32))
        ps_mb = sc1.enter_context(nc.psum_tensor(tag + "ps_mb", [128, TCH], F32))
        ps_rb = sc1.enter_context(nc.psum_tensor(tag + "ps_rb", [128, TCH], F32))
        ps_q0 = sc1.enter_context(nc.psum_tensor(tag + "ps_q0", [128, TCH], F32))
        ps_q1 = sc1.enter_context(nc.psum_tensor(tag + "ps_q1", [128, TCH], F32))
        ps_v0 = sc1.enter_context(nc.psum_tensor(tag + "ps_v0", [128, 512], F32))
        ps_v1 = sc1.enter_context(nc.psum_tensor(tag + "ps_v1", [128, 512], F32))
        consts = (onesf, ones2, ones_row, ps_sum, ps_mb, ps_rb, row1, rowr, rowv)

        @block.gpsimd
        def _(g):
            s_ld.inc(g.dma_start(
                out=WQ[:, :].rearrange("p (kc m) -> p kc m", kc=FC),
                in_=wqT_d.rearrange("(kc p) m -> p kc m", p=128)), 16)

        for tc in range(NTC):
            c0 = tc * TCH

            @block.sync
            def _(sy, c0=c0):
                s_a.wait(sy)
                s_v.wait(sy)
                s_pe.wait(sy)
                s_sp.wait(sy)
                s_g.inc(sy.dma_start(
                    out=XRC[:, :].rearrange("p (kc n) -> p kc n", kc=FC),
                    in_=xprov(c0, TCH)), 16)

            @block.tensor
            def _(t):
                s_g.wait(t)
                s_ld.wait(t)

            @block.vector
            def _(v):
                s_g.wait(v)

            @block.scalar
            def _(sc):
                s_g.wait(sc)

            emit_ln(ctx, XRC, LNB, XC, consts, 0, (s_pe, s_v, s_a), xstride=TCH)

            for grp in range(2):
                base_v = s_v.n
                base_pe = s_pe.n

                @block.tensor
                def _(t, grp=grp, base_v=base_v):
                    s_v.wait(t)
                    s_sp.wait(t)
                    for j in range(8):
                        ms = grp * 8 + j
                        ps = ps_q0 if j % 2 == 0 else ps_q1
                        if j >= 2:
                            t.wait_ge(s_v.h, base_v + j - 1)
                        last = None
                        for kc in range(FC):
                            last = t.matmul(ps[:, :],
                                            WQ[:, kc * 3072 + ms * 128:kc * 3072 + (ms + 1) * 128],
                                            LNB[:, kc * TCH:(kc + 1) * TCH],
                                            start=(kc == 0), stop=(kc == FC - 1))
                        s_pe.inc(last)

                @block.vector
                def _(v, grp=grp, base_pe=base_pe):
                    for j in range(8):
                        ps = ps_q0 if j % 2 == 0 else ps_q1
                        v.wait_ge(s_pe.h, base_pe + j + 1)
                        s_v.inc(v.tensor_copy(STG[:, j * TCH:(j + 1) * TCH], ps[:, :]))

                @block.sync
                def _(sy, c0=c0, grp=grp):
                    s_v.wait(sy)
                    s_sp.inc(sy.dma_start(
                        out=qkT_d.rearrange("(ms p) n -> p ms n", p=128)[:, grp * 8:(grp + 1) * 8,
                                                                        c0:c0 + TCH],
                        in_=STG[:, :].rearrange("p (ms n) -> p ms n", ms=8)), 16)

            base_pe = s_pe.n
            base_v = s_v.n

            @block.tensor
            def _(t, base_v=base_v):
                s_sp.wait(t)
                for i in range(8):
                    ts, half = divmod(i, 2)
                    ps = ps_v0 if i % 2 == 0 else ps_v1
                    if i >= 2:
                        t.wait_ge(s_v.h, base_v + i - 1)
                    last = None
                    for kc in range(FC):
                        last = t.matmul(
                            ps[:, :],
                            LNB[:, kc * TCH + ts * 128:kc * TCH + (ts + 1) * 128],
                            WQ[:, kc * 3072 + 2048 + half * 512:kc * 3072 + 2048 + (half + 1) * 512],
                            start=(kc == 0), stop=(kc == FC - 1))
                    s_pe.inc(last)

            @block.vector
            def _(v, base_pe=base_pe):
                for i in range(8):
                    ts, half = divmod(i, 2)
                    ps = ps_v0 if i % 2 == 0 else ps_v1
                    v.wait_ge(s_pe.h, base_pe + i + 1)
                    s_v.inc(v.tensor_copy(
                        VSTG[:, ts * 1024 + half * 512:ts * 1024 + (half + 1) * 512], ps[:, :]))

            @block.scalar
            def _(sc, c0=c0):
                s_v.wait(sc)
                s_sp.inc(sc.dma_start(
                    out=vtok_d[c0:c0 + TCH, :].rearrange("(ts p) n -> p ts n", p=128),
                    in_=VSTG[:, :].rearrange("p (ts n) -> p ts n", ts=4)), 16)

        ctx.barrier()

    with contextlib.ExitStack() as sc2:
        C1 = ctx.sbuf(tag + "c1", [128, S], F32, scope=sc2)
        C2 = ctx.sbuf(tag + "c2", [128, S], F32, scope=sc2)
        T1 = ctx.sbuf(tag + "t1", [128, S], F32, scope=sc2)
        T2 = ctx.sbuf(tag + "t2", [128, S], F32, scope=sc2)
        QIN = ctx.sbuf(tag + "qin", [128, S], BF16, scope=sc2)
        KFC = ctx.sbuf(tag + "kfc", [128, S], BF16, scope=sc2)
        Pm = ctx.sbuf(tag + "pm", [128, 128], BF16, scope=sc2)
        IDN = ctx.sbuf(tag + "idn", [128, 128], BF16, scope=sc2)
        pi32 = ctx.sbuf(tag + "pi32", [128, 1], I32, scope=sc2)
        sgn = ctx.sbuf(tag + "sgn", [128, 1], F32, scope=sc2)
        tsw = ctx.sbuf(tag + "tsw", [128, 1], F32, scope=sc2)
        invf = ctx.sbuf(tag + "invf", [128, 1], F32, scope=sc2)
        QF = ctx.sbuf(tag + "qf", [128, FC * S], BF16, scope=sc2)
        KT = ctx.sbuf(tag + "kt", [128, 16 * 1024], BF16, scope=sc2)
        va = [ctx.sbuf(tag + "va%d" % i, [128, 16 * 65], BF16, scope=sc2) for i in range(2)]
        kvs = ctx.sbuf(tag + "kvs", [64, 65], BF16, scope=sc2)
        kvs_hi = ctx.sbuf(tag + "kvsh", [128, 65], BF16, scope=sc2)
        outn = ctx.sbuf(tag + "outn", [128, FC * S], BF16, scope=sc2)
        rec_row = ctx.sbuf(tag + "rec", [1, S], BF16, scope=sc2)
        rb_sb = ctx.sbuf(tag + "rb", [64, 1024], BF16, scope=sc2)
        WOUT = ctx.sbuf(tag + "wout", [128, FC * 1024], BF16, scope=sc2)
        tbl_scope = contextlib.ExitStack()
        TH = ctx.sbuf(tag + "th", [128, S], F32, scope=tbl_scope)
        TI = ctx.sbuf(tag + "ti", [128, S], I32, scope=tbl_scope)
        i5 = ctx.sbuf(tag + "i5", [128, 1], I32, scope=tbl_scope)
        ib1 = ctx.sbuf(tag + "ib1", [128, 1], I32, scope=tbl_scope)
        iw = ctx.sbuf(tag + "iw", [128, 1], I32, scope=tbl_scope)
        recf = T1

        @block.gpsimd
        def _(g):
            s_ld.inc(g.dma_start(
                out=WOUT[:, :].rearrange("p (kc m) -> p kc m", kc=FC),
                in_=woT_d.rearrange("(kc p) m -> p kc m", p=128)), 16)
            g.iota(pi32[:, :], [[1, 1]], base=0, channel_multiplier=1)
            g.iota(TH[:, :], [[1, S]], base=0, channel_multiplier=0,
                   allow_small_or_imprecise_dtypes=True)
            s_cs.inc(g.memset(IDN[:, :], 0.0))
            s_cs.wait(g)
            g.affine_select(out=IDN[:, :], in_=IDN[:, :], compare_op=ALU.not_equal,
                            fill=1.0, base=0, pattern=[[-1, 128]], channel_multiplier=1)
            for i in range(2):
                g.memset(va[i][:, :], 1.0)
            s_g.inc(g.nop())

        @block.vector
        def _(v):
            s_g.wait(v)
            s_v.inc(v.tensor_scalar(i5[:, :], pi32[:, :], 5, None, ALU.arith_shift_right))
            s_v.wait(v)
            s_v.inc(v.tensor_scalar(ib1[:, :], i5[:, :], 1, None, ALU.bitwise_and))
            s_v.wait(v)
            s_v.inc(v.tensor_scalar(iw[:, :], ib1[:, :], -64, 32, ALU.mult, ALU.add))
            s_v.wait(v)
            s_v.inc(v.tensor_tensor(iw[:, :], iw[:, :], pi32[:, :], ALU.add))
            s_v.wait(v)
            s_v.inc(v.tensor_copy(tsw[:, :], iw[:, :]))
            s_v.wait(v)
            s_v.inc(v.tensor_scalar(sgn[:, :], ib1[:, :], 2, -1, ALU.mult, ALU.add))
            s_v.wait(v)
            s_v.inc(v.tensor_scalar(Pm[:, :], TH[:, 0:128], tsw[:, 0:1], None,
                                    ALU.is_equal))
            s_v.wait(v)
            s_v.inc(v.tensor_scalar(iw[:, :], i5[:, :], -32, None, ALU.mult))
            s_v.wait(v)
            s_v.inc(v.tensor_tensor(iw[:, :], iw[:, :], pi32[:, :], ALU.add))
            s_v.wait(v)
            s_v.inc(v.tensor_copy(invf[:, :], iw[:, :]))

        @block.scalar
        def _(sc):
            s_v.wait(sc)
            import math as _m
            s_a.inc(sc.activation(out=invf[:, :], in_=invf[:, :], func=AF.Exp,
                                  scale=-_m.log(10000.0) / 32.0))

        @block.vector
        def _(v):
            s_a.wait(v)
            s_v.inc(v.tensor_scalar(TH[:, :], TH[:, :], invf[:, 0:1], None, ALU.mult))
            s_v.wait(v)
            s_v.inc(v.tensor_scalar(T1[:, :], TH[:, :], 1.0 / (2 * PI), None, ALU.mult))
            s_v.wait(v)
            s_v.inc(v.tensor_copy(TI[:, :], T1[:, :]))
            s_v.wait(v)
            s_v.inc(v.tensor_copy(T1[:, :], TI[:, :]))
            s_v.wait(v)
            s_v.inc(v.scalar_tensor_tensor(T2[:, :], T1[:, :], -2 * PI, TH[:, :],
                                           ALU.mult, ALU.add))

        @block.scalar
        def _(sc):
            s_v.wait(sc)
            s_a.inc(sc.activation(out=C2[:, :], in_=T2[:, :], func=AF.Sin))

        @block.vector
        def _(v):
            s_a.wait(v)
            s_v.inc(v.tensor_scalar(C2[:, :], C2[:, :], sgn[:, 0:1], None, ALU.mult))
            s_v.wait(v)
            s_v.inc(v.tensor_scalar(T1[:, :], TH[:, :], 1.0 / (2 * PI), 1.0 / 4.0,
                                    ALU.mult, ALU.add))
            s_v.wait(v)
            s_v.inc(v.tensor_copy(TI[:, :], T1[:, :]))
            s_v.wait(v)
            s_v.inc(v.tensor_copy(T1[:, :], TI[:, :]))
            s_v.wait(v)
            s_v.inc(v.scalar_tensor_tensor(T2[:, :], T1[:, :], -2 * PI, TH[:, :],
                                           ALU.mult, ALU.add))
            s_v.wait(v)
            s_v.inc(v.tensor_scalar(T2[:, :], T2[:, :], 1.0, PI / 2.0, ALU.mult, ALU.add))

        @block.scalar
        def _(sc):
            s_v.wait(sc)
            s_a.inc(sc.activation(out=C1[:, :], in_=T2[:, :], func=AF.Sin))

        ctx.barrier()
        tbl_scope.close()

        for b in range(NB):
            sc2b = contextlib.ExitStack()
            ps_sw = sc2b.enter_context(nc.psum_tensor(tag + "ps_sw%d" % b, [128, S], F32))
            ps_tr = sc2b.enter_context(nc.psum_tensor(tag + "ps_tr%d" % b, [128, 128], BF16))
            for c in range(FC):
                for which in ("k", "q"):
                    rows0 = 1024 if which == "k" else 0

                    @block.sync
                    def _(sy, c=c, rows0=rows0, b=b):
                        s_v.wait(sy)
                        s_a.wait(sy)
                        s_sp.wait(sy)
                        s_at.inc(sy.dma_start(
                            out=QIN[:, :],
                            in_=qkT_d[rows0 + c * 128:rows0 + (c + 1) * 128,
                                      b * S:(b + 1) * S]), 16)

                    @block.tensor
                    def _(t, c=c):
                        s_at.wait(t)
                        s_v.wait(t)
                        s_g.wait(t)
                        last = None
                        for nq in range(4):
                            last = t.matmul(ps_sw[:, nq * 512:(nq + 1) * 512], Pm[:, :],
                                            QIN[:, nq * 512:(nq + 1) * 512],
                                            start=True, stop=True)
                        s_pe.inc(last)

                    @block.vector
                    def _(v, which=which, c=c):
                        s_pe.wait(v)
                        s_a.wait(v)
                        sc_ = 0.125 if which == "q" else 1.0
                        s_v.inc(v.tensor_mul(T1[:, :], QIN[:, :], C1[:, :]))
                        s_v.wait(v)
                        s_v.inc(v.tensor_mul(T2[:, :], ps_sw[:, :], C2[:, :]))
                        s_v.wait(v)
                        s_v.inc(v.tensor_add(T1[:, :], T1[:, :], T2[:, :]))
                        s_v.wait(v)
                        s_v.inc(v.tensor_scalar(T2[:, :], T1[:, :], 0.0, sc_,
                                                ALU.min, ALU.mult))
                        s_v.wait(v)
                        s_v.inc(v.tensor_scalar(T1[:, :], T1[:, :], 0.0, sc_,
                                                ALU.max, ALU.mult))

                    @block.scalar
                    def _(sc):
                        s_v.wait(sc)
                        s_a.inc(sc.activation(out=T2[:, :], in_=T2[:, :], func=AF.Exp))

                    @block.vector
                    def _(v, which=which, c=c):
                        s_a.wait(v)
                        dst = KFC[:, :] if which == "k" else QF[:, c * S:(c + 1) * S]
                        s_v.inc(v.tensor_add(dst, T1[:, :], T2[:, :]))

                    if which == "k":
                        @block.tensor
                        def _(t, c=c):
                            s_v.wait(t)
                            for j in range(16):
                                if j > 0:
                                    t.wait_ge(s_v.h, s_v.n + j)
                                t.transpose(ps_tr[:, :], KFC[:, j * 128:(j + 1) * 128],
                                            IDN[:, :])
                                s_pe.inc(t.nop())

                        @block.vector
                        def _(v, c=c):
                            base = s_pe.n - 16
                            for j in range(16):
                                v.wait_ge(s_pe.h, base + j + 1)
                                s_v.inc(v.tensor_copy(
                                    KT[:, j * 1024 + c * 128:j * 1024 + (c + 1) * 128],
                                    ps_tr[:, :]))

            ctx.barrier()
            sc2b.close()

            with contextlib.ExitStack() as sc3:
                ps_kv = sc3.enter_context(nc.psum_tensor(tag + "ps_kv%d" % b, [64, 65], F32))
                ps_out = sc3.enter_context(nc.psum_tensor(tag + "ps_out%d" % b, [65, S], F32))
                ps_rb2 = sc3.enter_context(nc.psum_tensor(tag + "ps_rb2%d" % b, [64, 1024], F32))

                for h in range(H):
                    ib = h % 2

                    @block.sync
                    def _(sy, h=h, ib=ib, b=b):
                        s_pe.wait(sy)
                        s_g.wait(sy)
                        s_at.inc(sy.dma_start(
                            out=va[ib][:, :].rearrange("p (kc d) -> p kc d", kc=16)[:, :, 0:64],
                            in_=vtok_d[b * S:(b + 1) * S, h * 64:(h + 1) * 64]
                            .rearrange("(kc p) d -> p kc d", p=128)), 16)

                    @block.tensor
                    def _(t, h=h, ib=ib):
                        s_at.wait(t)
                        s_v.wait(t)
                        last = None
                        for j in range(16):
                            last = t.matmul(ps_kv[:, :],
                                            KT[:, j * 1024 + h * 64:j * 1024 + (h + 1) * 64],
                                            va[ib][:, j * 65:(j + 1) * 65],
                                            start=(j == 0), stop=(j == 15))
                        s_pe.inc(last)

                    @block.vector
                    def _(v):
                        s_pe.wait(v)
                        with nc.allow_low_precision(reason="bf16 matmul operand"):
                            s_v.inc(v.tensor_copy(kvs[:, :], ps_kv[:, :]))

                    if h % 2 == 1:
                        @block.gpsimd
                        def _(g):
                            s_v.wait(g)
                            s_at.inc(g.dma_start(out=kvs_hi[64:128, :], in_=kvs[:, :]), 16)

                    @block.tensor
                    def _(t, h=h):
                        s_v.wait(t)
                        s_at.wait(t)
                        ch, half64 = h // 2, (h % 2) * 64
                        lhs = kvs[:, :] if h % 2 == 0 else kvs_hi[64:128, :]
                        last = None
                        for nq in range(4):
                            last = t.matmul(ps_out[:, nq * 512:(nq + 1) * 512], lhs,
                                            QF[half64:half64 + 64,
                                               ch * S + nq * 512:ch * S + (nq + 1) * 512],
                                            start=True, stop=True)
                        s_pe.inc(last)

                    @block.vector
                    def _(v):
                        s_pe.wait(v)
                        s_v.inc(v.tensor_scalar(recf[0:1, :], ps_out[64:65, :], 1e-6,
                                                None, ALU.max))
                        s_v.wait(v)
                        with nc.allow_low_precision(reason="bf16 matmul operand"):
                            s_v.inc(v.reciprocal(rec_row[:, :], recf[0:1, :]))

                    for half in range(2):
                        @block.tensor
                        def _(t, half=half):
                            s_v.wait(t)
                            if half == 1:
                                s_a.wait(t)
                            t.matmul(ps_rb2[:, 0:512], ones_row[:, 0:64],
                                     rec_row[:, half * 1024:half * 1024 + 512],
                                     start=True, stop=True)
                            s_pe.inc(t.matmul(ps_rb2[:, 512:1024], ones_row[:, 0:64],
                                              rec_row[:, half * 1024 + 512:(half + 1) * 1024],
                                              start=True, stop=True))

                        @block.scalar
                        def _(sc, half=half):
                            s_pe.wait(sc)
                            s_v.wait(sc)
                            s_a.inc(sc.copy(out=rb_sb[:, :], in_=ps_rb2[:, :]))

                        @block.vector
                        def _(v, h=h, half=half):
                            s_a.wait(v)
                            ch, half64 = h // 2, (h % 2) * 64
                            s_v.inc(v.tensor_mul(
                                outn[half64:half64 + 64,
                                     ch * S + half * 1024:ch * S + (half + 1) * 1024],
                                ps_out[0:64, half * 1024:(half + 1) * 1024], rb_sb[:, :]))

            ctx.barrier()

            with contextlib.ExitStack() as sc4:
                XLD = ctx.sbuf(tag + "xld%d" % b, [128, S], F32, scope=sc4)
                OST = ctx.sbuf(tag + "ost%d" % b, [128, S], F32, scope=sc4)
                ps_o = [sc4.enter_context(nc.psum_tensor(tag + "ps_o%d_%d" % (b, i),
                                                          [128, 512], F32))
                        for i in range(2)]

                for ms in range(8):
                    @block.sync
                    def _(sy, ms=ms, b=b):
                        s_v.wait(sy)
                        s_g.inc(sy.dma_start(
                            out=XLD[:, :].rearrange("p (j n) -> p j n", n=256),
                            in_=xrow(ms * 128, b * S, S)), 16)

                    @block.tensor
                    def _(t, ms=ms):
                        s_v.wait(t)
                        if ms == 0:
                            s_ld.wait(t)
                        s_a.wait(t)
                        s_sp.wait(t)
                        for q4 in range(4):
                            ps = ps_o[q4 % 2]
                            if q4 >= 2:
                                t.wait_ge(s_v.h, s_v.n + q4 - 1)
                            last = None
                            for kc in range(FC):
                                last = t.matmul(
                                    ps[:, :],
                                    WOUT[:, kc * 1024 + ms * 128:kc * 1024 + (ms + 1) * 128],
                                    outn[:, kc * S + q4 * 512:kc * S + (q4 + 1) * 512],
                                    start=(kc == 0), stop=(kc == FC - 1))
                            s_pe.inc(last)

                    @block.vector
                    def _(v, ms=ms):
                        base = s_pe.n - 4
                        s_g.wait(v)
                        s_sp.wait(v)
                        for q4 in range(4):
                            ps = ps_o[q4 % 2]
                            v.wait_ge(s_pe.h, base + q4 + 1)
                            s_v.inc(v.tensor_add(OST[:, q4 * 512:(q4 + 1) * 512],
                                                 XLD[:, q4 * 512:(q4 + 1) * 512], ps[:, :]))

                    @block.scalar
                    def _(sc, ms=ms, b=b):
                        s_v.wait(sc)
                        s_sp.inc(sc.dma_start(
                            out=yout_d[ms * 128:(ms + 1) * 128, b * S:(b + 1) * S],
                            in_=OST[:, :]), 16)

            ctx.barrier()




def emit_l1_ffn(ctx, xprov, xrow, yout_d, wr_d, br_d, w1_d, b1_d, w2_d, b2_d, lnx_d,
                tag="m1"):
    nc, block = ctx.nc, ctx.block
    s_ld = ctx.sem(tag + "ld")
    s_w = ctx.sem(tag + "wl")
    s_sp = ctx.sem(tag + "sp")
    s_pe = ctx.sem(tag + "pe")
    s_v = ctx.sem(tag + "vv")
    s_a = ctx.sem(tag + "aa")
    s_g = ctx.sem(tag + "gg")
    s_cs = ctx.sem(tag + "cs")

    ones2 = ctx.sbuf(tag + "ones2", [128, 2], BF16)
    onesf = ctx.sbuf(tag + "onesf", [128, 1], F32)
    ones_row = ctx.sbuf(tag + "onesr", [1, 128], BF16)

    @block.gpsimd
    def _(g):
        s_cs.inc(g.memset(ones2[:, 0:1], -1.0 / D))
        s_cs.wait(g)
        g.memset(ones2[:, 1:2], 1.0)
        g.memset(onesf[:, 0:1], -1.0 / D)
        s_g.inc(g.memset(ones_row[:, :], 1.0))

    with contextlib.ExitStack() as sca:
        XRC = ctx.sbuf(tag + "xrc", [128, FC * TCH], F32, scope=sca)
        XC = ctx.sbuf(tag + "xc", [128, FC * TCH], F32, scope=sca)
        LNB = ctx.sbuf(tag + "lnb", [128, FC * TCH], BF16, scope=sca)
        row1 = ctx.sbuf(tag + "row1", [1, TCH], BF16, scope=sca)
        rowr = ctx.sbuf(tag + "rowr", [1, TCH], BF16, scope=sca)
        rowv = ctx.sbuf(tag + "rowv", [1, TCH], F32, scope=sca)
        ps_sum = sca.enter_context(nc.psum_tensor(tag + "ps_sum", [1, TCH], F32))
        ps_mb = sca.enter_context(nc.psum_tensor(tag + "ps_mb", [128, TCH], F32))
        ps_rb = sca.enter_context(nc.psum_tensor(tag + "ps_rb", [128, TCH], F32))
        consts = (onesf, ones2, ones_row, ps_sum, ps_mb, ps_rb, row1, rowr, rowv)

        for tc in range(NTC):
            c0 = tc * TCH

            @block.sync
            def _(sy, c0=c0):
                s_a.wait(sy)
                s_v.wait(sy)
                s_pe.wait(sy)
                s_sp.wait(sy)
                s_g.inc(sy.dma_start(
                    out=XRC[:, :].rearrange("p (kc n) -> p kc n", kc=FC),
                    in_=xprov(c0, TCH)), 16)

            @block.tensor
            def _(t):
                s_g.wait(t)

            @block.vector
            def _(v):
                s_g.wait(v)

            @block.scalar
            def _(sc):
                s_g.wait(sc)

            emit_ln(ctx, XRC, LNB, XC, consts, 0, (s_pe, s_v, s_a), xstride=TCH)

            @block.sync
            def _(sy, c0=c0):
                s_v.wait(sy)
                s_sp.inc(sy.dma_start(
                    out=lnx_d.rearrange("(kc p) n -> p kc n", p=128)[:, :, c0:c0 + TCH],
                    in_=LNB[:, :].rearrange("p (kc n) -> p kc n", kc=FC)), 16)

        ctx.barrier()

    comb = ctx.sbuf(tag + "comb", [128, 32 * 16], F32)
    combT = ctx.sbuf(tag + "combT", [16, NT], BF16)
    SEL = ctx.sbuf(tag + "sel", [16, 16 * 128], BF16)

    with contextlib.ExitStack() as scb:
        WR = ctx.sbuf(tag + "wrr", [128, FC * 16], BF16, scope=scb)
        brb = ctx.sbuf(tag + "brb", [128, 16], F32, scope=scb)
        brrow = ctx.sbuf(tag + "brrow", [1, 16], BF16, scope=scb)
        LNL = ctx.sbuf(tag + "lnl", [128, FC * 128], BF16, scope=scb)
        lg = ctx.sbuf(tag + "lg", [128, 16], F32, scope=scb)
        pr2 = ctx.sbuf(tag + "pr2", [128, 16], F32, scope=scb)
        msk1 = ctx.sbuf(tag + "msk1", [128, 16], F32, scope=scb)
        msk2 = ctx.sbuf(tag + "msk2", [128, 16], F32, scope=scb)
        m1c = ctx.sbuf(tag + "m1c", [128, 1], F32, scope=scb)
        m2c = ctx.sbuf(tag + "m2c", [128, 1], F32, scope=scb)
        rcc = ctx.sbuf(tag + "rcc", [128, 1], F32, scope=scb)
        selio = ctx.sbuf(tag + "selio", [16, 16 * 128], F32, scope=scb)
        pi16 = ctx.sbuf(tag + "pi16", [16, 1], F32, scope=scb)
        ps_r = scb.enter_context(nc.psum_tensor(tag + "ps_r", [128, 16], F32))
        ps_t = scb.enter_context(nc.psum_tensor(tag + "ps_t", [16, 128], BF16))
        cmb_bf = ctx.sbuf(tag + "cmbbf", [128, 16], BF16, scope=scb)
        idn2 = ctx.sbuf(tag + "idn2", [128, 128], BF16, scope=scb)

        @block.gpsimd
        def _(g):
            s_ld.inc(g.dma_start(
                out=WR[:, :].rearrange("p (kc m) -> p kc m", kc=FC),
                in_=wr_d.rearrange("(kc p) m -> p kc m", p=128)), 16)
            s_ld.inc(g.dma_start(out=brrow[:, :], in_=br_d[0:1, :]), 16)
            g.iota(selio[:, :].rearrange("p (e n) -> p e n", e=16), [[1, 16], [0, 128]],
                   base=0, channel_multiplier=0, allow_small_or_imprecise_dtypes=True)
            s_cs.inc(g.memset(idn2[:, :], 0.0))
            s_cs.wait(g)
            g.affine_select(out=idn2[:, :], in_=idn2[:, :], compare_op=ALU.not_equal,
                            fill=1.0, base=0, pattern=[[-1, 128]], channel_multiplier=1)
            s_g.inc(g.iota(pi16[:, :], [[1, 1]], base=0, channel_multiplier=1,
                           allow_small_or_imprecise_dtypes=True))

        @block.vector
        def _(v):
            s_g.wait(v)
            s_v.inc(v.tensor_scalar(SEL[:, :], selio[:, :], pi16[:, 0:1], None,
                                    ALU.is_equal))

        @block.tensor
        def _(t):
            s_ld.wait(t)
            s_g.wait(t)
            s_pe.inc(t.matmul(ps_r[:, :], ones_row[:, :], brrow[:, :],
                              start=True, stop=True))

        @block.vector
        def _(v):
            s_pe.wait(v)
            s_v.inc(v.tensor_copy(brb[:, :], ps_r[:, :]))

        for ts in range(32):
            @block.sync
            def _(sy, ts=ts):
                s_pe.wait(sy)
                s_v.wait(sy)
                s_g.inc(sy.dma_start(
                    out=LNL[:, :].rearrange("p (kc n) -> p kc n", kc=FC),
                    in_=lnx_d.rearrange("(kc p) n -> p kc n", p=128)
                    [:, :, ts * 128:(ts + 1) * 128]), 16)

            @block.tensor
            def _(t, ts=ts):
                s_g.wait(t)
                s_v.wait(t)
                last = None
                for kc in range(FC):
                    last = t.matmul(ps_r[:, :], LNL[:, kc * 128:(kc + 1) * 128],
                                    WR[:, kc * 16:(kc + 1) * 16],
                                    start=(kc == 0), stop=(kc == FC - 1))
                s_pe.inc(last)

            @block.vector
            def _(v, ts=ts):
                s_pe.wait(v)
                s_a.wait(v)
                s_v.inc(v.tensor_add(lg[:, :], ps_r[:, :], brb[:, :]))
                s_v.wait(v)
                s_v.inc(v.reduce_max(m1c[:, :], lg[:, :], axis=AX.X))
                s_v.wait(v)
                s_v.inc(v.tensor_scalar(m2c[:, :], m1c[:, :], -1.0, None, ALU.mult))

            @block.scalar
            def _(sc):
                s_v.wait(sc)
                s_a.inc(sc.activation(out=lg[:, :], in_=lg[:, :], func=AF.Exp,
                                      bias=m2c[:, 0:1]))

            @block.vector
            def _(v, ts=ts):
                s_a.wait(v)
                s_v.inc(v.reduce_sum(m1c[:, :], lg[:, :], axis=AX.X))
                s_v.wait(v)
                s_v.inc(v.reciprocal(rcc[:, :], m1c[:, :]))
                s_v.wait(v)
                s_v.inc(v.tensor_scalar(lg[:, :], lg[:, :], rcc[:, 0:1], None, ALU.mult))
                s_v.wait(v)
                s_v.inc(v.reduce_max(m1c[:, :], lg[:, :], axis=AX.X))
                s_v.wait(v)
                s_v.inc(v.tensor_scalar(msk1[:, :], lg[:, :], m1c[:, 0:1], None,
                                        ALU.is_equal))
                s_v.wait(v)
                s_v.inc(v.scalar_tensor_tensor(pr2[:, :], msk1[:, :], -4.0, lg[:, :],
                                               ALU.mult, ALU.add))
                s_v.wait(v)
                s_v.inc(v.reduce_max(m2c[:, :], pr2[:, :], axis=AX.X))
                s_v.wait(v)
                s_v.inc(v.tensor_scalar(msk2[:, :], pr2[:, :], m2c[:, 0:1], None,
                                        ALU.is_equal))
                s_v.wait(v)
                s_v.inc(v.tensor_add(rcc[:, :], m1c[:, :], m2c[:, :]))
                s_v.wait(v)
                s_v.inc(v.tensor_scalar(rcc[:, :], rcc[:, :], 1e-9, None, ALU.max))
                s_v.wait(v)
                s_v.inc(v.reciprocal(rcc[:, :], rcc[:, :]))
                s_v.wait(v)
                s_v.inc(v.tensor_scalar(msk1[:, :], msk1[:, :], m1c[:, 0:1], None,
                                        ALU.mult))
                s_v.wait(v)
                s_v.inc(v.tensor_scalar(msk2[:, :], msk2[:, :], m2c[:, 0:1], None,
                                        ALU.mult))
                s_v.wait(v)
                s_v.inc(v.tensor_add(msk1[:, :], msk1[:, :], msk2[:, :]))
                s_v.wait(v)
                s_v.inc(v.tensor_scalar(comb[:, ts * 16:(ts + 1) * 16], msk1[:, :],
                                        rcc[:, 0:1], None, ALU.mult))
                s_v.wait(v)
                with nc.allow_low_precision(reason="bf16 matmul operand"):
                    s_v.inc(v.tensor_copy(cmb_bf[:, :], comb[:, ts * 16:(ts + 1) * 16]))

            @block.tensor
            def _(t, ts=ts):
                s_v.wait(t)
                s_g.wait(t)
                s_pe.inc(t.transpose(ps_t[:, 0:128], cmb_bf[:, :], idn2[:, :]))

            @block.vector
            def _(v, ts=ts):
                s_pe.wait(v)
                s_v.inc(v.tensor_copy(combT[:, ts * 128:(ts + 1) * 128], ps_t[0:16, 0:128]))

        ctx.barrier()

    with contextlib.ExitStack() as scc:
        W1B = [ctx.sbuf(tag + "w1b%d" % i, [128, FC * 2048], BF16, scope=scc)
               for i in range(2)]
        W2B = ctx.sbuf(tag + "w2b", [128, 16 * 1024], BF16, scope=scc)
        ACC = ctx.sbuf(tag + "acc", [128, FC * 1024], F32, scope=scc)
        LNX = ctx.sbuf(tag + "lnx3", [128, 2 * FC * 512], BF16, scope=scc)
        HSB = ctx.sbuf(tag + "hsb", [128, 16 * 512], BF16, scope=scc)
        GTMP = ctx.sbuf(tag + "gtmp", [128, 2 * 512], F32, scope=scc)
        B2S = ctx.sbuf(tag + "b2s", [16, 1024], BF16, scope=scc)
        b1c = ctx.sbuf(tag + "b1c", [128, 256], F32, scope=scc)
        XLD = ctx.sbuf(tag + "xld", [128, 512], F32, scope=scc)
        LNL2 = ctx.sbuf(tag + "lnl2", [128, 512], BF16, scope=scc)
        OST = ctx.sbuf(tag + "ost", [128, 512], F32, scope=scc)

        ps_h = [scc.enter_context(nc.psum_tensor(tag + "ps_h%d" % i, [128, 512], F32))
                for i in range(2)]
        ps_m = [scc.enter_context(nc.psum_tensor(tag + "ps_m%d" % i, [128, 512], F32))
                for i in range(2)]
        ps_cb = scc.enter_context(nc.psum_tensor(tag + "ps_cb", [128, 512], F32))

        @block.gpsimd
        def _(g):
            s_ld.inc(g.dma_start(out=B2S[:, :], in_=b2_d[:, :]), 16)
            s_ld.inc(g.dma_start(out=b1c[:, :], in_=b1_d[:, :]), 16)

        def load_w1(g, e, buf):
            s_w.inc(g.dma_start(
                out=W1B[buf][:, :].rearrange("p (kc m) -> p kc m", kc=FC),
                in_=w1_d[e].rearrange("(kc p) m -> p kc m", p=128)), 16)

        def load_w2(g, e):
            s_w.inc(g.dma_start(
                out=W2B[:, :].rearrange("p (hc m) -> p hc m", hc=16),
                in_=w2_d[e].rearrange("(hc p) m -> p hc m", p=128)), 16)

        NPASS = 4
        for half in range(NPASS):
            hc0 = half * 1024

            for tc in range(2):
                cbase = hc0 + tc * 512

                base_v0 = s_v.n

                @block.tensor
                def _(t, tc=tc, cbase=cbase, half=half, base_v0=base_v0):
                    s_v.wait(t)
                    s_ld.wait(t)
                    for os in range(8):
                        ps = ps_m[os % 2]
                        if os >= 2:
                            t.wait_ge(s_v.h, base_v0 + os - 1)
                        s_pe.inc(t.matmul(ps[:, :], B2S[:, os * 128:(os + 1) * 128],
                                          combT[:, cbase:cbase + 512],
                                          start=True, stop=True))

                @block.vector
                def _(v, tc=tc):
                    base = s_pe.n - 8
                    for os in range(8):
                        ps = ps_m[os % 2]
                        v.wait_ge(s_pe.h, base + os + 1)
                        s_v.inc(v.tensor_copy(ACC[:, os * 1024 + tc * 512:os * 1024 + (tc + 1) * 512],
                                              ps[:, :]))

            nld = [0]

            def lnx_load(it, half=half):
                e2, tc2 = divmod(it, 2)
                cb2 = half * 1024 + tc2 * 512

                @block.sync
                def _(sy, it=it, cb2=cb2):
                    s_pe.wait(sy)
                    sl0 = (it % 2) * FC * 512
                    s_g.inc(sy.dma_start(
                        out=LNX[:, sl0:sl0 + FC * 512].rearrange(
                            "p (kc n) -> p kc n", kc=FC),
                        in_=lnx_d.rearrange("(kc p) n -> p kc n", p=128)
                        [:, :, cb2:cb2 + 512]), 16)
                nld[0] += 1

            lnx_base_g = s_g.n
            lnx_load(0)

            for e in range(16):
                @block.gpsimd
                def _(g, e=e, half=half):
                    if e == 0 and half == 0:
                        load_w1(g, 0, 0)
                        load_w2(g, 0)
                    s_pe.wait(g)
                    nxt = e + 1 + half * 16
                    if nxt < 16 * 4:
                        load_w1(g, (e + 1) % 16, (e + 1) % 2)

                @block.gpsimd
                def _(g, e=e, half=half):
                    if e > 0 or half > 0:
                        s_pe.wait(g)
                        load_w2(g, e)

                for tc in range(2):
                    cbase = hc0 + tc * 512
                    it = e * 2 + tc
                    if it + 1 < 32:
                        lnx_load(it + 1)
                    lslot = (it % 2) * FC * 512

                    @block.tensor
                    def _(t, tc=tc, cbase=cbase, e=e, it=it, lslot=lslot):
                        t.wait_ge(s_g.h, lnx_base_g + 16 * (it + 1))
                        s_w.wait(t)
                        s_v.wait(t)
                        t.matmul(ps_cb[:, :], SEL[:, e * 128:(e + 1) * 128],
                                 combT[:, cbase:cbase + 512], start=True, stop=True)
                        s_pe.inc(t.nop())
                        base_a = s_a.n
                        for hs in range(16):
                            ps = ps_h[hs % 2]
                            if hs >= 2:
                                t.wait_ge(s_a.h, base_a + hs - 1)
                            last = None
                            for kc in range(FC):
                                last = t.matmul(
                                    ps[:, :],
                                    W1B[e % 2][:, kc * 2048 + hs * 128:kc * 2048 + (hs + 1) * 128],
                                    LNX[:, lslot + kc * 512:lslot + (kc + 1) * 512],
                                    start=(kc == 0), stop=(kc == FC - 1))
                            s_pe.inc(last)

                    @block.scalar
                    def _(sc, e=e):
                        base = s_pe.n - 16
                        base_v = s_v.n
                        for hs in range(16):
                            ps = ps_h[hs % 2]
                            sc.wait_ge(s_pe.h, base + hs + 1)
                            if hs > 1:
                                sc.wait_ge(s_v.h, base_v + hs - 1)
                            s_a.inc(sc.activation(
                                out=GTMP[:, (hs % 2) * 512:(hs % 2) * 512 + 512],
                                in_=ps[:, :], func=AF.Gelu,
                                bias=b1c[:, e * 16 + hs:e * 16 + hs + 1]))

                    @block.vector
                    def _(v, e=e):
                        base = s_a.n - 16
                        for hs in range(16):
                            v.wait_ge(s_a.h, base + hs + 1)
                            s_v.inc(v.tensor_mul(
                                HSB[:, hs * 512:(hs + 1) * 512],
                                GTMP[:, (hs % 2) * 512:(hs % 2) * 512 + 512],
                                ps_cb[:, :]))

                    @block.tensor
                    def _(t, tc=tc, e=e):
                        base_mul = s_v.n - 16
                        for os in range(8):
                            ps = ps_m[os % 2]
                            if os == 0:
                                pass
                            elif os == 1:
                                t.wait_ge(s_v.h, base_mul + 16)
                            elif os >= 2:
                                t.wait_ge(s_v.h, s_v.n + os - 1)
                            last = None
                            for hc in range(16):
                                if os == 0:
                                    t.wait_ge(s_v.h, base_mul + hc + 1)
                                last = t.matmul(
                                    ps[:, :],
                                    W2B[:, hc * 1024 + os * 128:hc * 1024 + (os + 1) * 128],
                                    HSB[:, hc * 512:(hc + 1) * 512],
                                    start=(hc == 0), stop=(hc == 15))
                            s_pe.inc(last)

                    @block.vector
                    def _(v, tc=tc):
                        base = s_pe.n - 8
                        for os in range(8):
                            ps = ps_m[os % 2]
                            v.wait_ge(s_pe.h, base + os + 1)
                            sl = ACC[:, os * 1024 + tc * 512:os * 1024 + (tc + 1) * 512]
                            s_v.inc(v.tensor_add(sl, sl, ps[:, :]))

            for os in range(8):
                for tc in range(2):
                    cbase = hc0 + tc * 512

                    @block.sync
                    def _(sy, os=os, tc=tc, cbase=cbase):
                        s_v.wait(sy)
                        s_sp.wait(sy)
                        s_g.inc(sy.dma_start(out=XLD[:, :].rearrange("p (j n) -> p j n", n=256),
                                             in_=xrow(os * 128, cbase, 512)), 16)
                        s_g.inc(sy.dma_start(out=LNL2[:, :],
                                             in_=lnx_d[os * 128:(os + 1) * 128,
                                                       cbase:cbase + 512]), 16)

                    @block.vector
                    def _(v, os=os, tc=tc):
                        s_g.wait(v)
                        s_v.inc(v.tensor_add(OST[:, :], XLD[:, :], LNL2[:, :]))
                        s_v.wait(v)
                        s_v.inc(v.tensor_add(OST[:, :], OST[:, :],
                                             ACC[:, os * 1024 + tc * 512:os * 1024 + (tc + 1) * 512]))

                    @block.scalar
                    def _(sc, os=os, cbase=cbase):
                        s_v.wait(sc)
                        s_sp.inc(sc.dma_start(
                            out=yout_d[os * 128:(os + 1) * 128, cbase:cbase + 512],
                            in_=OST[:, :]), 16)

            ctx.barrier()





def _build_fused(dbg=False):
    nc = bass.Bass("TRN2", target_bir_lowering=False, debug=False)
    F = lambda nm, sh: nc.dram_tensor(nm, sh, F32, kind="ExternalInput").ap()
    xT_h = nc.dram_tensor("xT", [8192, 4096], F32, kind="ExternalInput")
    idx_d = nc.dram_tensor("idx8", [128, 8], U32, kind="ExternalInput").ap()
    a_wq = F("a_wq", [D, 3072])
    a_bq = F("a_bq", [128, 24])
    a_wo = F("a_wo", [D, D])
    a_bo = F("a_bo", [128, 8])
    f_w = [F("f_w%d" % i, [D, D]) for i in range(3)]
    f_u = [F("f_u%d" % i, [D, D]) for i in range(3)]
    f_wo3 = F("f_wo3", [D, D])
    f_uo = F("f_uo", [D, D])
    f_bc = F("f_bc", [128, 32])
    f_ic = F("f_ic", [128, 8])
    b_wq = F("b_wq", [D, 3072])
    b_wo = F("b_wo", [D, D])
    m_wr = F("m_wr", [D, 16])
    m_br = F("m_br", [1, 16])
    m_w1 = F("m_w1", [16, D, 2048])
    m_b1 = F("m_b1", [128, 256])
    m_w2 = F("m_w2", [16, 2048, D])
    m_b2 = F("m_b2", [16, D])
    yout = nc.dram_tensor("yT", [D, NT], F32, kind="ExternalOutput").ap()

    qkT0 = nc.dram_tensor("qkT0", [2048, NTOT], BF16).ap()
    vtok0 = nc.dram_tensor("vtok0", [NTOT, D], BF16).ap()
    qn_h = nc.dram_tensor("qnd", [16, NTOT], BF16)
    km_h = nc.dram_tensor("kmd", [16, 16], F32)
    x1_h = nc.dram_tensor("x1d", [8192, 4096], F32)
    lnx_h = nc.dram_tensor("lnxf", [8192, 4096], BF16)
    cc_d = nc.dram_tensor("ccd", [D, S], F32).ap()
    cb_d = nc.dram_tensor("cbd", [D, S], BF16).ap()
    xown_h = nc.dram_tensor("xown", [D, NT], F32)
    lnown_h = nc.dram_tensor("lnown", [D, NT], BF16)
    y2_h = nc.dram_tensor("y2d", [D, NT], F32)
    qkT1 = nc.dram_tensor("qkT1", [2048, NT], BF16).ap()
    vtok1 = nc.dram_tensor("vtok1", [NT, D], BF16).ap()
    lnxm = nc.dram_tensor("lnxm", [D, NT], BF16).ap()
    y3 = nc.dram_tensor("y3", [D, NT], F32)
    if dbg:
        dx1 = nc.dram_tensor("dbg_x1", [8192, 4096], F32, kind="ExternalOutput").ap()
        dcc = nc.dram_tensor("dbg_cc", [D, S], F32, kind="ExternalOutput").ap()
        dy2 = nc.dram_tensor("dbg_y2", [D, NT], F32, kind="ExternalOutput").ap()

    with contextlib.ExitStack() as es:
        block = es.enter_context(nc.Block())
        ctx = Ctx(nc, block, es)
        emit_l0_qkv(ctx, xT_h, a_wq, a_bq, qkT0, vtok0)
        emit_qk_norms(ctx, qkT0, qn_h.ap(), km_h.ap())
        emit_l0_attn2(ctx, xT_h, x1_h, a_wo, a_bq, a_bo, qkT0, vtok0, qn_h.ap(), km_h)
        emit_ln_all(ctx, x1_h, lnx_h)
        emit_scan(ctx, lnx_h, f_w, f_u, f_bc, f_ic, cc_d, cb_d)
        emit_own_tail(ctx, x1_h, lnx_h, idx_d, xown_h, lnown_h, cb_d, f_wo3, f_uo,
                      f_bc, y2_h)
        xp, xr, _ = _prov_flat(y2_h)
        emit_l1_attn(ctx, xp, xr, y3.ap(), b_wq, b_wo, qkT1, vtok1)
        ctx.barrier()
        xp3, xr3, _ = _prov_flat(y3)
        emit_l1_ffn(ctx, xp3, xr3, yout, m_wr, m_br, m_w1, m_b1, m_w2, m_b2, lnxm)
        if dbg:
            s_dbg = ctx.sem("dbg")

            @block.sync
            def _(sy):
                for s in ctx._sems.values():
                    s.wait(sy)
                s_dbg.inc(sy.dma_start(out=dx1, in_=x1_h.ap()), 16)
                s_dbg.inc(sy.dma_start(out=dcc, in_=cc_d), 16)
                s_dbg.inc(sy.dma_start(out=dy2, in_=y2_h.ap()), 16)
                s_dbg.wait(sy)
        ctx.barrier(engines=("sync",))
    return nc


def _from_fm(y_t):
    return np.ascontiguousarray(
        y_t.reshape(8, D, NB, S).transpose(0, 2, 3, 1).reshape(16, S, D))


def _prep_inputs(inputs):
    f32 = lambda k: np.asarray(inputs[k], np.float32)
    tcol = lambda a: np.ascontiguousarray(np.asarray(a, np.float32).T)
    x = f32("L0_attn_x")
    xfm = np.ascontiguousarray(x.transpose(2, 0, 1).reshape(D, NTOT))
    xsl = np.ascontiguousarray(xfm.reshape(8192, 4096))
    bq2 = np.ascontiguousarray(f32("L0_attn_bqkv").reshape(24, 128).T)
    bo2 = np.ascontiguousarray(f32("L0_attn_bout").reshape(8, 128).T)
    bcol = np.ascontiguousarray(
        np.stack([f32("L0_ffn_b" + c) for c in "fico"]).reshape(32, 128).T)
    incc = np.ascontiguousarray(f32("L0_ffn_in_c").reshape(8, 128).T)
    b1c = np.ascontiguousarray(f32("L1_ffn_b1").reshape(16, 16, 128)
                               .transpose(2, 0, 1).reshape(128, 256))
    base = {"xT": xsl, "a_wq": f32("L0_attn_Wqkv"), "a_bq": bq2,
            "a_wo": f32("L0_attn_Wout"), "a_bo": bo2,
            "f_w0": tcol(f32("L0_ffn_wf")), "f_w1": tcol(f32("L0_ffn_wi")),
            "f_w2": tcol(f32("L0_ffn_wc")),
            "f_u0": tcol(f32("L0_ffn_uf")), "f_u1": tcol(f32("L0_ffn_ui")),
            "f_u2": tcol(f32("L0_ffn_uc")),
            "f_wo3": tcol(f32("L0_ffn_wo")), "f_uo": tcol(f32("L0_ffn_uo")),
            "f_bc": bcol, "f_ic": incc,
            "b_wq": tcol(f32("L1_attn_w_qkv")), "b_wo": tcol(f32("L1_attn_w_out")),
            "m_wr": f32("L1_ffn_Wr"), "m_br": f32("L1_ffn_br").reshape(1, 16),
            "m_w1": f32("L1_ffn_W1"), "m_b1": b1c, "m_w2": f32("L1_ffn_W2"),
            "m_b2": f32("L1_ffn_b2")}
    in_maps = []
    for r in range(8):
        idx = ((np.arange(8)[None, :] * 128 + np.arange(128)[:, None]) * 8
               + r).astype(np.uint32)
        in_maps.append(dict(base, idx8=np.ascontiguousarray(idx)))
    return in_maps


def kernel(**inputs):
    nc = _build_fused(dbg=False)
    in_maps = _prep_inputs(inputs)
    res = run_bass_kernel_spmd(nc, in_maps, core_ids=list(range(8)))
    return _from_fm(np.stack([np.asarray(r["yT"]) for r in res.results]))
